# version 06
# baseline (speedup 1.0000x reference)
import contextlib
import numpy as np
import ml_dtypes
import concourse.bass as bass
import concourse.mybir as mybir
from concourse.bass_utils import run_bass_kernel_spmd

F32 = mybir.dt.float32
BF16 = mybir.dt.bfloat16
I32 = mybir.dt.int32
AF = mybir.ActivationFunctionType
ALU = mybir.AluOpType
BF = ml_dtypes.bfloat16

D = 2048
TL = 2048
TF = 4096
NKD = D // 128

GI = 5504
NKG = GI // 128
R_HEADS, R_DH, R_C = 8, 256, 256
L_HEADS, L_DH, L_C = 6, 256, 64
KEY = L_HEADS * L_DH
LI = 8192
NKL = LI // 128

GAMMAS = (1.0 - np.exp2(np.linspace(np.log2(1 / 32), np.log2(1 / 512), R_HEADS)))
DN3_LIMIT = None


class Prog:
    def __init__(self, detect_races=True):
        self.nc = bass.Bass("TRN2", target_bir_lowering=False,
                            detect_race_conditions=detect_races)
        self.S = contextlib.ExitStack()
        self._sems = {}
        self.cnt = {}
        self._uid = 0

    def uid(self):
        self._uid += 1
        return self._uid

    def sem(self, name):
        if name not in self._sems:
            self._sems[name] = self.S.enter_context(self.nc.semaphore(name))
            self.cnt[name] = 0
        return self._sems[name]

    def sbuf(self, name, shape, dt, scope=None):
        return (scope or self.S).enter_context(
            self.nc.sbuf_tensor(f"{name}_{self.uid()}", shape, dt))

    def psum(self, name, shape, dt=F32, scope=None):
        return (scope or self.S).enter_context(
            self.nc.psum_tensor(f"{name}_{self.uid()}", shape, dt))

    def dram(self, name, shape, dt, **kw):
        return self.nc.dram_tensor(name, shape, dt, **kw)


CONST_VALS = [1e-5, 1e-6, -0.5, 0.5, -1.0, 1.0, 2.0, 1.0 / 16, 1.0 / 256]


def emit_consts(p):
    nc = p.nc
    c = {}
    c["ones1"] = p.sbuf("c_ones1", [1, 512], BF16)
    c["ones1f"] = p.sbuf("c_ones1f", [1, 128], F32)
    c["onesc"] = p.sbuf("c_onesc", [128, 1], BF16)
    c["ones65"] = p.sbuf("c_ones65", [65, 128], BF16)
    c["ident"] = p.sbuf("c_ident", [128, 128], BF16)
    c["cvals"] = p.sbuf("c_cvals", [128, len(CONST_VALS)], F32)
    io2 = p.sbuf("c_io2", [128, 256], BF16)
    for i, v in enumerate(CONST_VALS):
        nc.const_aps.register(F32, v, c["cvals"][:, i:i + 1])
    gx, vx = p.sem("gx"), p.sem("vx")
    with nc.Block(f"consts_{p.uid()}") as blk:
        @blk.gpsimd
        def _(g):
            g.iota(io2[:, 0:128], [[1, 128]], base=0, channel_multiplier=0,
                   allow_small_or_imprecise_dtypes=True)
            g.iota(io2[:, 128:256], [[0, 128]], base=0, channel_multiplier=1,
                   allow_small_or_imprecise_dtypes=True).then_inc(gx, 1)

        @blk.vector
        def _(v):
            v.memset(c["ones1"][:], 1.0)
            v.memset(c["ones1f"][:], 1.0)
            v.memset(c["onesc"][:], 1.0)
            v.memset(c["ones65"][:], 1.0)
            for i, val in enumerate(CONST_VALS):
                v.memset(c["cvals"][:, i:i + 1], val)
            v.wait_ge(gx, p.cnt["gx"] + 1)
            v.tensor_tensor(c["ident"][:], io2[:, 0:128], io2[:, 128:256],
                            ALU.is_equal).then_inc(vx, 1)

        @blk.scalar
        def _(s):
            s.wait_ge(vx, p.cnt["vx"] + 1)
    p.cnt["gx"] += 1
    p.cnt["vx"] += 1
    return c


def emit_ghost_build(p, io):
    nc = p.nc
    ls = contextlib.ExitStack()
    xl, vx, st = p.sem("xl"), p.sem("vx"), p.sem("st")
    lutcol = p.sbuf("gb_lutcol", [128, 16], F32, ls)
    idxs = p.sbuf("gb_idx", [128, 2 * 5504], I32, ls)
    idxb = p.sbuf("gb_idxb", [128, 5504], BF16, ls)
    accs = p.sbuf("gb_acc", [128, 4 * 5504], BF16, ls)
    scr = p.sbuf("gb_scr", [128, 5504], BF16, ls)
    outt = p.sbuf("gb_out", [128, 2 * 5504], BF16, ls)

    tiles = []
    for r in range(2):
        tiles.append((io["gidxT"][r * 128:(r + 1) * 128, :], io["wg_sh"][r * 128:(r + 1) * 128, :], 5504))
    for r in range(2):
        tiles.append((io["uidxT"][r * 128:(r + 1) * 128, :], io["wu_sh"][r * 128:(r + 1) * 128, :], 5504))
    for r in range(6):
        r0, r1 = r * 128, min(r * 128 + 128, 688)
        tiles.append((io["didxT"][r0:r1, :], io["wd_sh"][r0:r1, :], 2048))
    nt = len(tiles)
    xl0, vx0, st0 = p.cnt["xl"], p.cnt["vx"], p.cnt["st"]

    with nc.Block(f"ghostbuild_{p.uid()}") as blk:
        @blk.gpsimd
        def _(g):
            g.dma_start(out=lutcol[:], in_=bass.AP(io["lut"].tensor, 0, [[0, 128], [1, 16]])).then_inc(xl, 16)
            for i, (src, dst, w) in enumerate(tiles):
                if i >= 2:
                    g.wait_ge(vx, vx0 + i - 1)
                rows = src.shape[0]
                g.dma_start(out=idxs[0:rows, (i % 2) * 5504:(i % 2) * 5504 + w],
                            in_=src).then_inc(xl, 16)

        @blk.vector
        def _(v):
            for i, (src, dst, w) in enumerate(tiles):
                rows = src.shape[0]
                b = (i % 2) * 5504
                v.wait_ge(xl, xl0 + 16 * (i + 2))
                ib = idxb[0:rows, 0:w]
                v.tensor_copy(ib, idxs[0:rows, b:b + w])
                for j in range(16):
                    a = accs[0:rows, (j % 4) * 5504:(j % 4) * 5504 + w]
                    if j < 4:
                        v.tensor_scalar(a, ib, float(j), lutcol[0:rows, j:j + 1], ALU.is_equal, ALU.mult)
                    else:
                        sc = scr[0:rows, 0:w]
                        v.tensor_scalar(sc, ib, float(j), lutcol[0:rows, j:j + 1], ALU.is_equal, ALU.mult)
                        v.tensor_tensor(a, a, sc, ALU.add)
                A = lambda k: accs[0:rows, k * 5504:k * 5504 + w]
                if i >= 2:
                    v.wait_ge(st, st0 + 16 * (i - 1))
                o = outt[0:rows, b:b + w]
                v.tensor_tensor(A(0), A(0), A(1), ALU.add)
                v.tensor_tensor(A(2), A(2), A(3), ALU.add)
                v.tensor_tensor(o, A(0), A(2), ALU.add).then_inc(vx, 1)

        @blk.sync
        def _(sy):
            for i, (src, dst, w) in enumerate(tiles):
                rows = src.shape[0]
                sy.wait_ge(vx, vx0 + i + 1)
                sy.dma_start(out=dst, in_=outt[0:rows, (i % 2) * 5504:(i % 2) * 5504 + w]).then_inc(st, 16)
            sy.wait_ge(st, st0 + 16 * nt)
    p.cnt["xl"] = xl0 + 16 * (nt + 1)
    p.cnt["vx"] = vx0 + nt
    p.cnt["st"] = st0 + 16 * nt
    ls.close()


def emit_ln(p, c, x_dram, xln, T=TL):
    nc = p.nc
    NT = T // 512
    ls = contextlib.ExitStack()
    xcb = p.sbuf("ln_xc", [128, NKD * T], BF16, ls)
    xsq = p.sbuf("ln_xsq", [128, 4 * 512], BF16, ls)
    r_m = p.sbuf("ln_rm", [1, T], F32, ls)
    r_q = p.sbuf("ln_rq", [1, T], F32, ls)
    r_var = p.sbuf("ln_rvar", [1, T], F32, ls)
    r_tmp = p.sbuf("ln_rtmp", [1, T], F32, ls)
    r_rstd = p.sbuf("ln_rrstd", [1, T], F32, ls)
    mrb = p.sbuf("ln_mrb", [128, 2 * 512], F32, ls)
    scr = p.sbuf("ln_scr", [128, 512], F32, ls)
    ps_s = p.psum("ln_ps_s", [1, 512], scope=ls)
    ps_q = p.psum("ln_ps_q", [1, 512], scope=ls)
    ps_m = p.psum("ln_ps_m", [128, 512], scope=ls)
    ps_r = p.psum("ln_ps_r", [128, 512], scope=ls)
    xl, mm, mmf, ax, vx = p.sem("xl"), p.sem("mm"), p.sem("mmf"), p.sem("ax"), p.sem("vx")
    s0 = {k: p.cnt[k] for k in ("xl", "mm", "mmf", "ax", "vx")}
    AXTG = NKD + 2

    with nc.Block(f"ln_{p.uid()}") as blk:
        @blk.gpsimd
        def _(g):
            g.dma_start(out=xcb[:].rearrange("p (k t) -> p k t", k=NKD),
                        in_=x_dram.rearrange("(k p) t -> p k t", p=128)).then_inc(xl, 16)

        @blk.scalar
        def _(s):
            s.wait_ge(xl, s0["xl"] + 16)
            for tg in range(NT):
                for k in range(NKD):
                    gsq = tg * NKD + k
                    if gsq >= 4:
                        tq, kq = divmod(gsq - 4, NKD)
                        if kq < NKD - 1:
                            s.wait_ge(mmf, s0["mmf"] + tq * (NKD - 1) + kq + 1)
                        else:
                            s.wait_ge(mm, s0["mm"] + 2 * tq + 1)
                    s.activation(out=xsq[:, (gsq % 4) * 512:(gsq % 4) * 512 + 512],
                                 in_=xcb[:, k * T + tg * 512:k * T + tg * 512 + 512],
                                 func=AF.Square).then_inc(ax, 1)
                s.wait_ge(vx, s0["vx"] + 2 * tg + 1)
                sl = slice(tg * 512, tg * 512 + 512)
                s.activation(out=r_tmp[0:1, sl], in_=r_var[0:1, sl], func=AF.Ln, bias=1e-5)
                s.activation(out=r_rstd[0:1, sl], in_=r_tmp[0:1, sl], func=AF.Exp, scale=-0.5).then_inc(ax, 1)
                s.wait_ge(mm, s0["mm"] + 2 * tg + 2)
                s.activation(out=mrb[:, 0:512], in_=ps_m[:, :], func=AF.Copy)
                s.activation(out=mrb[:, 512:1024], in_=ps_r[:, :], func=AF.Copy).then_inc(ax, 1)

        @blk.tensor
        def _(t):
            t.wait_ge(xl, s0["xl"] + 16)
            for tg in range(NT):
                if tg >= 1:
                    t.wait_ge(vx, s0["vx"] + 2 * tg - 1)
                for k in range(NKD):
                    gsq = tg * NKD + k
                    t.matmul(ps_s[:, :], c["onesc"][:, :],
                             xcb[:, k * T + tg * 512:k * T + tg * 512 + 512],
                             start=(k == 0), stop=(k == NKD - 1))
                    t.wait_ge(ax, s0["ax"] + tg * AXTG + k + 1)
                    inst = t.matmul(ps_q[:, :], c["onesc"][:, :],
                             xsq[:, (gsq % 4) * 512:(gsq % 4) * 512 + 512],
                             start=(k == 0), stop=(k == NKD - 1))
                    inst.then_inc(mmf, 1) if k < NKD - 1 else inst.then_inc(mm, 1)
                t.wait_ge(vx, s0["vx"] + 2 * tg + 1)
                t.wait_ge(ax, s0["ax"] + tg * AXTG + NKD + 1)
                sl = slice(tg * 512, tg * 512 + 512)
                t.matmul(ps_m[:, :], c["ones1f"][0:1, 0:128], r_m[0:1, sl], start=True, stop=True)
                t.matmul(ps_r[:, :], c["ones1f"][0:1, 0:128], r_rstd[0:1, sl], start=True, stop=True).then_inc(mm, 1)

        @blk.vector
        def _(v):
            for tg in range(NT):
                sl = slice(tg * 512, tg * 512 + 512)
                v.wait_ge(mm, s0["mm"] + 2 * tg + 1)
                v.tensor_scalar(r_m[0:1, sl], ps_s[0:1, :], 1.0 / D, None, ALU.mult, ALU.bypass)
                v.tensor_scalar(r_q[0:1, sl], ps_q[0:1, :], 1.0 / D, None, ALU.mult, ALU.bypass)
                v.tensor_tensor(r_tmp[0:1, sl], r_m[0:1, sl], r_m[0:1, sl], ALU.mult)
                v.tensor_tensor(r_var[0:1, sl], r_q[0:1, sl], r_tmp[0:1, sl], ALU.subtract).then_inc(vx, 1)
                v.wait_ge(ax, s0["ax"] + tg * AXTG + NKD + 2)
                for k in range(NKD):
                    v.tensor_tensor(scr[:, :], xcb[:, k * T + tg * 512:k * T + tg * 512 + 512],
                                    mrb[:, 0:512], ALU.subtract)
                    inst = v.tensor_tensor(xln[:, k * T + tg * 512:k * T + tg * 512 + 512],
                                    scr[:, :], mrb[:, 512:1024], ALU.mult)
                inst.then_inc(vx, 1)

    p.cnt["xl"] = s0["xl"] + 16
    p.cnt["ax"] = s0["ax"] + NT * AXTG
    p.cnt["mm"] = s0["mm"] + NT * 2
    p.cnt["mmf"] = s0["mmf"] + NT * (NKD - 1)
    p.cnt["vx"] = s0["vx"] + NT * 2
    ls.close()


def emit_ffn(p, c, xln, io, I, scale=0.5, T=TL):
    nc = p.nc
    NKI = I // 128
    NT = T // 512
    OSLAB = 256
    slabs = []
    o0 = 0
    while o0 < I:
        w = min(OSLAB, I - o0)
        slabs.append((o0, w))
        o0 += w
    NS = len(slabs)
    G_of = [w // 128 * NT for (_, w) in slabs]
    Gtot = sum(G_of)

    ls = contextlib.ExitStack()
    wg_sb = p.sbuf("ff_wg", [128, 2 * NKD * OSLAB], BF16, ls)
    wu_sb = p.sbuf("ff_wu", [128, 2 * NKD * OSLAB], BF16, ls)
    sg = p.sbuf("ff_sg", [128, 2 * 512], BF16, ls)
    h_sb = p.sbuf("ff_h", [128, 2 * (OSLAB // 128) * T], BF16, ls)
    psg = [p.psum("ff_psg", [128, 512], scope=ls) for _ in range(2)]
    psu = [p.psum("ff_psu", [128, 512], scope=ls) for _ in range(2)]
    wl, mm, ax, vx, st = p.sem("wl"), p.sem("mm"), p.sem("ax"), p.sem("vx"), p.sem("st")
    s0 = {k: p.cnt[k] for k in ("wl", "mm", "ax", "vx", "st")}
    wsz = NKD * OSLAB

    def wslice(buf_sb, b, k, o, width):
        base = b * wsz + k * OSLAB + o
        return buf_sb[:, base:base + width]

    with nc.Block(f"ffnA_{p.uid()}") as blk:
        @blk.gpsimd
        def _(g):
            for s, (os0, w) in enumerate(slabs):
                if s >= 2:
                    g.wait_ge(mm, s0["mm"] + 2 * sum(G_of[:s - 1]))
                b = s % 2
                for wt_d, dst in ((io["wgT"], wg_sb), (io["wuT"], wu_sb)):
                    g.dma_start(
                        out=dst[:, b * wsz:(b + 1) * wsz]
                            .rearrange("p (k o) -> p k o", o=OSLAB)[:, :, 0:w],
                        in_=wt_d[:, os0:os0 + w].rearrange("(k p) o -> p k o", p=128),
                    ).then_inc(wl, 16)

        @blk.tensor
        def _(t):
            gi = 0
            for s, (os0, w) in enumerate(slabs):
                t.wait_ge(wl, s0["wl"] + 16 * 2 * (s + 1))
                b = s % 2
                for oc in range(w // 128):
                    for tg in range(NT):
                        if gi >= 2:
                            t.wait_ge(ax, s0["ax"] + gi - 1)
                            t.wait_ge(vx, s0["vx"] + gi - 1)
                        for k in range(NKD):
                            inst = t.matmul(psg[gi % 2][:, :], wslice(wg_sb, b, k, oc * 128, 128),
                                     xln[:, k * T + tg * 512:k * T + tg * 512 + 512],
                                     start=(k == 0), stop=(k == NKD - 1))
                        inst.then_inc(mm, 1)
                        for k in range(NKD):
                            inst = t.matmul(psu[gi % 2][:, :], wslice(wu_sb, b, k, oc * 128, 128),
                                     xln[:, k * T + tg * 512:k * T + tg * 512 + 512],
                                     start=(k == 0), stop=(k == NKD - 1))
                        inst.then_inc(mm, 1)
                        gi += 1

        @blk.scalar
        def _(s_):
            gi = 0
            for s, (os0, w) in enumerate(slabs):
                for oc in range(w // 128):
                    for tg in range(NT):
                        s_.wait_ge(mm, s0["mm"] + 2 * gi + 1)
                        if gi >= 2:
                            s_.wait_ge(vx, s0["vx"] + gi - 1)
                        s_.activation(out=sg[:, (gi % 2) * 512:(gi % 2) * 512 + 512],
                                      in_=psg[gi % 2][:, :], func=AF.Silu).then_inc(ax, 1)
                        gi += 1

        @blk.vector
        def _(v):
            gi = 0
            for s, (os0, w) in enumerate(slabs):
                b = s % 2
                if s >= 2:
                    v.wait_ge(st, s0["st"] + 16 * (s - 1))
                for oc in range(w // 128):
                    for tg in range(NT):
                        v.wait_ge(mm, s0["mm"] + 2 * gi + 2)
                        v.wait_ge(ax, s0["ax"] + gi + 1)
                        v.tensor_tensor(
                            h_sb[:, b * (OSLAB // 128) * T + oc * T + tg * 512:
                                    b * (OSLAB // 128) * T + oc * T + tg * 512 + 512],
                            psu[gi % 2][:, :], sg[:, (gi % 2) * 512:(gi % 2) * 512 + 512],
                            ALU.mult).then_inc(vx, 1)
                        gi += 1

        @blk.sync
        def _(sy):
            gi = 0
            for s, (os0, w) in enumerate(slabs):
                b = s % 2
                gi += G_of[s]
                sy.wait_ge(vx, s0["vx"] + gi)
                noc = w // 128
                sy.dma_start(
                    out=io["h"][os0 // 128:os0 // 128 + noc, :, :].rearrange("o p t -> p o t"),
                    in_=h_sb[:, b * (OSLAB // 128) * T:b * (OSLAB // 128) * T + noc * T]
                        .rearrange("p (o t) -> p o t", o=noc),
                ).then_inc(st, 16)
            sy.wait_ge(st, s0["st"] + 16 * NS)

    p.cnt["wl"] = s0["wl"] + 16 * 2 * NS
    p.cnt["mm"] = s0["mm"] + 2 * Gtot
    p.cnt["ax"] = s0["ax"] + Gtot
    p.cnt["vx"] = s0["vx"] + Gtot
    p.cnt["st"] = s0["st"] + 16 * NS
    ls.close()

    emit_down(p, c, io["h"], io["wdT"], io["x_acc"], NKI, scale, T)


def emit_copy_dram(p, src, dst):
    nc = p.nc
    sg = p.sem("sg")
    s0 = p.cnt["sg"]
    with nc.Block(f"copyx_{p.uid()}") as blk:
        @blk.gpsimd
        def _(g):
            g.dma_start(out=dst, in_=src).then_inc(sg, 16)
            g.wait_ge(sg, s0 + 16)
    p.cnt["sg"] = s0 + 16


def emit_retnet(p, c, io):
    nc = p.nc
    H = 4
    OW = H * R_DH
    NOC = OW // 128
    NTG = TF // 512
    CPG = 512 // R_C
    ls = contextlib.ExitStack()
    xg_sb = p.sbuf("rn_xg", [128, NKD * 512], BF16, ls)
    wq_sb = p.sbuf("rn_wq", [128, NKD * OW], BF16, ls)
    wk_sb = p.sbuf("rn_wk", [128, NKD * OW], BF16, ls)
    wv_sb = p.sbuf("rn_wv", [128, NKD * OW], BF16, ls)
    bq_sb = p.sbuf("rn_bq", [1, OW], BF16, ls)
    bk_sb = p.sbuf("rn_bk", [1, OW], BF16, ls)
    bv_sb = p.sbuf("rn_bv", [1, OW], BF16, ls)
    dt_sb = p.sbuf("rn_dt", [128, H * 2 * 256], BF16, ls)
    cs_sb = p.sbuf("rn_cs", [128, H * 256], BF16, ls)
    kv_sb = p.sbuf("rn_kv", [128, H * 256], BF16, ls)
    gmc_sb = p.sbuf("rn_gmc", [128, H], F32, ls)
    q_sb = p.sbuf("rn_q", [128, 2 * NOC * 512], BF16, ls)
    k_sb = p.sbuf("rn_k", [128, 2 * NOC * 512], BF16, ls)
    v_sb = p.sbuf("rn_v", [128, 2 * 4 * OW], BF16, ls)
    o_sb = p.sbuf("rn_o", [128, NOC * 512], BF16, ls)
    S_f = p.sbuf("rn_S", [128, H * 2 * 256], F32, ls)
    S_b = p.sbuf("rn_Sb", [128, H * 2 * 256], BF16, ls)
    at_sb = p.sbuf("rn_at", [128, 2 * 256], BF16, ls)
    qt_sb = p.sbuf("rn_qt", [128, 2 * 256], BF16, ls)
    kt_sb = p.sbuf("rn_kt", [128, 2 * 256], BF16, ls)
    ktr_sb = p.sbuf("rn_ktr", [128, 2 * 256], BF16, ls)
    ps_p = [p.psum("rn_pp", [128, 512], scope=ls) for _ in range(2)]
    ps_a = [p.psum("rn_pa", [128, 256], scope=ls) for _ in range(2)]
    ps_o = [p.psum("rn_po", [128, 256], scope=ls) for _ in range(2)]
    ps_t = p.psum("rn_pt", [128, 512], BF16, scope=ls)
    xl, wl, mm, ax, vx, st = (p.sem(k) for k in ("xl", "wl", "mm", "ax", "vx", "st"))
    s0 = {k: p.cnt[k] for k in ("xl", "wl", "mm", "ax", "vx", "st")}

    steps = [("P", 0), ("P", 1)]
    for t in range(8):
        steps.append(("R", t))
        if t + 2 < NTG:
            steps.append(("P", t + 2))
    NCH = NTG * CPG * H
    mmA = [0] * NCH; mmB = [0] * NCH; mmC = [0] * NCH; mmD = [0] * NCH
    vx1 = [0] * NCH; vx2 = [0] * NCH; vx3 = [0] * NCH
    axO = [0] * NCH; axK = [0] * NCH
    ax_after_P = [0] * NTG; mm_after_P = [0] * NTG
    axP = {}
    mmP = {}
    mm_cnt = s0["mm"]; ax_cnt = s0["ax"]; vx_cnt = s0["vx"]
    n_glob = 0
    for kind, t in steps:
        if kind == "P":
            for gi in range(3 * NOC):
                mm_cnt += 1
                mmP[(t, gi)] = mm_cnt
                ax_cnt += 1
                axP[(t, gi)] = ax_cnt
            ax_after_P[t] = ax_cnt
            mm_after_P[t] = mm_cnt
        else:
            for l in range(CPG * H):
                n = n_glob
                vx_cnt += 1; vx1[n] = vx_cnt
                mm_cnt += 1; mmA[n] = mm_cnt
                vx_cnt += 1; vx2[n] = vx_cnt
                mm_cnt += 1; mmB[n] = mm_cnt
                ax_cnt += 1; axO[n] = ax_cnt
                mm_cnt += 1; mmC[n] = mm_cnt
                ax_cnt += 1; axK[n] = ax_cnt
                mm_cnt += 1; mmD[n] = mm_cnt
                vx_cnt += 1; vx3[n] = vx_cnt
                n_glob += 1
    mm_tot, ax_tot, vx_tot = mm_cnt, ax_cnt, vx_cnt

    def ch_info(n):
        tg, r = divmod(n, CPG * H)
        cc, j = divmod(r, H)
        return tg, cc, j, tg * CPG + cc

    def qk_sl(buf, sb, j, kc, lo, w):
        base = buf * NOC * 512 + (j * 2 + kc) * 512
        return sb[:, base + lo:base + lo + w]

    def v_sl(buf, tsub, lo, w):
        base = buf * 4 * OW + tsub * OW
        return v_sb[:, base + lo:base + lo + w]

    with nc.Block(f"retnet_{p.uid()}") as blk:
        @blk.gpsimd
        def _(g):
            for wt, dst in ((io["wqT"], wq_sb), (io["wkT"], wk_sb), (io["wvT"], wv_sb)):
                g.dma_start(out=dst[:].rearrange("p (k o) -> p k o", k=NKD),
                            in_=wt.rearrange("(k p) o -> p k o", p=128)).then_inc(wl, 16)
            for bb, dst in ((io["bq"], bq_sb), (io["bk"], bk_sb), (io["bv"], bv_sb)):
                g.dma_start(out=dst[:, :], in_=bass.AP(bb.tensor, 0, [[0, 1], [1, OW]])).then_inc(wl, 16)
            g.dma_start(out=dt_sb[:].rearrange("p (h m i) -> p h m i", h=H, m=2),
                        in_=io["dtT"].rearrange("h (m p) i -> p h m i", p=128)).then_inc(wl, 16)
            g.dma_start(out=cs_sb[:].rearrange("p (h i) -> p h i", h=H),
                        in_=io["cst"].rearrange("h p i -> p h i")).then_inc(wl, 16)
            g.dma_start(out=kv_sb[:].rearrange("p (h i) -> p h i", h=H),
                        in_=io["kvt"].rearrange("h p i -> p h i")).then_inc(wl, 16)
            g.dma_start(out=gmc_sb[:, :], in_=bass.AP(io["gmc"].tensor, 0, [[0, 128], [1, H]])).then_inc(wl, 16)
            for tg in range(NTG):
                if tg >= 1:
                    g.wait_ge(mm, mm_after_P[tg - 1])
                slot, tgl = divmod(tg, 4)
                g.dma_start(out=xg_sb[:].rearrange("p (k t) -> p k t", k=NKD),
                            in_=io["xg"][slot, :, tgl * 512:tgl * 512 + 512]
                            .rearrange("(k p) t -> p k t", p=128)).then_inc(xl, 16)

        @blk.tensor
        def _(t_):
            t_.wait_ge(wl, s0["wl"] + 16 * 10)
            n = 0
            for kind, tg in steps:
                if kind == "P":
                    t_.wait_ge(xl, s0["xl"] + 16 * (tg + 1))
                    b = tg % 2
                    gi = 0
                    for w_sb, b_sb, feat in ((wq_sb, bq_sb, True), (wk_sb, bk_sb, True),
                                             (wv_sb, bv_sb, False)):
                        for oc in range(NOC):
                            pp = ps_p[gi % 2]
                            key = (tg, gi - 2)
                            if key in axP:
                                t_.wait_ge(ax, axP[key])
                            elif tg >= 1 and gi < 2:
                                t_.wait_ge(ax, axP[(tg - 1, 3 * NOC - 2 + gi)])
                            if feat:
                                for kk in range(NKD):
                                    t_.matmul(pp[:, :], w_sb[:, kk * OW + oc * 128:kk * OW + oc * 128 + 128],
                                              xg_sb[:, kk * 512:(kk + 1) * 512],
                                              start=(kk == 0), stop=False)
                                t_.matmul(pp[:, :], b_sb[0:1, oc * 128:oc * 128 + 128],
                                          c["ones1"][0:1, 0:512], start=False, stop=True).then_inc(mm, 1)
                            else:
                                tsub, og = divmod(oc, 2)
                                for kk in range(NKD):
                                    t_.matmul(pp[:, :],
                                              xg_sb[:, kk * 512 + tsub * 128:kk * 512 + tsub * 128 + 128],
                                              w_sb[:, kk * OW + og * 512:kk * OW + (og + 1) * 512],
                                              start=(kk == 0), stop=False)
                                t_.matmul(pp[:, :], c["ones1"][0:1, 0:128],
                                          b_sb[0:1, og * 512:og * 512 + 512],
                                          start=False, stop=True).then_inc(mm, 1)
                            gi += 1
                else:
                    for l in range(CPG * H):
                        tg_, cc, j, cab = ch_info(n)
                        b = tg_ % 2
                        lo = cc * 256
                        if n >= 1:
                            t_.wait_ge(vx, vx3[n - 1])
                        if l == 0:
                            t_.wait_ge(ax, ax_after_P[tg_])
                        for mj in range(2):
                            for kc in range(2):
                                inst = t_.matmul(ps_a[mj][:, :],
                                          qk_sl(b, k_sb, j, kc, lo + mj * 128, 128),
                                          qk_sl(b, q_sb, j, kc, lo, 256),
                                          start=(kc == 0), stop=(kc == 1))
                        inst.then_inc(mm, 1)
                        t_.wait_ge(vx, vx2[n])
                        if n >= 2:
                            t_.wait_ge(ax, axO[n - 2])
                        if n >= 4:
                            t_.wait_ge(vx, vx3[n - 4])
                        for vc in range(2):
                            po = ps_o[vc]
                            first = True
                            for jc in range(2):
                                t_.matmul(po[:, :], v_sl(b, cc * 2 + jc, j * 256 + vc * 128, 128),
                                          at_sb[:, jc * 256:jc * 256 + 256],
                                          start=first, stop=False)
                                first = False
                            for kc in range(2):
                                inst = t_.matmul(po[:, :], S_b[:, (j * 2 + kc) * 256 + vc * 128:
                                                           (j * 2 + kc) * 256 + vc * 128 + 128],
                                          qt_sb[:, kc * 256:kc * 256 + 256],
                                          start=False, stop=(kc == 1))
                        inst.then_inc(mm, 1)
                        if n >= 1:
                            t_.wait_ge(ax, axK[n - 1])
                        for jc in range(2):
                            for kc in range(2):
                                inst = t_.transpose(ps_t[:, (jc * 2 + kc) * 128:(jc * 2 + kc) * 128 + 128],
                                             kt_sb[:, kc * 256 + jc * 128:kc * 256 + jc * 128 + 128],
                                             c["ident"][:, :])
                        inst.then_inc(mm, 1)
                        t_.wait_ge(ax, axK[n])
                        for kcd in range(2):
                            pa = ps_a[kcd]
                            for jc in range(2):
                                inst = t_.matmul(pa[:, :], ktr_sb[:, jc * 256 + kcd * 128:jc * 256 + kcd * 128 + 128],
                                          v_sl(b, cc * 2 + jc, j * 256, 256),
                                          start=(jc == 0), stop=(jc == 1))
                        inst.then_inc(mm, 1)
                        n += 1

        @blk.vector
        def _(v):
            v.wait_ge(wl, s0["wl"] + 16 * 10)
            v.memset(S_f[:, :], 0.0)
            v.memset(S_b[:, :], 0.0)
            n = 0
            for kind, tg in steps:
                if kind != "R":
                    continue
                for l in range(CPG * H):
                    tg_, cc, j, cab = ch_info(n)
                    b = tg_ % 2
                    lo = cc * 256
                    if l == 0:
                        v.wait_ge(ax, ax_after_P[tg_])
                    if n >= 1:
                        v.wait_ge(mm, mmC[n - 1])
                    for kc in range(2):
                        v.tensor_tensor(qt_sb[:, kc * 256:kc * 256 + 256],
                                        qk_sl(b, q_sb, j, kc, lo, 256),
                                        cs_sb[:, j * 256:j * 256 + 256], ALU.mult)
                    for kc in range(2):
                        inst = v.tensor_tensor(kt_sb[:, kc * 256:kc * 256 + 256],
                                               qk_sl(b, k_sb, j, kc, lo, 256),
                                               kv_sb[:, j * 256:j * 256 + 256], ALU.mult)
                    inst.then_inc(vx, 1)
                    v.wait_ge(mm, mmA[n])
                    for mj in range(2):
                        inst = v.tensor_tensor(at_sb[:, mj * 256:mj * 256 + 256],
                                               ps_a[mj][:, :],
                                               dt_sb[:, (j * 2 + mj) * 256:(j * 2 + mj) * 256 + 256],
                                               ALU.mult)
                    inst.then_inc(vx, 1)
                    v.wait_ge(mm, mmD[n])
                    for kcd in range(2):
                        sl_ = slice((j * 2 + kcd) * 256, (j * 2 + kcd) * 256 + 256)
                        v.scalar_tensor_tensor(S_f[:, sl_], S_f[:, sl_], gmc_sb[:, j:j + 1],
                                               ps_a[kcd][:, :], ALU.mult, ALU.add)
                    for kcd in range(2):
                        inst = v.tensor_copy(S_b[:, (j * 2 + kcd) * 256:(j * 2 + kcd) * 256 + 256],
                                             S_f[:, (j * 2 + kcd) * 256:(j * 2 + kcd) * 256 + 256])
                    inst.then_inc(vx, 1)
                    n += 1

        @blk.scalar
        def _(s_):
            n = 0
            for kind, tg in steps:
                if kind == "P":
                    b = tg % 2
                    gi = 0
                    for which in range(3):
                        for oc in range(NOC):
                            s_.wait_ge(mm, mmP[(tg, gi)])
                            dst = (q_sb, k_sb, v_sb)[which]
                            if which < 2:
                                s_.activation(out=dst[:, b * NOC * 512 + oc * 512:b * NOC * 512 + (oc + 1) * 512],
                                              in_=ps_p[gi % 2][:, :], func=AF.Copy).then_inc(ax, 1)
                            else:
                                tsub, og = divmod(oc, 2)
                                s_.activation(out=v_sb[:, b * 4 * OW + tsub * OW + og * 512:
                                                          b * 4 * OW + tsub * OW + og * 512 + 512],
                                              in_=ps_p[gi % 2][:, :], func=AF.Copy).then_inc(ax, 1)
                            gi += 1
                else:
                    if tg >= 1:
                        s_.wait_ge(st, s0["st"] + 16 * tg)
                    for l in range(CPG * H):
                        tg_, cc, j, cab = ch_info(n)
                        b = tg_ % 2
                        s_.wait_ge(mm, mmB[n])
                        for vc in range(2):
                            inst = s_.activation(out=o_sb[:, (j * 2 + vc) * 512 + cc * 256:
                                                     (j * 2 + vc) * 512 + cc * 256 + 256],
                                          in_=ps_o[vc][:, :], func=AF.Copy)
                        inst.then_inc(ax, 1)
                        s_.wait_ge(mm, mmC[n])
                        for jc in range(2):
                            for kc in range(2):
                                inst = s_.activation(out=ktr_sb[:, jc * 256 + kc * 128:jc * 256 + kc * 128 + 128],
                                              in_=ps_t[:, (jc * 2 + kc) * 128:(jc * 2 + kc) * 128 + 128],
                                              func=AF.Copy)
                        inst.then_inc(ax, 1)
                        n += 1

        @blk.sync
        def _(sy):
            n_end = 0
            for kind, tg in steps:
                if kind != "R":
                    continue
                n_end += CPG * H
                sy.wait_ge(ax, axO[n_end - 1])
                b = tg % 2
                sy.dma_start(out=io["o_sh"][:, :, tg * 512:tg * 512 + 512].rearrange("o p t -> p o t"),
                             in_=o_sb[:].rearrange("p (o t) -> p o t", o=NOC)).then_inc(st, 16)
            sy.wait_ge(st, s0["st"] + 16 * NTG)

    p.cnt["xl"] = s0["xl"] + 16 * NTG
    p.cnt["wl"] = s0["wl"] + 16 * 10
    p.cnt["mm"] = mm_tot
    p.cnt["ax"] = ax_tot
    p.cnt["vx"] = vx_tot
    p.cnt["st"] = s0["st"] + 16 * NTG
    ls.close()


def emit_down(p, c, hT, wdT, x_acc, NKI, scale=0.5, T=TL, sel=None):
    nc = p.nc
    NT = T // 512
    ls = contextlib.ExitStack()
    hblk = p.sbuf("ff_hblk", [128, NKI * 512], BF16, ls)
    if sel is not None:
        hraw = p.sbuf("ff_hraw", [128, 2 * NKI * 512], BF16, ls)
        hscr = p.sbuf("ff_hscr", [128, 512], BF16, ls)
    vx = p.sem("vx")
    wd_sb = p.sbuf("ff_wd", [128, 2 * NKI * 128], BF16, ls)
    fsc = p.sbuf("ff_fsc", [128, 2 * 512], F32, ls)
    psd = [p.psum("ff_psd", [128, 512], scope=ls) for _ in range(2)]
    wl, xl, mm, ax, sg = (p.sem(k) for k in ("wl", "xl", "mm", "ax", "sg"))
    s0 = {k: p.cnt[k] for k in ("wl", "xl", "mm", "ax", "sg", "vx")}

    NTOT = NT * NKD

    def wd_load(g, i):
        dc = i % NKD
        if i >= 2:
            g.wait_ge(mm, s0["mm"] + i - 1)
        g.dma_start(out=wd_sb[:, (i % 2) * NKI * 128:(i % 2) * NKI * 128 + NKI * 128]
                    .rearrange("p (k o) -> p k o", k=NKI),
                    in_=wdT[:, dc * 128:dc * 128 + 128]
                    .rearrange("(k p) o -> p k o", p=128)).then_inc(wl, 16)

    def h_load(g, tg):
        if tg >= 1:
            g.wait_ge(mm, s0["mm"] + tg * NKD)
        if sel is None:
            g.dma_start(out=hblk[:].rearrange("p (k t) -> p k t", k=NKI),
                        in_=hT[:, :, tg * 512:tg * 512 + 512]
                        .rearrange("k p t -> p k t")).then_inc(xl, 16)
        else:
            hT_ex, msel, ncper = sel
            nslot = NKI // ncper
            for hf in range(2):
                for slot in range(nslot):
                    g.dma_start(
                        out=hraw[:, (hf * NKI + slot * ncper) * 512:(hf * NKI + (slot + 1) * ncper) * 512]
                        .rearrange("p (k t) -> p k t", k=ncper),
                        in_=hT_ex[slot, :, :, hf * TL + tg * 512:hf * TL + tg * 512 + 512]
                        .rearrange("k p t -> p k t")).then_inc(xl, 16)

    with nc.Block(f"ffnB_{p.uid()}") as blk:
        @blk.gpsimd
        def _(g):
            h_load(g, 0)
            wd_load(g, 0)
            wd_load(g, 1)
            for tg in range(NT):
                for dc in range(NKD):
                    i = tg * NKD + dc
                    g.wait_ge(ax, s0["ax"] + i + 1)
                    g.dma_start(out=x_acc[dc * 128:dc * 128 + 128, tg * 512:tg * 512 + 512],
                                in_=fsc[:, (i % 2) * 512:(i % 2) * 512 + 512],
                                accum_op=ALU.add).then_inc(sg, 16)
                    if dc == NKD - 1 and tg + 1 < NT:
                        h_load(g, tg + 1)
                    if i + 2 < NTOT:
                        wd_load(g, i + 2)
            g.wait_ge(sg, s0["sg"] + 16 * NTOT)

        if sel is not None:
            @blk.vector
            def _(v):
                hT_ex, msel, ncper = sel
                for tg in range(NT):
                    v.wait_ge(xl, s0["xl"] + 16 * 4 * (tg + 1))
                    if tg >= 1:
                        v.wait_ge(mm, s0["mm"] + tg * NKD)
                    for k in range(NKI):
                        inst = v.tensor_scalar(hscr[:, :], hraw[:, k * 512:k * 512 + 512],
                                               msel[:, 0:1], None, ALU.mult, ALU.bypass)
                        inst = v.scalar_tensor_tensor(hblk[:, k * 512:k * 512 + 512],
                                                      hraw[:, (NKI + k) * 512:(NKI + k) * 512 + 512],
                                                      msel[:, 1:2], hscr[:, :], ALU.mult, ALU.add)
                    inst.then_inc(vx, 1)

        @blk.tensor
        def _(t):
            for tg in range(NT):
                if sel is None:
                    t.wait_ge(xl, s0["xl"] + 16 * (tg + 1))
                else:
                    t.wait_ge(vx, s0["vx"] + tg + 1)
                for dc in range(NKD):
                    i = tg * NKD + dc
                    t.wait_ge(wl, s0["wl"] + 16 * (i + 1))
                    if i >= 2:
                        t.wait_ge(ax, s0["ax"] + i - 1)
                    for k in range(NKI):
                        inst = t.matmul(psd[i % 2][:, :],
                                 wd_sb[:, (i % 2) * NKI * 128 + k * 128:(i % 2) * NKI * 128 + k * 128 + 128],
                                 hblk[:, k * 512:k * 512 + 512],
                                 start=(k == 0), stop=(k == NKI - 1))
                    inst.then_inc(mm, 1)

        @blk.scalar
        def _(s_):
            for tg in range(NT):
                for dc in range(NKD):
                    i = tg * NKD + dc
                    s_.wait_ge(mm, s0["mm"] + i + 1)
                    if i >= 2:
                        s_.wait_ge(sg, s0["sg"] + 16 * (i - 1))
                    s_.activation(out=fsc[:, (i % 2) * 512:(i % 2) * 512 + 512],
                                  in_=psd[i % 2][:, :], func=AF.Copy, scale=scale).then_inc(ax, 1)

    p.cnt["wl"] = s0["wl"] + 16 * NT * NKD
    p.cnt["xl"] = s0["xl"] + 16 * NT * (1 if sel is None else 4)
    p.cnt["mm"] = s0["mm"] + NT * NKD
    p.cnt["ax"] = s0["ax"] + NT * NKD
    p.cnt["sg"] = s0["sg"] + 16 * NT * NKD
    if sel is not None:
        p.cnt["vx"] = s0["vx"] + NT
    ls.close()




def emit_resid_add(p, c, osel, x_acc, scale=1.0, T=TL):
    nc = p.nc
    NT = T // 512
    ls = contextlib.ExitStack()
    tb = p.sbuf("ra_t", [128, 2 * NKD * 512], BF16, ls)
    fb = p.sbuf("ra_f", [128, 2 * 512], F32, ls)
    xl, ax, sg = (p.sem(k) for k in ("xl", "ax", "sg"))
    s0 = {k: p.cnt[k] for k in ("xl", "ax", "sg")}
    with nc.Block(f"resid_{p.uid()}") as blk:
        @blk.gpsimd
        def _(g):
            for tg in range(NT):
                if tg >= 2:
                    g.wait_ge(ax, s0["ax"] + (tg - 1) * NKD)
                g.dma_start(out=tb[:, (tg % 2) * NKD * 512:(tg % 2 + 1) * NKD * 512]
                            .rearrange("p (k t) -> p k t", k=NKD),
                            in_=osel[:, :, tg * 512:tg * 512 + 512]
                            .rearrange("k p t -> p k t")).then_inc(xl, 16)
                for dc in range(NKD):
                    i = tg * NKD + dc
                    g.wait_ge(ax, s0["ax"] + i + 1)
                    g.dma_start(out=x_acc[dc * 128:dc * 128 + 128, tg * 512:tg * 512 + 512],
                                in_=fb[:, (i % 2) * 512:(i % 2) * 512 + 512],
                                accum_op=ALU.add).then_inc(sg, 16)
            g.wait_ge(sg, s0["sg"] + 16 * NT * NKD)

        @blk.scalar
        def _(s_):
            for tg in range(NT):
                s_.wait_ge(xl, s0["xl"] + 16 * (tg + 1))
                for dc in range(NKD):
                    i = tg * NKD + dc
                    if i >= 2:
                        s_.wait_ge(sg, s0["sg"] + 16 * (i - 1))
                    s_.activation(out=fb[:, (i % 2) * 512:(i % 2) * 512 + 512],
                                  in_=tb[:, (tg % 2) * NKD * 512 + dc * 512:
                                         (tg % 2) * NKD * 512 + dc * 512 + 512],
                                  func=AF.Copy, scale=scale).then_inc(ax, 1)

    p.cnt["xl"] = s0["xl"] + 16 * NT
    p.cnt["ax"] = s0["ax"] + NT * NKD
    p.cnt["sg"] = s0["sg"] + 16 * NT * NKD
    ls.close()


def emit_deltanet(p, c, io, phases=(1, 2, 3, 4)):
    nc = p.nc
    H = 3
    OW = H * L_DH
    NPC = OW // 128
    NTG = TF // 512
    NCH = TF // L_C
    ls = contextlib.ExitStack()
    rows15 = p.sbuf("dn_rows15", [15, TF], F32, ls)
    selhot = p.sbuf("dn_selhot", [15, 15 * 128], F32, ls)
    trim = p.sbuf("dn_trim", [64, 192], BF16, ls)
    ct_sb = p.sbuf("dn_ct", [128, 3 * NPC * 4], F32, ls)
    onw_sb = p.sbuf("dn_onw", [128, 2], F32, ls)
    eA_sb = p.sbuf("dn_eA", [3, 1], F32, ls)
    dtb_sb = p.sbuf("dn_dtb", [3, 1], F32, ls)
    l12 = contextlib.ExitStack()
    beta_r = p.sbuf("dn_beta", [3, TF], F32, l12)
    row_t = p.sbuf("dn_rowt", [3, TF], F32, l12)
    xl, wl, mm, mmf, ax, vx, st = (p.sem(k) for k in ("xl", "wl", "mm", "mmf", "ax", "vx", "st"))

    l1 = contextlib.ExitStack()
    xg_sb = p.sbuf("dn_xg", [128, NKD * 512], BF16, l1)
    wA_sb = p.sbuf("dn_wA", [128, NKD * OW], BF16, l1)
    wB_sb = p.sbuf("dn_wB", [128, NKD * OW], BF16, l1)
    wb_sb = p.sbuf("dn_wb", [128, NKD * 3], BF16, l1)
    wa_sb = p.sbuf("dn_wa", [128, NKD * 3], BF16, l1)
    prj = p.sbuf("dn_prj", [128, 2 * NPC * 512], BF16, l1)
    ps_p = [p.psum("dn_pp", [128, 512], scope=l1) for _ in range(2)]
    ps_b = p.psum("dn_pb", [3, 512], scope=l1)
    ps_a2 = p.psum("dn_pa2", [3, 512], scope=l1)
    s0 = {k: p.cnt[k] for k in ("xl", "wl", "mm", "ax", "st")}
    GPT = 2 * NPC
    mm_of = {}; ax_of = {}
    mm_c, ax_c = s0["mm"], s0["ax"]
    for ps_ in range(2):
        for tg in range(NTG):
            for gi in range(GPT):
                mm_c += 1; mm_of[(ps_, tg, gi)] = mm_c
                ax_c += 1; ax_of[(ps_, tg, gi)] = ax_c
            if ps_ == 0:
                mm_c += 2; ax_c += 1
    mm_end1, ax_end1 = mm_c, ax_c

    def mm_tg_end(ps_, tg):
        return mm_of[(ps_, tg, GPT - 1)] + (2 if ps_ == 0 else 0)

    with nc.Block(f"dn1_{p.uid()}") as blk:
        @blk.gpsimd
        def _(g):
            g.dma_start(out=wA_sb[:].rearrange("p (k o) -> p k o", k=NKD),
                        in_=io["wqT"].rearrange("(k p) o -> p k o", p=128)).then_inc(wl, 16)
            g.dma_start(out=wB_sb[:].rearrange("p (k o) -> p k o", k=NKD),
                        in_=io["wkT"].rearrange("(k p) o -> p k o", p=128)).then_inc(wl, 16)
            g.dma_start(out=wb_sb[:].rearrange("p (k o) -> p k o", k=NKD),
                        in_=io["wbT"].rearrange("(k p) o -> p k o", p=128)).then_inc(wl, 16)
            g.dma_start(out=wa_sb[:].rearrange("p (k o) -> p k o", k=NKD),
                        in_=io["waT"].rearrange("(k p) o -> p k o", p=128)).then_inc(wl, 16)
            for tg in range(NTG):
                if tg >= 1:
                    g.wait_ge(mm, mm_tg_end(0, tg - 1) + (0 if tg > 1 else 0))
                if tg >= 1:
                    pass
                slot, tgl = divmod(tg, 4)
                g.dma_start(out=xg_sb[:].rearrange("p (k t) -> p k t", k=NKD),
                            in_=io["xg"][slot, :, tgl * 512:tgl * 512 + 512]
                            .rearrange("(k p) t -> p k t", p=128)).then_inc(xl, 16)
            g.wait_ge(mm, mm_tg_end(0, NTG - 1))
            g.dma_start(out=wA_sb[:].rearrange("p (k o) -> p k o", k=NKD),
                        in_=io["wvT"].rearrange("(k p) o -> p k o", p=128)).then_inc(wl, 16)
            g.dma_start(out=wB_sb[:].rearrange("p (k o) -> p k o", k=NKD),
                        in_=io["wgT"].rearrange("(k p) o -> p k o", p=128)).then_inc(wl, 16)
            for tg in range(NTG):
                g.wait_ge(mm, mm_tg_end(1, tg - 1) if tg >= 1 else mm_tg_end(0, NTG - 1))
                slot, tgl = divmod(tg, 4)
                g.dma_start(out=xg_sb[:].rearrange("p (k t) -> p k t", k=NKD),
                            in_=io["xg"][slot, :, tgl * 512:tgl * 512 + 512]
                            .rearrange("(k p) t -> p k t", p=128)).then_inc(xl, 16)

        @blk.tensor
        def _(t_):
            for ps_ in range(2):
                t_.wait_ge(wl, s0["wl"] + 16 * (4 if ps_ == 0 else 6))
                for tg in range(NTG):
                    t_.wait_ge(xl, s0["xl"] + 16 * (ps_ * NTG + tg + 1))
                    gi = 0
                    for w_s in (wA_sb, wB_sb):
                        for oc in range(NPC):
                            key = (ps_, tg, gi - 2)
                            if gi >= 2:
                                t_.wait_ge(ax, ax_of[key])
                            elif (ps_, tg) != (0, 0):
                                pt = (ps_, tg - 1) if tg else (0, NTG - 1)
                                t_.wait_ge(ax, ax_of[pt + (GPT - 2 + gi,)])
                            pp = ps_p[gi % 2]
                            for kk in range(NKD):
                                inst = t_.matmul(pp[:, :],
                                                 w_s[:, kk * OW + oc * 128:kk * OW + oc * 128 + 128],
                                                 xg_sb[:, kk * 512:(kk + 1) * 512],
                                                 start=(kk == 0), stop=(kk == NKD - 1))
                            inst.then_inc(mm, 1)
                            gi += 1
                    if ps_ == 0:
                        for kk in range(NKD):
                            inst = t_.matmul(ps_b[0:3, :], wb_sb[:, kk * 3:kk * 3 + 3],
                                             xg_sb[:, kk * 512:(kk + 1) * 512],
                                             start=(kk == 0), stop=(kk == NKD - 1))
                        inst.then_inc(mm, 1)
                        for kk in range(NKD):
                            inst = t_.matmul(ps_a2[0:3, :], wa_sb[:, kk * 3:kk * 3 + 3],
                                             xg_sb[:, kk * 512:(kk + 1) * 512],
                                             start=(kk == 0), stop=(kk == NKD - 1))
                        inst.then_inc(mm, 1)

        @blk.scalar
        def _(s_):
            for ps_ in range(2):
                names = (("qraw", "kraw") if ps_ == 0 else ("vraw", "gsp"))
                for tg in range(NTG):
                    gi = 0
                    for wi in range(2):
                        fn = AF.Silu if (ps_ == 1 and wi == 1) else AF.Copy
                        for oc in range(NPC):
                            s_.wait_ge(mm, mm_of[(ps_, tg, gi)])
                            if (tg or ps_) and gi == 0:
                                s_.wait_ge(st, s0["st"] + 16 * 2 * (ps_ * NTG + tg))
                            s_.activation(out=prj[:, (wi * NPC + oc) * 512:(wi * NPC + oc) * 512 + 512],
                                          in_=ps_p[gi % 2][:, :], func=fn).then_inc(ax, 1)
                            gi += 1
                    if ps_ == 0:
                        s_.wait_ge(mm, mm_tg_end(0, tg))
                        sl = slice(tg * 512, tg * 512 + 512)
                        s_.activation(out=beta_r[0:3, sl], in_=ps_b[0:3, :], func=AF.Sigmoid)
                        s_.activation(out=row_t[0:3, sl], in_=ps_a2[0:3, :], func=AF.Copy).then_inc(ax, 1)

        @blk.sync
        def _(sy):
            for ps_ in range(2):
                names = (("qraw", "kraw") if ps_ == 0 else ("vraw", "gsp"))
                for tg in range(NTG):
                    sy.wait_ge(ax, ax_of[(ps_, tg, GPT - 1)])
                    for wi, dst in enumerate(names):
                        sy.dma_start(out=io[dst][:, :, tg * 512:tg * 512 + 512].rearrange("o p t -> p o t"),
                                     in_=prj[:, wi * NPC * 512:(wi + 1) * NPC * 512]
                                     .rearrange("p (o t) -> p o t", o=NPC)).then_inc(st, 16)
            sy.wait_ge(st, s0["st"] + 16 * 4 * NTG)

    p.cnt["xl"] = s0["xl"] + 16 * 2 * NTG
    p.cnt["wl"] = s0["wl"] + 16 * 6
    p.cnt["mm"] = mm_end1
    p.cnt["ax"] = ax_end1
    p.cnt["st"] = s0["st"] + 16 * 4 * NTG
    l1.close()
    if 2 not in phases:
        l12.close()
        ls.close()
        return
    l2 = contextlib.ExitStack()
    ga_r = p.sbuf("dn_ga", [3, TF], F32, l2)
    ga2_r = p.sbuf("dn_ga2", [3, TF], F32, l2)
    raw = p.sbuf("dn_raw", [128, TF], BF16, l2)
    acc = p.sbuf("dn_acc", [128, TF], BF16, l2)
    tmp = p.sbuf("dn_tmp", [128, TF], BF16, l2)
    sq2 = p.sbuf("dn_sq", [128, 2 * TF], BF16, l2)
    cnv = p.sbuf("dn_cnv", [128, 2 * TF], BF16, l2)
    nrm = p.sbuf("dn_nrm", [128, 4 * 512], BF16, l2)
    vt4 = p.sbuf("dn_vt4", [128, 2 * 512], BF16, l2)
    vrow = p.sbuf("dn_vrow", [128, TF], BF16, l2)
    rbt = p.sbuf("dn_rbt", [128, 2 * 512], BF16, l2)
    tmod = p.sbuf("dn_tmod", [3, TF], mybir.dt.int8, l2)
    mskb = p.sbuf("dn_mskb", [3, TF], mybir.dt.int8, l2)
    ep_r = p.sbuf("dn_ep", [3, TF], BF16, l2)
    rn2 = p.sbuf("dn_rn2", [65, TF], BF16, l2)
    rk2 = p.sbuf("dn_rk2", [65, TF], BF16, l2)
    rnrow = p.sbuf("dn_rnr", [1, 512], F32, l2)
    ps_ss = p.psum("dn_pss", [1, 512], scope=l2)
    ps_rb = p.psum("dn_prb", [128, 512], scope=l2)
    ps_tt = p.psum("dn_ptt", [128, 512], BF16, scope=l2)
    sg = p.sem("sg")
    s0 = {k: p.cnt[k] for k in ("xl", "wl", "mm", "ax", "vx", "gx", "sg")}
    gx = p.sem("gx")

    NROW = 18
    V_conv = [0] * NROW; A_sq = [0] * NROW
    M_ss = {}; A_rn = {}; M_rb = {}; A_rb = {}; V_m = {}
    M_tt = {}; A_tt = {}; ST = {}
    vx_c = s0["vx"] + 4
    ax_c = s0["ax"] + 3
    mm_c = s0["mm"]
    st_c = s0["sg"] + 16 * 5
    for i in range(NROW):
        mi, pc = divmod(i, NPC)
        vx_c += 1; V_conv[i] = vx_c
        ax_c += 1; A_sq[i] = ax_c
        if mi < 2 and pc % 2 == 1:
            for tg in range(NTG):
                mm_c += 1; M_ss[(i, tg)] = mm_c
                ax_c += 1; A_rn[(i, tg)] = ax_c
            for pc2 in range(2):
                for tg in range(NTG):
                    mm_c += 1; M_rb[(i, pc2, tg)] = mm_c
                    ax_c += 1; A_rb[(i, pc2, tg)] = ax_c
                    vx_c += 1; V_m[(i, pc2, tg)] = vx_c
                    st_c += 16; ST[(i, pc2, tg)] = st_c
        if mi == 2:
            for tg in range(NTG):
                mm_c += 1; M_tt[(i, tg)] = mm_c
                ax_c += 1; A_tt[(i, tg)] = ax_c
                st_c += 16; ST[(i, tg)] = st_c
    vx_end, ax_end, mm_end, st_end = vx_c, ax_c, mm_c, st_c
    applies = sorted([k for k in V_m], key=lambda k: V_m[k])
    vdmas = sorted([k for k in M_tt], key=lambda k: M_tt[k])

    ssc = [p.cnt["mmf"]]
    with nc.Block(f"dn2_{p.uid()}") as blk:
        @blk.gpsimd
        def _(g):
            g.dma_start(out=ct_sb[:].rearrange("p (m c t) -> p m c t", m=3, c=NPC),
                        in_=io["convw"].rearrange("m (c p) t -> p m c t", p=128)).then_inc(wl, 16)
            g.dma_start(out=trim[:, :], in_=io["trimask"]).then_inc(wl, 16)
            g.dma_start(out=selhot[:, :], in_=io["selhot"]).then_inc(wl, 16)
            g.dma_start(out=onw_sb[:, :], in_=io["onw"].rearrange("(a p) -> p a", p=128), allow_slow_non_contiguous=True).then_inc(wl, 16)
            g.dma_start(out=eA_sb[:, :], in_=io["alog"].rearrange("(a b) -> a b", b=1)).then_inc(wl, 16)
            g.dma_start(out=dtb_sb[:, :], in_=io["dtb"].rearrange("(a b) -> a b", b=1)).then_inc(wl, 16)
            g.iota(tmod[0:3, :].rearrange("r (a b) -> r a b", a=64), [[0, 64], [1, 64]],
                   base=0, channel_multiplier=0, allow_small_or_imprecise_dtypes=True).then_inc(gx, 1)
            g.wait_ge(vx, s0["vx"] + 4)
            g.wait_ge(ax, s0["ax"] + 3)
            g.dma_start(out=rows15[0:3, :], in_=ga_r[0:3, :]).then_inc(sg, 16)
            g.dma_start(out=rows15[3:6, :], in_=beta_r[0:3, :]).then_inc(sg, 16)
            g.dma_start(out=rows15[6:9, :], in_=ep_r[0:3, :]).then_inc(sg, 16)
            g.dma_start(out=rows15[9:12, :], in_=row_t[0:3, :]).then_inc(sg, 16)
            g.dma_start(out=rows15[12:15, :], in_=ga2_r[0:3, :]).then_inc(sg, 16)

            def spills(g, j):
                mi, pc = divmod(j, NPC)
                if mi < 2 and pc % 2 == 1:
                    for pc2b in range(2):
                        for tg in range(NTG):
                            key = (j, pc2b, tg)
                            g.wait_ge(vx, V_m[key])
                            g.dma_start(out=io[("qhat", "khat")[mi]][pc - 1 + pc2b, :, tg * 512:tg * 512 + 512],
                                        in_=nrm[:, (V_m[key] % 4) * 512:(V_m[key] % 4) * 512 + 512]).then_inc(sg, 16)
                elif mi == 2:
                    for tg in range(NTG):
                        g.wait_ge(ax, A_tt[(j, tg)])
                        g.dma_start(out=io["vtok"][tg * 512:(tg + 1) * 512, (j - 12) * 128:(j - 11) * 128]
                                    .rearrange("(a p) o -> p a o", p=128),
                                    in_=vt4[:, (tg % 2) * 512:(tg % 2) * 512 + 512]
                                    .rearrange("p (a o) -> p a o", o=128)).then_inc(sg, 16)

            for i in range(NROW):
                if i >= 2:
                    spills(g, i - 2)
                mi, pc = divmod(i, NPC)
                if i >= 1:
                    g.wait_ge(vx, V_conv[i - 1])
                g.dma_start(out=raw[:, :],
                            in_=io[("qraw", "kraw", "vraw")[mi]][pc, :, :]).then_inc(xl, 16)
            spills(g, NROW - 2)
            spills(g, NROW - 1)
            g.wait_ge(sg, st_end)

        @blk.scalar
        def _(s_):
            def ssync(inst):
                ssc[0] += 1
                inst.then_inc(mmf, 1)
                s_.wait_ge(mmf, ssc[0])
            s_.wait_ge(wl, s0["wl"] + 16 * 6)
            s_.activation(out=eA_sb[0:3, :], in_=eA_sb[0:3, :], func=AF.Exp)
            s_.wait_ge(vx, s0["vx"] + 1)
            ssync(s_.activation(out=ga_r[0:3, :], in_=ga2_r[0:3, :], func=AF.Abs))
            ssync(s_.activation(out=ga_r[0:3, :], in_=ga_r[0:3, :], func=AF.Exp, scale=-1.0))
            s_.activation(out=ga_r[0:3, :], in_=ga_r[0:3, :], func=AF.Ln, bias=1.0).then_inc(ax, 1)
            s_.wait_ge(vx, s0["vx"] + 3)
            s_.activation(out=ep_r[0:3, :], in_=ga_r[0:3, :], func=AF.Exp).then_inc(ax, 1)
            s_.wait_ge(vx, s0["vx"] + 4)
            s_.activation(out=ga2_r[0:3, :], in_=ga2_r[0:3, :], func=AF.Exp).then_inc(ax, 1)
            for i in range(NROW):
                mi, pc = divmod(i, NPC)
                s_.wait_ge(vx, V_conv[i])
                if mi < 2:
                    pc2 = pc % 2
                    if pc2 == 0 and i >= 2:
                        pa = i - 1 if (i - 1) % NPC % 2 == 1 and (i - 1) // NPC < 2 else i - 2
                        if pa >= 0 and pa // NPC < 2 and pa % NPC % 2 == 1:
                            s_.wait_ge(vx, V_m[(pa, 1, NTG - 1)])
                    ssync(s_.activation(out=cnv[:, pc2 * TF:(pc2 + 1) * TF], in_=acc[:, :], func=AF.Silu))
                    s_.activation(out=sq2[:, pc2 * TF:(pc2 + 1) * TF],
                                  in_=cnv[:, pc2 * TF:(pc2 + 1) * TF], func=AF.Square).then_inc(ax, 1)
                    if pc2 == 1:
                        hh = (pc - 1) // 2
                        rr = (rn2, rk2)[mi]
                        rsl = slice(32 * hh, 32 * hh + 1)
                        for tg in range(NTG):
                            sl = slice(tg * 512, tg * 512 + 512)
                            s_.wait_ge(mm, M_ss[(i, tg)])
                            ssync(s_.activation(out=rnrow[0:1, :], in_=ps_ss[0:1, :], func=AF.Ln, bias=1e-6))
                            s_.activation(out=rr[rsl, sl], in_=rnrow[0:1, :],
                                          func=AF.Exp, scale=-0.5).then_inc(ax, 1)
                        for pc2b in range(2):
                            for tg in range(NTG):
                                s_.wait_ge(mm, M_rb[(i, pc2b, tg)])
                                k2 = (pc2b, tg)
                                if k2 > (0, 1):
                                    prev2 = (pc2b, tg - 2) if tg >= 2 else (pc2b - 1, NTG - 2 + tg)
                                    s_.wait_ge(vx, V_m[(i,) + prev2])
                                s_.activation(out=rbt[:, (tg % 2) * 512:(tg % 2) * 512 + 512],
                                              in_=ps_rb[:, :], func=AF.Copy).then_inc(ax, 1)
                else:
                    if i >= 13:
                        s_.wait_ge(mm, M_tt[(i - 1, NTG - 1)])
                    s_.activation(out=vrow[:, :], in_=acc[:, :], func=AF.Silu).then_inc(ax, 1)
                    for tg in range(NTG):
                        s_.wait_ge(mm, M_tt[(i, tg)])
                        if (i, tg) > (12, 1):
                            pk = (i, tg - 2) if tg >= 2 else (i - 1, NTG - 2 + tg)
                            s_.wait_ge(sg, ST[pk])
                        for tc4 in range(4):
                            inst = s_.activation(
                                out=vt4[:, (tg % 2) * 512 + tc4 * 128:(tg % 2) * 512 + tc4 * 128 + 128],
                                in_=ps_tt[:, tc4 * 128:tc4 * 128 + 128], func=AF.Copy)
                        inst.then_inc(ax, 1)

        @blk.vector
        def _(v):
            v.wait_ge(wl, s0["wl"] + 16 * 6)
            v.tensor_scalar(ga2_r[0:3, :], row_t[0:3, :], dtb_sb[0:3, 0:1], None,
                            ALU.add, ALU.bypass).then_inc(vx, 1)
            v.wait_ge(ax, s0["ax"] + 1)
            v.tensor_scalar(row_t[0:3, :], ga2_r[0:3, :], 0.0, None, ALU.max, ALU.bypass)
            v.tensor_tensor(row_t[0:3, :], row_t[0:3, :], ga_r[0:3, :], ALU.add)
            v.tensor_scalar(ga_r[0:3, :], row_t[0:3, :], eA_sb[0:3, 0:1], -1.0,
                            ALU.mult, ALU.mult).then_inc(vx, 1)
            v.wait_ge(gx, s0["gx"] + 1)
            src_, dst_ = ga_r, ga2_r
            for ss in range(6):
                w = 1 << ss
                v.tensor_scalar(mskb[0:3, :], tmod[0:3, :], float(w), None, ALU.is_ge, ALU.bypass)
                v.tensor_tensor(row_t[0:3, 0:TF - w], src_[0:3, 0:TF - w], mskb[0:3, w:TF], ALU.mult)
                v.tensor_copy(dst_[0:3, 0:w], src_[0:3, 0:w])
                v.tensor_tensor(dst_[0:3, w:TF], src_[0:3, w:TF], row_t[0:3, 0:TF - w], ALU.add)
                src_, dst_ = dst_, src_
            if src_ is not ga_r:
                v.tensor_copy(ga_r[0:3, :], src_[0:3, :])
            v.tensor_copy(ga2_r[0:3, :], ga_r[0:3, :]).then_inc(vx, 1)
            v.wait_ge(ax, s0["ax"] + 2)
            v.tensor_tensor(row_t[0:3, :], beta_r[0:3, :], ep_r[0:3, :], ALU.mult)
            gae = bass.AP(ga_r.ap().tensor, 63, [ga_r.ap().ap[0], [64, 64], [0, 64]])
            v.tensor_tensor(ga2_r[0:3, :].rearrange("r (a b) -> r a b", a=64),
                            gae, ga_r[0:3, :].rearrange("r (a b) -> r a b", a=64),
                            ALU.subtract).then_inc(vx, 1)
            for i in range(NROW):
                mi, pc = divmod(i, NPC)
                v.wait_ge(xl, s0["xl"] + 16 * (i + 1))
                rb_ = raw[:, :]
                wcol = lambda tap, mi=mi, pc=pc: ct_sb[:, (mi * NPC + pc) * 4 + tap:(mi * NPC + pc) * 4 + tap + 1]
                if i >= 1:
                    v.wait_ge(ax, A_sq[i - 1])
                inst = v.tensor_scalar(acc[:, :], rb_, wcol(3), None, ALU.mult, ALU.bypass)
                for tap in range(3):
                    d = 3 - tap
                    v.tensor_scalar(tmp[:, 0:TF - d], rb_[:, 0:TF - d], wcol(tap), None, ALU.mult, ALU.bypass)
                    inst = v.tensor_tensor(acc[:, d:TF], acc[:, d:TF], tmp[:, 0:TF - d], ALU.add)
                inst.then_inc(vx, 1)
                if mi < 2 and pc % 2 == 1:
                    for pc2b in range(2):
                        for tg in range(NTG):
                            key = (i, pc2b, tg)
                            v.wait_ge(ax, A_rb[key])
                            kidx = V_m[key]
                            aidx = applies.index(key)
                            if aidx >= 4:
                                v.wait_ge(sg, ST[applies[aidx - 4]])
                            v.tensor_tensor(nrm[:, (kidx % 4) * 512:(kidx % 4) * 512 + 512],
                                            cnv[:, pc2b * TF + tg * 512:pc2b * TF + tg * 512 + 512],
                                            rbt[:, (tg % 2) * 512:(tg % 2) * 512 + 512],
                                            ALU.mult).then_inc(vx, 1)

        @blk.tensor
        def _(t_):
            for i in range(NROW):
                mi, pc = divmod(i, NPC)
                if mi < 2 and pc % 2 == 1:
                    rr = (rn2, rk2)[mi]
                    hh = (pc - 1) // 2
                    rsl = slice(32 * hh, 32 * hh + 1)
                    sskeys = sorted(M_ss, key=M_ss.get)
                    for tg in range(NTG):
                        t_.wait_ge(ax, A_sq[i])
                        idx = sskeys.index((i, tg))
                        if idx >= 1:
                            t_.wait_ge(ax, A_rn[sskeys[idx - 1]])
                        for pc2 in range(2):
                            inst = t_.matmul(ps_ss[0:1, :], c["onesc"][:, :],
                                             sq2[:, pc2 * TF + tg * 512:pc2 * TF + tg * 512 + 512],
                                             start=(pc2 == 0), stop=(pc2 == 1))
                        inst.then_inc(mm, 1)
                    for pc2b in range(2):
                        for tg in range(NTG):
                            t_.wait_ge(ax, A_rn[(i, NTG - 1)])
                            key = (pc2b, tg)
                            if key != (0, 0):
                                prev = (pc2b, tg - 1) if tg else (pc2b - 1, NTG - 1)
                                t_.wait_ge(ax, A_rb[(i,) + prev])
                            else:
                                prevs = [kk for kk in A_rb if kk[0] < i]
                                if prevs:
                                    t_.wait_ge(ax, A_rb[max(prevs, key=lambda kk: A_rb[kk])])
                            sl = slice(tg * 512, tg * 512 + 512)
                            t_.matmul(ps_rb[:, :], c["ones65"][rsl, 0:128], rr[rsl, sl],
                                      start=True, stop=True).then_inc(mm, 1)
                elif mi == 2:
                    for tg in range(NTG):
                        t_.wait_ge(ax, A_sq[i])
                        if tg >= 1:
                            t_.wait_ge(ax, A_tt[(i, tg - 1)])
                        elif i >= 13:
                            t_.wait_ge(ax, A_tt[(i - 1, NTG - 1)])
                        for tc4 in range(4):
                            inst = t_.transpose(ps_tt[:, tc4 * 128:tc4 * 128 + 128],
                                                vrow[:, tg * 512 + tc4 * 128:tg * 512 + tc4 * 128 + 128],
                                                c["ident"][:, :])
                        inst.then_inc(mm, 1)

    p.cnt["xl"] = s0["xl"] + 16 * NROW
    p.cnt["wl"] = s0["wl"] + 16 * 6
    p.cnt["mm"] = mm_end
    p.cnt["ax"] = ax_end
    p.cnt["vx"] = vx_end
    p.cnt["gx"] = s0["gx"] + 1
    p.cnt["sg"] = st_end
    p.cnt["mmf"] = ssc[0]
    l2.close()
    l12.close()
    if 3 not in phases:
        ls.close()
        return
    l3 = contextlib.ExitStack()
    identf = p.sbuf("dn_identf", [128, 128], F32, l3)
    qch = p.sbuf("dn_qch", [128, 2 * NPC * 64], BF16, l3)
    kch = p.sbuf("dn_kch", [128, 2 * NPC * 64], BF16, l3)
    vch = p.sbuf("dn_vch", [64, 2 * OW], BF16, l3)
    tcol = p.sbuf("dn_tcol", [64, 16], F32, l3)
    e_sb = p.sbuf("dn_e", [64, 64], F32, l3)
    e2_sb = p.sbuf("dn_e2", [64, 64], F32, l3)
    d1_sb = p.sbuf("dn_d1", [64, 64], F32, l3)
    w1_sb = p.sbuf("dn_w1", [64, 64], F32, l3)
    mb_sb = p.sbuf("dn_mb", [64, 64], F32, l3)
    n_sb = p.sbuf("dn_n", [64, 64], BF16, l3)
    q1_sb = p.sbuf("dn_q1", [64, 64], BF16, l3)
    gt_sb = p.sbuf("dn_gt", [64, 64], BF16, l3)
    g_sb = p.sbuf("dn_g", [64, 64], BF16, l3)
    pq_sb = p.sbuf("dn_pq", [64, 4 * 64], BF16, l3)
    ep_t = p.sbuf("dn_ept", [128, 64], BF16, l3)
    bep_t = p.sbuf("dn_bept", [128, 64], BF16, l3)
    eba_t = p.sbuf("dn_ebat", [128, 64], BF16, l3)
    ala_c = p.sbuf("dn_ala", [128, 1], F32, l3)
    kp_sb = p.sbuf("dn_kp", [128, 2 * 64], BF16, l3)
    qp_sb = p.sbuf("dn_qp", [128, 2 * 64], BF16, l3)
    kb_sb = p.sbuf("dn_kb", [128, 2 * 64], BF16, l3)
    kbt_sb = p.sbuf("dn_kbt", [64, 256], BF16, l3)
    b_sb = p.sbuf("dn_b", [64, 256], BF16, l3)
    u_sb = p.sbuf("dn_u", [64, 256], BF16, l3)
    S_f = p.sbuf("dn_S", [128, 3 * 2 * 256], F32, l3)
    S_b = p.sbuf("dn_Sb", [128, 3 * 2 * 256], BF16, l3)
    obuf = p.sbuf("dn_obuf", [128, NPC * TF], BF16, l3)
    l3p = contextlib.ExitStack()
    psGA = p.psum("dn_psGA", [64, 64], scope=l3p)
    psBB = p.psum("dn_psBB", [64, 64], scope=l3p)
    psMM = p.psum("dn_psMM", [64, 64], scope=l3p)
    psPU = p.psum("dn_psPU", [64, 256], scope=l3p)
    psO0 = p.psum("dn_psO0", [128, 256], scope=l3p)
    psO1 = p.psum("dn_psO1", [128, 256], scope=l3p)
    psT2 = p.psum("dn_psT2", [64, 256], BF16, scope=l3p)
    psTC = p.psum("dn_psTC", [64, 16], scope=l3p)

    class _Sched:
        def __init__(self, p):
            self.p = p
            self.names = {"t": "mm", "s": "ax", "v": "vx", "g": "xl", "y": "st"}
            self.sems = {e: p.sem(nm) for e, nm in self.names.items()}
            self.cnt = {e: p.cnt[nm] for e, nm in self.names.items()}
            self.events = {e: [] for e in self.names}

        def ev(self, eng, deps, fn, inc=None):
            if inc is None:
                inc = 16 if eng in "gy" else 1
            self.cnt[eng] += inc
            h = (eng, self.cnt[eng])
            waits = {}
            for d in deps:
                if d is None:
                    continue
                de, dc = d
                if de != eng:
                    waits[de] = max(waits.get(de, 0), dc)
            self.events[eng].append((waits, fn, inc))
            return h

        def body(self, eng, proxy):
            for waits, fn, inc in self.events[eng]:
                for de, dc in waits.items():
                    proxy.wait_ge(self.sems[de], dc)
                fn(proxy).then_inc(self.sems[eng], inc)

        def finalize(self):
            for e, nm in self.names.items():
                self.p.cnt[nm] = self.cnt[e]

    sch = _Sched(p)
    NCHK = TF // L_C
    NHD = 3
    if DN3_LIMIT is not None:
        NCHK, NHD = DN3_LIMIT
    L = {}

    e_idf = sch.ev("v", [], lambda v: v.tensor_copy(identf[:, :], c["ident"][:, :]))
    e_S0 = sch.ev("v", [], lambda v: v.memset(S_f[:, :], 0.0))
    e_S0b = sch.ev("v", [e_S0], lambda v: v.memset(S_b[:, :], 0.0))

    def chsl(sb, nb, oc, w=64):
        return sb[:, (nb * NPC + oc) * 64:(nb * NPC + oc) * 64 + w]

    for n in range(NCHK):
        nb = n % 2
        c0 = n * L_C
        m2l = (n - 2) * 3 + NHD - 1
        depL = [L.get(("alused", m2l)), L.get(("bdone", m2l)), L.get(("mq", m2l))]
        L[("ldq", n)] = sch.ev("g", depL, lambda g, n=n, nb=nb: g.dma_start(
            out=qch[:, nb * NPC * 64:(nb + 1) * NPC * 64].rearrange("p (o t) -> p o t", o=NPC),
            in_=io["qhat"][:, :, n * L_C:(n + 1) * L_C].rearrange("o p t -> p o t")))
        L[("ldk", n)] = sch.ev("g", depL, lambda g, n=n, nb=nb: g.dma_start(
            out=kch[:, nb * NPC * 64:(nb + 1) * NPC * 64].rearrange("p (o t) -> p o t", o=NPC),
            in_=io["khat"][:, :, n * L_C:(n + 1) * L_C].rearrange("o p t -> p o t")))
        L[("ldv", n)] = sch.ev("g", depL, lambda g, n=n, nb=nb: g.dma_start(
            out=vch[0:64, nb * OW:(nb + 1) * OW],
            in_=io["vtok"][n * L_C:(n + 1) * L_C, :]))
        e_tcT = sch.ev("t", [e_idf, L.get(("tc", n - 1))],
                       lambda t_, c0=c0: t_.transpose(psTC[:, 0:15], rows15[0:15, c0:c0 + 64], identf[0:15, 0:15]))
        e_tc = sch.ev("s", [e_tcT, L.get(("bdone", n * 3 - 1))],
                      lambda s_: s_.activation(out=tcol[:, 0:15], in_=psTC[:, 0:15], func=AF.Copy))
        L[("tc", n)] = e_tc
        for h in range(NHD):
            m = n * 3 + h
            gac = tcol[:, h:h + 1]
            bc = tcol[:, 3 + h:4 + h]
            sm0 = S_b[:, (h * 2 + 0) * 256:(h * 2 + 0) * 256 + 256]
            sm1 = S_b[:, (h * 2 + 1) * 256:(h * 2 + 1) * 256 + 256]
            sf0 = S_f[:, (h * 2 + 0) * 256:(h * 2 + 0) * 256 + 256]
            sf1 = S_f[:, (h * 2 + 1) * 256:(h * 2 + 1) * 256 + 256]
            e_ga = sch.ev("t", [e_tc, L.get(("pqP", m - 1)), L.get(("d1", m - 1)), L.get(("oev", m - 1))],
                          lambda t_, c0=c0, h=h: t_.matmul(psGA[:, :], selhot[:, h * 128:h * 128 + 64],
                                                           rows15[:, c0:c0 + 64], start=True, stop=True))
            e_bb = sch.ev("t", [e_tc, L.get(("pqQ", m - 1)), L.get(("mb", m - 1))],
                          lambda t_, c0=c0, h=h: t_.matmul(psBB[:, :], selhot[:, (3 + h) * 128:(3 + h) * 128 + 64],
                                                           rows15[:, c0:c0 + 64], start=True, stop=True))

            def f_al(t_, c0=c0, h=h):
                t_.matmul(psO1[:, 0:64], selhot[:, (6 + h) * 128:(6 + h) * 128 + 128],
                          rows15[:, c0:c0 + 64], start=True, stop=True)
                t_.matmul(psO1[:, 64:128], selhot[:, (9 + h) * 128:(9 + h) * 128 + 128],
                          rows15[:, c0:c0 + 64], start=True, stop=True)
                t_.matmul(psO1[:, 128:192], selhot[:, (12 + h) * 128:(12 + h) * 128 + 128],
                          rows15[:, c0:c0 + 64], start=True, stop=True)
                return t_.matmul(psO1[:, 192:193], selhot[:, h * 128:h * 128 + 128],
                                 rows15[:, c0 + 63:c0 + 64], start=True, stop=True)
            e_al = sch.ev("t", [e_tc, L.get(("s2", m - 1))], f_al)

            def f_alc(s_):
                s_.activation(out=ep_t[:, :], in_=psO1[:, 0:64], func=AF.Copy)
                s_.activation(out=bep_t[:, :], in_=psO1[:, 64:128], func=AF.Copy)
                s_.activation(out=eba_t[:, :], in_=psO1[:, 128:192], func=AF.Copy)
                return s_.activation(out=ala_c[:, :], in_=psO1[:, 192:193], func=AF.Exp)
            e_alc = sch.ev("s", [e_al, L.get(("s2", m - 1)), L.get(("alused", m - 1))], f_alc)
            e_d1 = sch.ev("v", [e_ga], lambda v, gac=gac: v.tensor_scalar(
                d1_sb[:, :], psGA[:, :], gac, None, ALU.subtract, ALU.bypass))
            L[("d1", m)] = e_d1
            e_e0 = sch.ev("v", [e_d1, L.get(("eused", m - 1))], lambda v: v.tensor_scalar(
                w1_sb[:, :], d1_sb[:, :], 0.0, None, ALU.min, ALU.bypass))
            e_eE = sch.ev("s", [e_e0, L.get(("eused", m - 1))],
                          lambda s_: s_.activation(out=e_sb[:, :], in_=w1_sb[:, :], func=AF.Exp))
            e_e2p = sch.ev("v", [e_d1, e_eE], lambda v: v.tensor_scalar(
                mb_sb[:, :], d1_sb[:, :], -1.0, 0.0, ALU.mult, ALU.min))
            e_eE2 = sch.ev("s", [e_e2p, L.get(("e2used", m - 1))],
                           lambda s_: s_.activation(out=e2_sb[:, :], in_=mb_sb[:, :], func=AF.Exp))
            e_mt = sch.ev("t", [L[("ldk", n)], L.get(("mtused", m - 1))],
                          lambda t_, nb=nb, h=h: [
                              t_.matmul(psMM[:, :], chsl(kch, nb, h * 2), chsl(kch, nb, h * 2),
                                        start=True, stop=False),
                              t_.matmul(psMM[:, :], chsl(kch, nb, h * 2 + 1), chsl(kch, nb, h * 2 + 1),
                                        start=False, stop=True)][-1])
            e_w1 = sch.ev("v", [e_mt, e_eE], lambda v: v.tensor_tensor(
                w1_sb[:, :], psMM[:, :], e_sb[:, :], ALU.mult))
            e_mb = sch.ev("v", [e_bb, e_eE2], lambda v: v.tensor_tensor(
                mb_sb[:, :], psBB[:, :], trim[:, 0:64], ALU.mult))
            L[("mb", m)] = e_mb
            e_wt = sch.ev("v", [e_w1, e_mb], lambda v: v.tensor_tensor(
                d1_sb[:, :], w1_sb[:, :], mb_sb[:, :], ALU.mult))
            e_n = sch.ev("v", [e_wt, L.get(("u", m - 1))], lambda v: v.tensor_scalar(
                n_sb[:, :], d1_sb[:, :], -1.0, None, ALU.mult, ALU.bypass))
            e_q1a = sch.ev("v", [e_mt, e_eE2], lambda v: v.tensor_tensor(
                w1_sb[:, :], psMM[:, :], e2_sb[:, :], ALU.mult))
            L[("e2used", m)] = e_q1a
            e_q1b = sch.ev("v", [e_q1a], lambda v: v.tensor_tensor(
                w1_sb[:, :], w1_sb[:, :], trim[:, 128:192], ALU.mult))
            e_q1 = sch.ev("v", [e_q1b, L.get(("pqQ0", m - 1))], lambda v, bc=bc: v.tensor_scalar(
                q1_sb[:, :], w1_sb[:, :], bc, -1.0, ALU.mult, ALU.mult))
            e_mq = sch.ev("t", [e_q1a, L[("ldq", n)], L.get(("gtused", m - 1))],
                          lambda t_, nb=nb, h=h: [
                              t_.matmul(psMM[:, :], chsl(kch, nb, h * 2), chsl(qch, nb, h * 2),
                                        start=True, stop=False),
                              t_.matmul(psMM[:, :], chsl(kch, nb, h * 2 + 1), chsl(qch, nb, h * 2 + 1),
                                        start=False, stop=True)][-1])
            L[("mq", m)] = e_mq
            e_gt1 = sch.ev("v", [e_mq], lambda v: v.tensor_tensor(
                w1_sb[:, :], psMM[:, :], e_sb[:, :], ALU.mult))
            L[("eused", m)] = e_gt1
            e_gt = sch.ev("v", [e_gt1, L.get(("gtused", m - 1))], lambda v: v.tensor_tensor(
                gt_sb[:, :], w1_sb[:, :], trim[:, 64:128], ALU.mult))
            e_g0 = sch.ev("v", [e_n, L.get(("u", m - 1))], lambda v: v.tensor_tensor(
                g_sb[:, :], n_sb[:, :], c["ident"][0:64, 0:64], ALU.add))
            e_p0 = sch.ev("v", [e_n, L.get(("pqused", m - 1))], lambda v: v.tensor_copy(
                pq_sb[:, 0:64], n_sb[:, :]))
            e_q0 = sch.ev("v", [e_q1], lambda v: v.tensor_copy(pq_sb[:, 64:128], q1_sb[:, :]))
            eg, ep_, eq = e_g0, e_p0, e_q0
            for it in range(5):
                pb, qb = (it % 2) * 128, (it % 2) * 128 + 64
                pn, qn = ((it + 1) % 2) * 128, ((it + 1) % 2) * 128 + 64
                e_m2 = sch.ev("t", [ep_, eq, e_gt1 if it == 0 else eg,
                                    e_d1 if it == 0 else None], lambda t_, pb=pb, qb=qb: t_.matmul(
                    psGA[:, :], pq_sb[:, qb:qb + 64], pq_sb[:, pb:pb + 64], start=True, stop=True))
                e_m3 = sch.ev("t", [ep_, eq, e_m2, e_mb if it == 0 else None], lambda t_, pb=pb, qb=qb: t_.matmul(
                    psBB[:, :], pq_sb[:, pb:pb + 64], pq_sb[:, qb:qb + 64], start=True, stop=True))
                ep_ = sch.ev("s", [e_m2], lambda s_, pn=pn: s_.activation(
                    out=pq_sb[:, pn:pn + 64], in_=psGA[:, :], func=AF.Copy))
                eq = sch.ev("s", [e_m3], lambda s_, qn=qn: s_.activation(
                    out=pq_sb[:, qn:qn + 64], in_=psBB[:, :], func=AF.Copy))
                e_m1 = sch.ev("t", [eg, eq, e_m3], lambda t_, qn=qn: t_.matmul(
                    psMM[:, :], pq_sb[:, qn:qn + 64], g_sb[:, :], start=True, stop=True))
                eg = sch.ev("v", [e_m1], lambda v: v.tensor_tensor(
                    g_sb[:, :], g_sb[:, :], psMM[:, :], ALU.add))
            L[("sq", m)] = eg
            L[("mtused", m)] = eg
            L[("pqP", m)] = ep_
            L[("pqQ", m)] = eq
            L[("pqQ0", m)] = eq
            L[("pqused", m)] = eg
            e_kp = sch.ev("v", [e_alc, L[("ldk", n)], L.get(("p2", m - 1))], lambda v, nb=nb, h=h: [
                v.tensor_tensor(kp_sb[:, 0:64], chsl(kch, nb, h * 2), bep_t[:, :], ALU.mult),
                v.tensor_tensor(kp_sb[:, 64:128], chsl(kch, nb, h * 2 + 1), bep_t[:, :], ALU.mult)][-1])
            e_qp = sch.ev("v", [e_alc, L[("ldq", n)], L.get(("oq", m - 1))], lambda v, nb=nb, h=h: [
                v.tensor_tensor(qp_sb[:, 0:64], chsl(qch, nb, h * 2), ep_t[:, :], ALU.mult),
                v.tensor_tensor(qp_sb[:, 64:128], chsl(qch, nb, h * 2 + 1), ep_t[:, :], ALU.mult)][-1])
            e_kb = sch.ev("v", [e_alc, L[("ldk", n)], L.get(("kbused", m - 1))], lambda v, nb=nb, h=h: [
                v.tensor_tensor(kb_sb[:, 0:64], chsl(kch, nb, h * 2), eba_t[:, :], ALU.mult),
                v.tensor_tensor(kb_sb[:, 64:128], chsl(kch, nb, h * 2 + 1), eba_t[:, :], ALU.mult)][-1])
            L[("alused", m)] = e_kb
            e_p2 = sch.ev("t", [e_kp, L.get(("uused", m - 1)), L.get(("sdone", m - 3), e_S0b)],
                          lambda t_, sm0=sm0, sm1=sm1: [
                t_.matmul(psPU[:, :], kp_sb[:, 0:64], sm0, start=True, stop=False),
                t_.matmul(psPU[:, :], kp_sb[:, 64:128], sm1, start=False, stop=True)][-1])
            L[("p2", m)] = e_p2
            e_b = sch.ev("v", [e_p2, L[("ldv", n)]], lambda v, nb=nb, h=h, bc=bc: v.scalar_tensor_tensor(
                b_sb[0:64, :], vch[0:64, nb * OW + h * 256:nb * OW + h * 256 + 256], bc,
                psPU[:, :], ALU.mult, ALU.subtract))
            L[("bdone", m)] = e_b
            e_u = sch.ev("t", [L[("sq", m)], e_b], lambda t_: t_.matmul(
                psPU[:, :], g_sb[:, :], b_sb[0:64, :], start=True, stop=True))
            L[("u", m)] = e_u
            e_ue = sch.ev("s", [e_u], lambda s_: s_.activation(out=u_sb[0:64, :], in_=psPU[:, :], func=AF.Copy))
            L[("uused", m)] = e_ue
            def f_o(t_, mv, ps, nb=nb, h=h):
                t_.matmul(ps[:, 0:64],
                          S_b[:, (h * 2 + 0) * 256 + mv * 128:(h * 2 + 0) * 256 + mv * 128 + 128],
                          qp_sb[:, 0:64], start=True, stop=False)
                t_.matmul(ps[:, 0:64],
                          S_b[:, (h * 2 + 1) * 256 + mv * 128:(h * 2 + 1) * 256 + mv * 128 + 128],
                          qp_sb[:, 64:128], start=False, stop=False)
                return t_.matmul(ps[:, 0:64], u_sb[0:64, mv * 128:mv * 128 + 128], gt_sb[:, :],
                                 start=False, stop=True)
            e_o0 = sch.ev("t", [e_qp, e_ue, e_gt, L.get(("oev", m - 1)), L.get(("s1", m - 1)),
                                L.get(("sdone", m - 3), e_S0b)],
                          lambda t_, f=f_o: f(t_, 0, psO0))
            e_o1 = sch.ev("t", [e_qp, e_ue, e_gt, e_alc, L.get(("oev", m - 1)), L.get(("s2", m - 1)),
                                L.get(("sdone", m - 3), e_S0b)],
                          lambda t_, f=f_o: f(t_, 1, psO1))
            L[("oq", m)] = e_o1
            L[("gtused", m)] = e_o1

            def f_oev(s_, h=h, n=n):
                s_.activation(out=obuf[:, (h * 2) * TF + n * 64:(h * 2) * TF + n * 64 + 64],
                              in_=psO0[:, 0:64], func=AF.Copy, scale=1.0 / 16)
                return s_.activation(out=obuf[:, (h * 2 + 1) * TF + n * 64:(h * 2 + 1) * TF + n * 64 + 64],
                                     in_=psO1[:, 0:64], func=AF.Copy, scale=1.0 / 16)
            e_oev = sch.ev("s", [e_o0, e_o1], f_oev)
            L[("oev", m)] = e_oev
            e_kbT = sch.ev("t", [e_kb, L.get(("kbtev", m - 1))], lambda t_: [
                t_.transpose(psT2[:, 0:128], kb_sb[:, 0:64], c["ident"][:, :]),
                t_.transpose(psT2[:, 128:256], kb_sb[:, 64:128], c["ident"][:, :])][-1])
            L[("kbused", m)] = e_kbT
            e_kbt = sch.ev("s", [e_kbT], lambda s_: [
                s_.activation(out=kbt_sb[0:64, 0:128], in_=psT2[:, 0:128], func=AF.Copy),
                s_.activation(out=kbt_sb[0:64, 128:256], in_=psT2[:, 128:256], func=AF.Copy)][-1])
            L[("kbtev", m)] = e_kbt
            e_sc0 = sch.ev("t", [e_kbt, e_ue, e_oev], lambda t_: t_.matmul(
                psO0[:, :], kbt_sb[0:64, 0:128], u_sb[0:64, :], start=True, stop=True))
            e_sc1 = sch.ev("t", [e_kbt, e_ue, e_oev, e_sc0], lambda t_: t_.matmul(
                psO1[:, :], kbt_sb[0:64, 128:256], u_sb[0:64, :], start=True, stop=True))
            e_s1 = sch.ev("v", [e_sc0, e_alc], lambda v, sf0=sf0: v.scalar_tensor_tensor(
                sf0, sf0, ala_c[:, 0:1], psO0[:, :], ALU.mult, ALU.add))
            e_s2 = sch.ev("v", [e_sc1, e_s1], lambda v, sf1=sf1: v.scalar_tensor_tensor(
                sf1, sf1, ala_c[:, 0:1], psO1[:, :], ALU.mult, ALU.add))
            L[("s1", m)] = e_s1
            L[("s2", m)] = e_s2
            e_sb1 = sch.ev("v", [e_s1], lambda v, sm0=sm0, sf0=sf0: v.tensor_copy(sm0, sf0))
            e_sb2 = sch.ev("v", [e_s2], lambda v, sm1=sm1, sf1=sf1: v.tensor_copy(sm1, sf1))
            L[("sdone", m)] = e_sb2

    with nc.Block(f"dn3_{p.uid()}") as blk:
        @blk.tensor
        def _(t_):
            sch.body("t", t_)

        @blk.scalar
        def _(s_):
            sch.body("s", s_)

        @blk.vector
        def _(v):
            sch.body("v", v)

        @blk.gpsimd
        def _(g):
            sch.body("g", g)
    sch.finalize()
    if "dbg1" in io:
        sgD = p.sem("sg"); s0d = p.cnt["sg"]
        with nc.Block(f"dn3dbg_{p.uid()}") as blk:
            @blk.gpsimd
            def _(sy):
                sy.dma_start(out=io["dbg1"][:, 0:64], in_=e_sb[:, :]).then_inc(sgD, 16)
                sy.dma_start(out=io["dbg1"][:, 64:128], in_=e2_sb[:, :]).then_inc(sgD, 16)
                sy.dma_start(out=io["dbg1"][:, 128:144], in_=tcol[:, :]).then_inc(sgD, 16)
                sy.dma_start(out=io["dbg1"][:, 144:208], in_=g_sb[:, :]).then_inc(sgD, 16)
                sy.dma_start(out=io["dbg1"][:, 208:464], in_=u_sb[0:64, :]).then_inc(sgD, 16)
                sy.dma_start(out=io["dbg1"][:, 464:720], in_=b_sb[0:64, :]).then_inc(sgD, 16)
                sy.dma_start(out=io["dbg1"][:, 720:784], in_=gt_sb[:, :]).then_inc(sgD, 16)
                sy.dma_start(out=io["dbg1"][:, 784:848], in_=q1_sb[:, :]).then_inc(sgD, 16)
                sy.dma_start(out=io["dbg1"][:, 848:912], in_=n_sb[:, :]).then_inc(sgD, 16)
                sy.dma_start(out=io["dbg3"][:, 0:64], in_=ep_t[:, :]).then_inc(sgD, 16)
                sy.dma_start(out=io["dbg3"][:, 64:128], in_=bep_t[:, :]).then_inc(sgD, 16)
                sy.dma_start(out=io["dbg3"][:, 128:192], in_=eba_t[:, :]).then_inc(sgD, 16)
                sy.dma_start(out=io["dbg3"][:, 192:193], in_=ala_c[:, :], allow_slow_non_contiguous=True).then_inc(sgD, 16)
                sy.dma_start(out=io["dbg3"][:, 193:321], in_=kp_sb[:, :]).then_inc(sgD, 16)
                sy.dma_start(out=io["dbg3"][:, 321:449], in_=qp_sb[:, :]).then_inc(sgD, 16)
                sy.dma_start(out=io["dbg4"][:, 0:1536], in_=S_b[:, :]).then_inc(sgD, 16)
                sy.dma_start(out=io["dbg4"][:, 1536:3072], in_=S_f[:, :]).then_inc(sgD, 16)
                sy.wait_ge(sgD, s0d + 16 * 17)
        p.cnt["sg"] = s0d + 16 * 17
    l3p.close()
    if "odbg" in io:
        st2 = p.sem("st")
        s0d = p.cnt["st"]
        with nc.Block(f"odbg_{p.uid()}") as blk:
            @blk.sync
            def _(sy):
                sy.dma_start(out=io["odbg"].rearrange("o p t -> p o t"),
                             in_=obuf[:].rearrange("p (o t) -> p o t", o=NPC)).then_inc(st2, 16)
                sy.wait_ge(st2, s0d + 16)
        p.cnt["st"] = s0d + 16
    if 4 not in phases:
        l3.close()
        ls.close()
        return
    l4 = contextlib.ExitStack()
    sqo = p.sbuf("dn_sqo", [128, 2 * 512], BF16, l4)
    grow = p.sbuf("dn_grow", [128, TF], BF16, l4)
    rbt2 = p.sbuf("dn_rbt2", [128, 2 * 512], BF16, l4)
    ot_sb = p.sbuf("dn_ot", [128, TF], BF16, l4)
    t4 = p.sbuf("dn_t4", [128, 512], BF16, l4)
    rno = p.sbuf("dn_rno", [1, TF], F32, l4)
    rns = p.sbuf("dn_rns", [1, 512], F32, l4)
    ps_s4 = p.psum("dn_ps4", [1, 512], scope=l4)
    ps_r4 = p.psum("dn_pr4", [128, 512], scope=l4)
    sg = p.sem("sg")
    s0 = {k: p.cnt[k] for k in ("xl", "mm", "ax", "vx", "sg")}
    A_sq4 = {}; M_ss4 = {}; A_rn4 = {}; M_rb4 = {}; A_rb4 = {}; V_m4 = {}
    ax_c = s0["ax"]; mm_c = s0["mm"]; vx_c = s0["vx"]
    for h in range(3):
        for tg in range(NTG):
            ax_c += 1; A_sq4[(h, tg)] = ax_c
            mm_c += 1; M_ss4[(h, tg)] = mm_c
            ax_c += 1; A_rn4[(h, tg)] = ax_c
        for oc2 in range(2):
            for tg in range(NTG):
                mm_c += 1; M_rb4[(h, oc2, tg)] = mm_c
                ax_c += 1; A_rb4[(h, oc2, tg)] = ax_c
                vx_c += 1; V_m4[(h, oc2, tg)] = vx_c
    kof = lambda h, oc2, tg: (h * 2 + oc2) * TF + tg * 512

    with nc.Block(f"dn4_{p.uid()}") as blk:
        @blk.gpsimd
        def _(g):
            for h in range(3):
                for oc2 in range(2):
                    oc = h * 2 + oc2
                    if oc >= 1:
                        g.wait_ge(vx, V_m4[(h, oc2 - 1, NTG - 1)] if oc2 else V_m4[(h - 1, 1, NTG - 1)])
                    g.dma_start(out=grow[:, :], in_=io["gsp"][oc, :, :]).then_inc(xl, 16)
                    g.wait_ge(vx, V_m4[(h, oc2, NTG - 1)])
                    g.dma_start(out=io["ot"][oc, :, :], in_=ot_sb[:, :]).then_inc(sg, 16)
            g.wait_ge(sg, s0["sg"] + 16 * 6)

        @blk.scalar
        def _(s_):
            for h in range(3):
                for tg in range(NTG):
                    if (h, tg) != (0, 0):
                        s_.wait_ge(mm, M_ss4[(h, tg - 1)] if tg else M_ss4[(h - 1, NTG - 1)])
                    s_.activation(out=sqo[:, 0:512], in_=obuf[:, kof(h, 0, tg):kof(h, 0, tg) + 512],
                                  func=AF.Square)
                    s_.activation(out=sqo[:, 512:1024], in_=obuf[:, kof(h, 1, tg):kof(h, 1, tg) + 512],
                                  func=AF.Square).then_inc(ax, 1)
                    s_.wait_ge(mm, M_ss4[(h, tg)])
                    sl = slice(tg * 512, tg * 512 + 512)
                    s_.activation(out=rns[0:1, :], in_=ps_s4[0:1, :], func=AF.Ln,
                                  scale=1.0 / 256, bias=1e-5)
                    s_.activation(out=rno[0:1, sl], in_=rns[0:1, :], func=AF.Exp,
                                  scale=-0.5).then_inc(ax, 1)
                for oc2 in range(2):
                    for tg in range(NTG):
                        s_.wait_ge(mm, M_rb4[(h, oc2, tg)])
                        k2 = (oc2, tg)
                        if k2 > (0, 1):
                            prev2 = (oc2, tg - 2) if tg >= 2 else (oc2 - 1, NTG - 2 + tg)
                            s_.wait_ge(vx, V_m4[(h,) + prev2])
                        elif h > 0 and k2 <= (0, 1):
                            s_.wait_ge(vx, V_m4[(h - 1, 1, NTG - 2 + tg)])
                        s_.activation(out=rbt2[:, (tg % 2) * 512:(tg % 2) * 512 + 512],
                                      in_=ps_r4[:, :], func=AF.Copy).then_inc(ax, 1)

        @blk.tensor
        def _(t_):
            for h in range(3):
                for tg in range(NTG):
                    t_.wait_ge(ax, A_sq4[(h, tg)])
                    if (h, tg) != (0, 0):
                        t_.wait_ge(ax, A_rn4[(h, tg - 1)] if tg else A_rn4[(h - 1, NTG - 1)])
                    t_.matmul(ps_s4[0:1, :], c["onesc"][:, :], sqo[:, 0:512], start=True, stop=False)
                    inst = t_.matmul(ps_s4[0:1, :], c["onesc"][:, :], sqo[:, 512:1024],
                                     start=False, stop=True)
                    inst.then_inc(mm, 1)
                for oc2 in range(2):
                    for tg in range(NTG):
                        t_.wait_ge(ax, A_rn4[(h, NTG - 1)])
                        k2 = (oc2, tg)
                        if k2 != (0, 0):
                            prev = (oc2, tg - 1) if tg else (oc2 - 1, NTG - 1)
                            t_.wait_ge(ax, A_rb4[(h,) + prev])
                        elif h > 0:
                            t_.wait_ge(ax, A_rb4[(h - 1, 1, NTG - 1)])
                        sl = slice(tg * 512, tg * 512 + 512)
                        t_.matmul(ps_r4[:, :], c["ones1f"][0:1, 0:128], rno[0:1, sl],
                                  start=True, stop=True).then_inc(mm, 1)

        @blk.vector
        def _(v):
            for h in range(3):
                for oc2 in range(2):
                    oc = h * 2 + oc2
                    for tg in range(NTG):
                        v.wait_ge(ax, A_rb4[(h, oc2, tg)])
                        v.wait_ge(xl, s0["xl"] + 16 * (oc + 1))
                        if tg == 0 and oc >= 1:
                            v.wait_ge(sg, s0["sg"] + 16 * oc)
                        v.scalar_tensor_tensor(t4[:, :], obuf[:, kof(h, oc2, tg):kof(h, oc2, tg) + 512],
                                               onw_sb[:, oc2:oc2 + 1],
                                               rbt2[:, (tg % 2) * 512:(tg % 2) * 512 + 512],
                                               ALU.mult, ALU.mult)
                        v.tensor_tensor(ot_sb[:, tg * 512:tg * 512 + 512], t4[:, :],
                                        grow[:, tg * 512:tg * 512 + 512], ALU.mult).then_inc(vx, 1)

    p.cnt["xl"] = s0["xl"] + 16 * 6
    p.cnt["mm"] = mm_c
    p.cnt["ax"] = ax_c
    p.cnt["vx"] = vx_c
    p.cnt["sg"] = s0["sg"] + 16 * 6
    l4.close()
    l3.close()
    ls.close()


_PROGS = {}


def _drt(nc, nm, sh, dt, kind=None):
    t = nc.dram_tensor(nm, sh, dt, kind=kind) if kind else nc.dram_tensor(nm, sh, dt)
    return t.ap()


def build_ghost_build():
    p = Prog()
    nc = p.nc
    io = {
        "gidxT": _drt(nc, "gidxT", [256, GI], I32, "ExternalInput"),
        "uidxT": _drt(nc, "uidxT", [256, GI], I32, "ExternalInput"),
        "didxT": _drt(nc, "didxT", [688, D], I32, "ExternalInput"),
        "lut": _drt(nc, "lut", [16], F32, "ExternalInput"),
        "wg_sh": _drt(nc, "wg_sh", [256, GI], BF16, "ExternalOutput"),
        "wu_sh": _drt(nc, "wu_sh", [256, GI], BF16, "ExternalOutput"),
        "wd_sh": _drt(nc, "wd_sh", [688, D], BF16, "ExternalOutput"),
    }
    c = emit_consts(p)
    emit_ghost_build(p, io)
    p.S.close()
    return p


def build_ffn(I, wdt):
    p = Prog()
    nc = p.nc
    io = {
        "x": _drt(nc, "x", [D, TL], F32, "ExternalInput"),
        "wgT": _drt(nc, "wgT", [D, I], wdt, "ExternalInput"),
        "wuT": _drt(nc, "wuT", [D, I], wdt, "ExternalInput"),
        "wdT": _drt(nc, "wdT", [I, D], wdt, "ExternalInput"),
        "xo": _drt(nc, "xo", [D, TL], F32, "ExternalOutput"),
        "h": _drt(nc, "hspill", [I // 128, 128, TL], BF16),
    }
    c = emit_consts(p)
    emit_copy_dram(p, io["x"], io["xo"])
    xln = p.sbuf("xln", [128, NKD * TL], BF16)
    emit_ln(p, c, io["x"], xln)
    emit_ffn(p, c, xln, dict(wgT=io["wgT"], wuT=io["wuT"], wdT=io["wdT"],
                             h=io["h"], x_acc=io["xo"]), I=I, scale=0.5)
    p.S.close()
    return p


def build_lnout():
    p = Prog()
    nc = p.nc
    x = _drt(nc, "x", [D, TL], F32, "ExternalInput")
    xlno = _drt(nc, "xlno", [D, TL], BF16, "ExternalOutput")
    c = emit_consts(p)
    xln = p.sbuf("xln", [128, NKD * TL], BF16)
    emit_ln(p, c, x, xln)
    st = p.sem("st")
    s0 = p.cnt["st"]
    with nc.Block(f"out_{p.uid()}") as blk:
        @blk.sync
        def _(sy):
            sy.dma_start(out=xlno.rearrange("(k p) t -> p k t", p=128),
                         in_=xln[:].rearrange("p (k t) -> p k t", k=NKD)).then_inc(st, 16)
            sy.wait_ge(st, s0 + 16)
    p.cnt["st"] = s0 + 16
    p.S.close()
    return p


def build_retnet():
    p = Prog()
    nc = p.nc
    io = {
        "xg": _drt(nc, "xg", [2, D, TL], BF16, "ExternalInput"),
        "wqT": _drt(nc, "wqT", [D, 1024], F32, "ExternalInput"),
        "wkT": _drt(nc, "wkT", [D, 1024], F32, "ExternalInput"),
        "wvT": _drt(nc, "wvT", [D, 1024], F32, "ExternalInput"),
        "bq": _drt(nc, "bq", [1024], F32, "ExternalInput"),
        "bk": _drt(nc, "bk", [1024], F32, "ExternalInput"),
        "bv": _drt(nc, "bv", [1024], F32, "ExternalInput"),
        "dtT": _drt(nc, "dtT", [4, 256, 256], F32, "ExternalInput"),
        "cst": _drt(nc, "cst", [4, 128, 256], F32, "ExternalInput"),
        "kvt": _drt(nc, "kvt", [4, 128, 256], F32, "ExternalInput"),
        "gmc": _drt(nc, "gmc", [4], F32, "ExternalInput"),
        "o_sh": _drt(nc, "o_sh", [8, 128, TF], BF16, "ExternalOutput"),
    }
    c = emit_consts(p)
    emit_retnet(p, c, io)
    p.S.close()
    return p


def build_resid():
    p = Prog()
    nc = p.nc
    x = _drt(nc, "x", [D, TL], F32, "ExternalInput")
    osel = _drt(nc, "osel", [NKD, 128, TL], BF16, "ExternalInput")
    xo = _drt(nc, "xo", [D, TL], F32, "ExternalOutput")
    c = emit_consts(p)
    emit_copy_dram(p, x, xo)
    emit_resid_add(p, c, osel, xo, scale=1.0)
    p.S.close()
    return p


def build_deltanet():
    p = Prog()
    nc = p.nc
    io = {
        "xg": _drt(nc, "xg", [2, D, TL], BF16, "ExternalInput"),
        "wqT": _drt(nc, "wqT", [D, 768], F32, "ExternalInput"),
        "wkT": _drt(nc, "wkT", [D, 768], F32, "ExternalInput"),
        "wvT": _drt(nc, "wvT", [D, 768], F32, "ExternalInput"),
        "wgT": _drt(nc, "wgT", [D, 768], F32, "ExternalInput"),
        "wbT": _drt(nc, "wbT", [D, 3], F32, "ExternalInput"),
        "waT": _drt(nc, "waT", [D, 3], F32, "ExternalInput"),
        "convw": _drt(nc, "convw", [3, 768, 4], F32, "ExternalInput"),
        "onw": _drt(nc, "onw", [256], F32, "ExternalInput"),
        "alog": _drt(nc, "alog", [3], F32, "ExternalInput"),
        "dtb": _drt(nc, "dtb", [3], F32, "ExternalInput"),
        "trimask": _drt(nc, "trimask", [64, 192], F32, "ExternalInput"),
        "selhot": _drt(nc, "selhot", [15, 15 * 128], F32, "ExternalInput"),
        "qraw": _drt(nc, "qraw", [6, 128, TF], BF16),
        "kraw": _drt(nc, "kraw", [6, 128, TF], BF16),
        "vraw": _drt(nc, "vraw", [6, 128, TF], BF16),
        "gsp": _drt(nc, "gsp", [6, 128, TF], BF16),
        "qhat": _drt(nc, "qhat", [6, 128, TF], BF16),
        "khat": _drt(nc, "khat", [6, 128, TF], BF16),
        "vtok": _drt(nc, "vtok", [TF, 768], BF16),
        "ot": _drt(nc, "ot", [6, 128, TF], BF16, "ExternalOutput"),
    }
    c = emit_consts(p)
    emit_deltanet(p, c, io)
    p.S.close()
    return p


def build_wo_resid():
    p = Prog()
    nc = p.nc
    x = _drt(nc, "x", [D, TL], F32, "ExternalInput")
    osel = _drt(nc, "osel", [12, 128, TL], BF16, "ExternalInput")
    woT = _drt(nc, "woT", [KEY, D], F32, "ExternalInput")
    xo = _drt(nc, "xo", [D, TL], F32, "ExternalOutput")
    c = emit_consts(p)
    emit_copy_dram(p, x, xo)
    emit_down(p, c, osel, woT, xo, NKI=12, scale=1.0)
    p.S.close()
    return p


def _prog(name, fn):
    if name not in _PROGS:
        _PROGS[name] = fn()
    return _PROGS[name]


def _run(p, in_maps, **kw):
    return run_bass_kernel_spmd(p.nc, in_maps, core_ids=list(range(8)), **kw).results


def _retnet_consts():
    g = GAMMAS.astype(np.float64)
    idx = np.arange(R_C)
    ii, jj = idx[:, None], idx[None, :]
    out = {}
    for hg in range(2):
        hs = slice(hg * 4, hg * 4 + 4)
        Dm = np.where(ii >= jj, g[hs][:, None, None] ** (ii - jj), 0.0)
        out[hg] = dict(
            dtT=np.ascontiguousarray(Dm.transpose(0, 2, 1)).astype(np.float32),
            cst=np.broadcast_to((g[hs][:, None] ** (idx + 1))[:, None, :],
                                (4, 128, R_C)).astype(np.float32).copy(),
            kvt=np.broadcast_to((g[hs][:, None] ** (R_C - 1 - idx))[:, None, :],
                                (4, 128, R_C)).astype(np.float32).copy(),
            gmc=(g[hs] ** R_C).astype(np.float32),
        )
    return out


def _delta_consts():
    ii, jj = np.arange(64)[:, None], np.arange(64)[None, :]
    trimask = np.concatenate([(ii < jj), (ii <= jj), (ii > jj)], 1).astype(np.float32)
    selhot = np.zeros((15, 15 * 128), np.float32)
    for r in range(15):
        selhot[r, r * 128:(r + 1) * 128] = 1.0
    return trimask, selhot


def kernel(**inp):
    ac = np.ascontiguousarray
    x = inp["L0_attn_x"]
    xs = [ac(x[c // 2, (c % 2) * TL:(c % 2 + 1) * TL, :].T.astype(np.float32)) for c in range(8)]

    gT = ac(inp["L0_ffn_gate_idx"].T)
    uT = ac(inp["L0_ffn_up_idx"].T)
    dT = ac(inp["L0_ffn_down_idx"].T)
    p = _prog("ghost", build_ghost_build)
    r = _run(p, [dict(gidxT=ac(gT[256 * c:256 * (c + 1)]), uidxT=ac(uT[256 * c:256 * (c + 1)]),
                      didxT=ac(dT[688 * c:688 * (c + 1)]), lut=inp["L0_ffn_lut"])
                 for c in range(8)])
    WgT = np.concatenate([r[c]["wg_sh"] for c in range(8)], 0)
    WuT = np.concatenate([r[c]["wu_sh"] for c in range(8)], 0)
    WdT = np.concatenate([r[c]["wd_sh"] for c in range(8)], 0)

    def ffn_launch(xs, wgT, wuT, wdT, I, wdt):
        p = _prog(f"ffn{I}", lambda: build_ffn(I, wdt))
        r = _run(p, [dict(x=xs[c], wgT=wgT, wuT=wuT, wdT=wdT) for c in range(8)])
        return [r[c]["xo"] for c in range(8)]

    def ln_launch(xs):
        p = _prog("lnout", build_lnout)
        r = _run(p, [dict(x=xs[c]) for c in range(8)])
        return [r[c]["xlno"] for c in range(8)]

    xs = ffn_launch(xs, WgT, WuT, WdT, GI, BF16)

    xln = ln_launch(xs)
    rc = _retnet_consts()
    Wq0, Wk0, Wv0 = inp["L0_attn_Wq"], inp["L0_attn_Wk"], inp["L0_attn_Wv"]
    bq0, bk0, bv0 = inp["L0_attn_bq"], inp["L0_attn_bk"], inp["L0_attn_bv"]
    p = _prog("retnet", build_retnet)
    r = _run(p, [dict(xg=np.stack([xln[(c // 2) * 2], xln[(c // 2) * 2 + 1]]),
                      wqT=ac(Wq0[:, (c % 2) * 1024:(c % 2 + 1) * 1024]),
                      wkT=ac(Wk0[:, (c % 2) * 1024:(c % 2 + 1) * 1024]),
                      wvT=ac(Wv0[:, (c % 2) * 1024:(c % 2 + 1) * 1024]),
                      bq=ac(bq0[(c % 2) * 1024:(c % 2 + 1) * 1024]),
                      bk=ac(bk0[(c % 2) * 1024:(c % 2 + 1) * 1024]),
                      bv=ac(bv0[(c % 2) * 1024:(c % 2 + 1) * 1024]),
                      **rc[c % 2]) for c in range(8)])
    p = _prog("resid", build_resid)
    r2in = []
    for c in range(8):
        pb = (c // 2) * 2
        osel = np.stack([r[pb + dc // 8]["o_sh"][dc % 8][:, (c % 2) * TL:(c % 2 + 1) * TL]
                         for dc in range(NKD)])
        r2in.append(dict(x=xs[c], osel=osel))
    r2 = _run(p, r2in)
    xs = [r2[c]["xo"] for c in range(8)]

    xs = ffn_launch(xs, WgT, WuT, WdT, GI, BF16)

    wgu = inp["L1_ffn_w_gate_up"]
    wgT1 = ac(wgu[:LI].T)
    wuT1 = ac(wgu[LI:].T)
    wdT1 = ac(inp["L1_ffn_w_down"].T)
    xs = ffn_launch(xs, wgT1, wuT1, wdT1, LI, F32)

    xln = ln_launch(xs)
    trimask, selhot = _delta_consts()
    Wq1, Wk1, Wv1, Wg1 = (inp["L1_attn_Wq"], inp["L1_attn_Wk"], inp["L1_attn_Wv"], inp["L1_attn_Wg"])
    Wb1, Wa1 = inp["L1_attn_Wb"], inp["L1_attn_Wa"]
    cq1, ck1, cv1 = inp["L1_attn_conv_q"], inp["L1_attn_conv_k"], inp["L1_attn_conv_v"]
    p = _prog("delta", build_deltanet)
    r = _run(p, [dict(xg=np.stack([xln[(c // 2) * 2], xln[(c // 2) * 2 + 1]]),
                      wqT=ac(Wq1[(c % 2) * 768:(c % 2 + 1) * 768].T),
                      wkT=ac(Wk1[(c % 2) * 768:(c % 2 + 1) * 768].T),
                      wvT=ac(Wv1[(c % 2) * 768:(c % 2 + 1) * 768].T),
                      wgT=ac(Wg1[(c % 2) * 768:(c % 2 + 1) * 768].T),
                      wbT=ac(Wb1[(c % 2) * 3:(c % 2 + 1) * 3].T),
                      waT=ac(Wa1[(c % 2) * 3:(c % 2 + 1) * 3].T),
                      convw=np.stack([cq1[(c % 2) * 768:(c % 2 + 1) * 768],
                                      ck1[(c % 2) * 768:(c % 2 + 1) * 768],
                                      cv1[(c % 2) * 768:(c % 2 + 1) * 768]]),
                      onw=inp["L1_attn_o_norm_w"],
                      alog=ac(inp["L1_attn_A_log"][(c % 2) * 3:(c % 2 + 1) * 3]),
                      dtb=ac(inp["L1_attn_dt_bias"][(c % 2) * 3:(c % 2 + 1) * 3]),
                      trimask=trimask, selhot=selhot) for c in range(8)])
    p = _prog("wo", build_wo_resid)
    woT = ac(inp["L1_attn_Wo"].T)
    r2in = []
    for c in range(8):
        pb = (c // 2) * 2
        osel = np.stack([r[pb + oc // 6]["ot"][oc % 6][:, (c % 2) * TL:(c % 2 + 1) * TL]
                         for oc in range(12)])
        r2in.append(dict(x=xs[c], osel=osel, woT=woT))
    r2 = _run(p, r2in)
    xs = [r2[c]["xo"] for c in range(8)]

    xs = ffn_launch(xs, wgT1, wuT1, wdT1, LI, F32)

    out = np.empty((4, TF, D), np.float32)
    for c in range(8):
        out[c // 2, (c % 2) * TL:(c % 2 + 1) * TL, :] = xs[c].T
    return out


def emit_collective(p, kind, groups, in_ap, out_ap, alu=None):
    nc = p.nc
    cc = p.sem("cc")
    s0 = p.cnt["cc"]
    with nc.Block(f"cc_{p.uid()}") as blk:
        @blk.gpsimd
        def _(g):
            g.collective_compute(kind, alu or ALU.bypass, replica_groups=groups,
                                 ins=[in_ap], outs=[out_ap]).then_inc(cc, 1)
            g.wait_ge(cc, s0 + 1)
    p.cnt["cc"] = s0 + 1


def emit_xln_out(p, c, xln, dst):
    nc = p.nc
    st = p.sem("st")
    s0 = p.cnt["st"]
    with nc.Block(f"xlnout_{p.uid()}") as blk:
        @blk.sync
        def _(sy):
            sy.dma_start(out=dst.rearrange("(k p) t -> p k t", p=128),
                         in_=xln[:].rearrange("p (k t) -> p k t", k=NKD)).then_inc(st, 16)
            sy.wait_ge(st, s0 + 16)
    p.cnt["st"] = s0 + 16


def emit_halfsel_resid(p, c, o_ex, msel, x_acc, ncper, T=TL):
    nc = p.nc
    NT = T // 512
    ls = contextlib.ExitStack()
    tb = p.sbuf("hs_t", [128, 2 * 2 * 512], BF16, ls)
    fb = p.sbuf("hs_f", [128, 2 * 512], F32, ls)
    sc_t = p.sbuf("hs_s", [128, 2 * 512], BF16, ls)
    xl, vx, sg = (p.sem(k) for k in ("xl", "vx", "sg"))
    s0 = {k: p.cnt[k] for k in ("xl", "vx", "sg")}
    NTOT = NKD * NT
    with nc.Block(f"halfsel_{p.uid()}") as blk:
        @blk.gpsimd
        def _(g):
            for dc in range(NKD):
                slot, ch = divmod(dc, ncper)
                for tg in range(NT):
                    i = dc * NT + tg
                    if i >= 2:
                        g.wait_ge(vx, s0["vx"] + i - 1)
                    for hf in range(2):
                        g.dma_start(out=tb[:, ((i % 2) * 2 + hf) * 512:((i % 2) * 2 + hf) * 512 + 512],
                                    in_=o_ex[slot, ch, :, hf * TL + tg * 512:hf * TL + tg * 512 + 512]
                                    ).then_inc(xl, 16)
                    g.wait_ge(vx, s0["vx"] + i + 1)
                    g.dma_start(out=x_acc[dc * 128:dc * 128 + 128, tg * 512:tg * 512 + 512],
                                in_=fb[:, (i % 2) * 512:(i % 2) * 512 + 512],
                                accum_op=ALU.add).then_inc(sg, 16)
            g.wait_ge(sg, s0["sg"] + 16 * NTOT)

        @blk.vector
        def _(v):
            for dc in range(NKD):
                for tg in range(NT):
                    i = dc * NT + tg
                    v.wait_ge(xl, s0["xl"] + 32 * (i + 1))
                    if i >= 2:
                        v.wait_ge(sg, s0["sg"] + 16 * (i - 1))
                    b0 = ((i % 2) * 2) * 512
                    b1 = ((i % 2) * 2 + 1) * 512
                    v.tensor_scalar(sc_t[:, (i % 2) * 512:(i % 2) * 512 + 512],
                                    tb[:, b0:b0 + 512], msel[:, 0:1],
                                    None, ALU.mult, ALU.bypass)
                    v.scalar_tensor_tensor(fb[:, (i % 2) * 512:(i % 2) * 512 + 512],
                                           tb[:, b1:b1 + 512], msel[:, 1:2],
                                           sc_t[:, (i % 2) * 512:(i % 2) * 512 + 512],
                                           ALU.mult, ALU.add).then_inc(vx, 1)
    p.cnt["xl"] = s0["xl"] + 32 * NTOT
    p.cnt["vx"] = s0["vx"] + NTOT
    p.cnt["sg"] = s0["sg"] + 16 * NTOT
    ls.close()


def build_fused():
    p = Prog()
    nc = p.nc
    io = {}
    io["x"] = _drt(nc, "x", [D, TL], F32, "ExternalInput")
    io["gidxT"] = _drt(nc, "gidxT", [256, GI], I32, "ExternalInput")
    io["uidxT"] = _drt(nc, "uidxT", [256, GI], I32, "ExternalInput")
    io["didxT"] = _drt(nc, "didxT", [688, D], I32, "ExternalInput")
    io["lut"] = _drt(nc, "lut", [16], F32, "ExternalInput")
    for nm, sh in (("r_wqT", [D, 1024]), ("r_wkT", [D, 1024]), ("r_wvT", [D, 1024]),
                   ("r_bq", [1024]), ("r_bk", [1024]), ("r_bv", [1024]),
                   ("r_dtT", [4, 256, 256]), ("r_cst", [4, 128, 256]),
                   ("r_kvt", [4, 128, 256]), ("r_gmc", [4]),
                   ("d_wqT", [D, 768]), ("d_wkT", [D, 768]), ("d_wvT", [D, 768]),
                   ("d_wgT", [D, 768]), ("d_wbT", [D, 3]), ("d_waT", [D, 3]),
                   ("d_convw", [3, 768, 4]), ("d_onw", [256]), ("d_alog", [3]),
                   ("d_dtb", [3]), ("d_trimask", [64, 192]), ("d_selhot", [15, 15 * 128]),
                   ("f_wgT", [D, LI]), ("f_wuT", [D, LI]), ("f_wdT", [LI, D]),
                   ("woT", [KEY, D]), ("msel", [2])):
        io[nm] = _drt(nc, nm, sh, F32, "ExternalInput")
    io["xo"] = _drt(nc, "xo", [D, TL], F32, "ExternalOutput")
    wg_sh = _drt(nc, "wg_sh", [256, GI], BF16)
    wu_sh = _drt(nc, "wu_sh", [256, GI], BF16)
    wd_sh = _drt(nc, "wd_sh", [688, D], BF16)
    WgT = nc.dram_tensor("WgT", [D, GI], BF16, addr_space="Shared").ap()
    WuT = nc.dram_tensor("WuT", [D, GI], BF16, addr_space="Shared").ap()
    WdT = nc.dram_tensor("WdT", [GI, D], BF16, addr_space="Shared").ap()
    h0 = _drt(nc, "h0", [GI // 128, 128, TL], BF16)
    h1 = _drt(nc, "h1", [LI // 128, 128, TL], BF16)
    agin = _drt(nc, "agin", [D, TL], BF16)
    xg0 = _drt(nc, "xg0", [2, D, TL], BF16)
    o_sh = _drt(nc, "o_sh", [8, 128, TF], BF16)
    o_ex = _drt(nc, "o_ex", [2, 8, 128, TF], BF16)
    xg1 = _drt(nc, "xg1", [2, D, TL], BF16)
    ot = _drt(nc, "ot", [6, 128, TF], BF16)
    ot_ex = _drt(nc, "ot_ex", [2, 6, 128, TF], BF16)
    dnsp = {nm: _drt(nc, nm, [6, 128, TF], BF16)
            for nm in ("qraw", "kraw", "vraw", "gsp", "qhat", "khat")}
    vtok = _drt(nc, "vtok", [TF, 768], BF16)

    PAIRS = [[0, 1], [2, 3], [4, 5], [6, 7]]
    ALL8 = [list(range(8))]

    c = emit_consts(p)
    msel_sb = p.sbuf("mselsb", [128, 2], F32)
    xl = p.sem("xl")
    s0 = p.cnt["xl"]
    with nc.Block(f"mselld_{p.uid()}") as blk:
        @blk.gpsimd
        def _(g):
            g.dma_start(out=msel_sb[:, :],
                        in_=bass.AP(io["msel"].tensor, 0, [[0, 128], [1, 2]])).then_inc(xl, 16)
            g.wait_ge(xl, s0 + 16)
    p.cnt["xl"] = s0 + 16

    emit_copy_dram(p, io["x"], io["xo"])
    emit_ghost_build(p, dict(gidxT=io["gidxT"], uidxT=io["uidxT"], didxT=io["didxT"],
                             lut=io["lut"], wg_sh=wg_sh, wu_sh=wu_sh, wd_sh=wd_sh))
    emit_collective(p, "AllGather", ALL8, wg_sh, WgT)
    emit_collective(p, "AllGather", ALL8, wu_sh, WuT)
    emit_collective(p, "AllGather", ALL8, wd_sh, WdT)

    def ffn_phase(wgT, wuT, wdT, I, h):
        ls = contextlib.ExitStack()
        xln = p.sbuf("xln", [128, NKD * TL], BF16, ls)
        emit_ln(p, c, io["xo"], xln)
        emit_ffn(p, c, xln, dict(wgT=wgT, wuT=wuT, wdT=wdT, h=h, x_acc=io["xo"]),
                 I=I, scale=0.5)
        ls.close()

    def ln_ag_phase(xg):
        ls = contextlib.ExitStack()
        xln = p.sbuf("xln", [128, NKD * TL], BF16, ls)
        emit_ln(p, c, io["xo"], xln)
        emit_xln_out(p, c, xln, agin)
        ls.close()
        emit_collective(p, "AllGather", PAIRS, agin,
                        xg.rearrange("a d t -> (a d) t"))

    ffn_phase(WgT, WuT, WdT, GI, h0)
    ln_ag_phase(xg0)
    emit_retnet(p, c, dict(xg=xg0, wqT=io["r_wqT"], wkT=io["r_wkT"], wvT=io["r_wvT"],
                           bq=io["r_bq"], bk=io["r_bk"], bv=io["r_bv"], dtT=io["r_dtT"],
                           cst=io["r_cst"], kvt=io["r_kvt"], gmc=io["r_gmc"], o_sh=o_sh))
    emit_collective(p, "AllGather", PAIRS, o_sh.rearrange("a p t -> (a p) t"),
                    o_ex.rearrange("s a p t -> (s a p) t"))
    emit_halfsel_resid(p, c, o_ex, msel_sb, io["xo"], ncper=8)
    ffn_phase(WgT, WuT, WdT, GI, h0)
    ffn_phase(io["f_wgT"], io["f_wuT"], io["f_wdT"], LI, h1)
    ln_ag_phase(xg1)
    emit_deltanet(p, c, dict(xg=xg1, wqT=io["d_wqT"], wkT=io["d_wkT"], wvT=io["d_wvT"],
                             wgT=io["d_wgT"], wbT=io["d_wbT"], waT=io["d_waT"],
                             convw=io["d_convw"], onw=io["d_onw"], alog=io["d_alog"],
                             dtb=io["d_dtb"], trimask=io["d_trimask"], selhot=io["d_selhot"],
                             vtok=vtok, ot=ot, **dnsp))
    emit_collective(p, "AllGather", PAIRS, ot.rearrange("a p t -> (a p) t"),
                    ot_ex.rearrange("s a p t -> (s a p) t"))
    emit_down(p, c, None, io["woT"], io["xo"], NKI=12, scale=1.0,
              sel=(ot_ex, msel_sb, 6))
    ffn_phase(io["f_wgT"], io["f_wuT"], io["f_wdT"], LI, h1)
    p.S.close()
    return p


FUSED = True


def kernel_fused(**inp):
    ac = np.ascontiguousarray
    x = inp["L0_attn_x"]
    rc = _retnet_consts()
    trimask, selhot = _delta_consts()
    gT = ac(inp["L0_ffn_gate_idx"].T)
    uT = ac(inp["L0_ffn_up_idx"].T)
    dT = ac(inp["L0_ffn_down_idx"].T)
    wgu = inp["L1_ffn_w_gate_up"]
    wgT1, wuT1 = ac(wgu[:LI].T), ac(wgu[LI:].T)
    wdT1 = ac(inp["L1_ffn_w_down"].T)
    woT = ac(inp["L1_attn_Wo"].T)
    Wq1, Wk1, Wv1, Wg1 = (inp["L1_attn_Wq"], inp["L1_attn_Wk"], inp["L1_attn_Wv"], inp["L1_attn_Wg"])
    Wb1, Wa1 = inp["L1_attn_Wb"], inp["L1_attn_Wa"]
    cq1, ck1, cv1 = inp["L1_attn_conv_q"], inp["L1_attn_conv_k"], inp["L1_attn_conv_v"]
    in_maps = []
    for cid in range(8):
        hg = cid % 2
        hr = slice(hg * 1024, (hg + 1) * 1024)
        dr_ = slice(hg * 768, (hg + 1) * 768)
        hs3 = slice(hg * 3, (hg + 1) * 3)
        in_maps.append(dict(
            x=ac(x[cid // 2, (cid % 2) * TL:(cid % 2 + 1) * TL, :].T.astype(np.float32)),
            gidxT=ac(gT[256 * cid:256 * (cid + 1)]), uidxT=ac(uT[256 * cid:256 * (cid + 1)]),
            didxT=ac(dT[688 * cid:688 * (cid + 1)]), lut=inp["L0_ffn_lut"],
            r_wqT=ac(inp["L0_attn_Wq"][:, hr]), r_wkT=ac(inp["L0_attn_Wk"][:, hr]),
            r_wvT=ac(inp["L0_attn_Wv"][:, hr]),
            r_bq=ac(inp["L0_attn_bq"][hr]), r_bk=ac(inp["L0_attn_bk"][hr]),
            r_bv=ac(inp["L0_attn_bv"][hr]),
            r_dtT=rc[hg]["dtT"], r_cst=rc[hg]["cst"], r_kvt=rc[hg]["kvt"], r_gmc=rc[hg]["gmc"],
            d_wqT=ac(Wq1[dr_].T), d_wkT=ac(Wk1[dr_].T), d_wvT=ac(Wv1[dr_].T),
            d_wgT=ac(Wg1[dr_].T), d_wbT=ac(Wb1[hs3].T), d_waT=ac(Wa1[hs3].T),
            d_convw=np.stack([cq1[dr_], ck1[dr_], cv1[dr_]]),
            d_onw=inp["L1_attn_o_norm_w"], d_alog=ac(inp["L1_attn_A_log"][hs3]),
            d_dtb=ac(inp["L1_attn_dt_bias"][hs3]),
            d_trimask=trimask, d_selhot=selhot,
            f_wgT=wgT1, f_wuT=wuT1, f_wdT=wdT1, woT=woT,
            msel=np.array([1.0 - (cid % 2), float(cid % 2)], np.float32),
        ))
    p = _prog("fused", build_fused)
    r = _run(p, in_maps)
    out = np.empty((4, TF, D), np.float32)
    for cid in range(8):
        out[cid // 2, (cid % 2) * TL:(cid % 2 + 1) * TL, :] = r[cid]["xo"].T
    return out


_kernel_unfused = kernel


def kernel(**inp):
    if FUSED:
        return kernel_fused(**inp)
    return _kernel_unfused(**inp)


# version 09
# speedup vs baseline: 1.1314x; 1.1314x over previous
import contextlib
import numpy as np
import ml_dtypes
import concourse.bass as bass
import concourse.mybir as mybir
from concourse.bass_utils import run_bass_kernel_spmd

F32 = mybir.dt.float32
BF16 = mybir.dt.bfloat16
I32 = mybir.dt.int32
AF = mybir.ActivationFunctionType
ALU = mybir.AluOpType
BF = ml_dtypes.bfloat16

D = 2048
TL = 2048
TF = 4096
NKD = D // 128

GI = 5504
NKG = GI // 128
R_HEADS, R_DH, R_C = 8, 256, 256
L_HEADS, L_DH, L_C = 6, 256, 64
KEY = L_HEADS * L_DH
LI = 8192
NKL = LI // 128

GAMMAS = (1.0 - np.exp2(np.linspace(np.log2(1 / 32), np.log2(1 / 512), R_HEADS)))
DN3_LIMIT = None
DN3_SERIAL = False
DN3_EVLIM = None


class Prog:
    def __init__(self, detect_races=True):
        self.nc = bass.Bass("TRN2", target_bir_lowering=False,
                            detect_race_conditions=detect_races)
        self.S = contextlib.ExitStack()
        self._sems = {}
        self.cnt = {}
        self._uid = 0

    def uid(self):
        self._uid += 1
        return self._uid

    def sem(self, name):
        if name not in self._sems:
            self._sems[name] = self.S.enter_context(self.nc.semaphore(name))
            self.cnt[name] = 0
        return self._sems[name]

    def sbuf(self, name, shape, dt, scope=None):
        return (scope or self.S).enter_context(
            self.nc.sbuf_tensor(f"{name}_{self.uid()}", shape, dt))

    def psum(self, name, shape, dt=F32, scope=None):
        return (scope or self.S).enter_context(
            self.nc.psum_tensor(f"{name}_{self.uid()}", shape, dt))

    def dram(self, name, shape, dt, **kw):
        return self.nc.dram_tensor(name, shape, dt, **kw)


CONST_VALS = [1e-5, 1e-6, -0.5, 0.5, -1.0, 1.0, 2.0, 1.0 / 16, 1.0 / 256]


def emit_consts(p):
    nc = p.nc
    c = {}
    c["ones1"] = p.sbuf("c_ones1", [1, 512], BF16)
    c["ones1f"] = p.sbuf("c_ones1f", [1, 128], F32)
    c["onesc"] = p.sbuf("c_onesc", [128, 1], BF16)
    c["ones65"] = p.sbuf("c_ones65", [65, 128], BF16)
    c["ident"] = p.sbuf("c_ident", [128, 128], BF16)
    c["cvals"] = p.sbuf("c_cvals", [128, len(CONST_VALS)], F32)
    io2 = p.sbuf("c_io2", [128, 256], BF16)
    for i, v in enumerate(CONST_VALS):
        nc.const_aps.register(F32, v, c["cvals"][:, i:i + 1])
    gx, vx = p.sem("gx"), p.sem("vx")
    with nc.Block(f"consts_{p.uid()}") as blk:
        @blk.gpsimd
        def _(g):
            g.iota(io2[:, 0:128], [[1, 128]], base=0, channel_multiplier=0,
                   allow_small_or_imprecise_dtypes=True)
            g.iota(io2[:, 128:256], [[0, 128]], base=0, channel_multiplier=1,
                   allow_small_or_imprecise_dtypes=True).then_inc(gx, 1)

        @blk.vector
        def _(v):
            v.memset(c["ones1"][:], 1.0)
            v.memset(c["ones1f"][:], 1.0)
            v.memset(c["onesc"][:], 1.0)
            v.memset(c["ones65"][:], 1.0)
            for i, val in enumerate(CONST_VALS):
                v.memset(c["cvals"][:, i:i + 1], val)
            v.wait_ge(gx, p.cnt["gx"] + 1)
            v.tensor_tensor(c["ident"][:], io2[:, 0:128], io2[:, 128:256],
                            ALU.is_equal).then_inc(vx, 1)

        @blk.scalar
        def _(s):
            s.wait_ge(vx, p.cnt["vx"] + 1)
    p.cnt["gx"] += 1
    p.cnt["vx"] += 1
    return c


def emit_ghost_build(p, io):
    nc = p.nc
    ls = contextlib.ExitStack()
    xl, vx, st = p.sem("xl"), p.sem("vx"), p.sem("st")
    lutcol = p.sbuf("gb_lutcol", [128, 16], F32, ls)
    idxs = p.sbuf("gb_idx", [128, 2 * 5504], I32, ls)
    idxb = p.sbuf("gb_idxb", [128, 5504], BF16, ls)
    accs = p.sbuf("gb_acc", [128, 4 * 5504], BF16, ls)
    scr = p.sbuf("gb_scr", [128, 5504], BF16, ls)
    outt = p.sbuf("gb_out", [128, 2 * 5504], BF16, ls)

    tiles = []
    for r in range(2):
        tiles.append((io["gidxT"][r * 128:(r + 1) * 128, :], io["wg_sh"][r * 128:(r + 1) * 128, :], 5504))
    for r in range(2):
        tiles.append((io["uidxT"][r * 128:(r + 1) * 128, :], io["wu_sh"][r * 128:(r + 1) * 128, :], 5504))
    for r in range(6):
        r0, r1 = r * 128, min(r * 128 + 128, 688)
        tiles.append((io["didxT"][r0:r1, :], io["wd_sh"][r0:r1, :], 2048))
    nt = len(tiles)
    xl0, vx0, st0 = p.cnt["xl"], p.cnt["vx"], p.cnt["st"]

    with nc.Block(f"ghostbuild_{p.uid()}") as blk:
        @blk.gpsimd
        def _(g):
            g.dma_start(out=lutcol[:], in_=bass.AP(io["lut"].tensor, 0, [[0, 128], [1, 16]])).then_inc(xl, 16)
            for i, (src, dst, w) in enumerate(tiles):
                if i >= 2:
                    g.wait_ge(vx, vx0 + i - 1)
                rows = src.shape[0]
                g.dma_start(out=idxs[0:rows, (i % 2) * 5504:(i % 2) * 5504 + w],
                            in_=src).then_inc(xl, 16)

        @blk.vector
        def _(v):
            for i, (src, dst, w) in enumerate(tiles):
                rows = src.shape[0]
                b = (i % 2) * 5504
                v.wait_ge(xl, xl0 + 16 * (i + 2))
                ib = idxb[0:rows, 0:w]
                v.tensor_copy(ib, idxs[0:rows, b:b + w])
                for j in range(16):
                    a = accs[0:rows, (j % 4) * 5504:(j % 4) * 5504 + w]
                    if j < 4:
                        v.tensor_scalar(a, ib, float(j), lutcol[0:rows, j:j + 1], ALU.is_equal, ALU.mult)
                    else:
                        sc = scr[0:rows, 0:w]
                        v.tensor_scalar(sc, ib, float(j), lutcol[0:rows, j:j + 1], ALU.is_equal, ALU.mult)
                        v.tensor_tensor(a, a, sc, ALU.add)
                A = lambda k: accs[0:rows, k * 5504:k * 5504 + w]
                if i >= 2:
                    v.wait_ge(st, st0 + 16 * (i - 1))
                o = outt[0:rows, b:b + w]
                v.tensor_tensor(A(0), A(0), A(1), ALU.add)
                v.tensor_tensor(A(2), A(2), A(3), ALU.add)
                v.tensor_tensor(o, A(0), A(2), ALU.add).then_inc(vx, 1)

        @blk.sync
        def _(sy):
            for i, (src, dst, w) in enumerate(tiles):
                rows = src.shape[0]
                sy.wait_ge(vx, vx0 + i + 1)
                sy.dma_start(out=dst, in_=outt[0:rows, (i % 2) * 5504:(i % 2) * 5504 + w]).then_inc(st, 16)
            sy.wait_ge(st, st0 + 16 * nt)
    p.cnt["xl"] = xl0 + 16 * (nt + 1)
    p.cnt["vx"] = vx0 + nt
    p.cnt["st"] = st0 + 16 * nt
    ls.close()


def emit_ln(p, c, x_dram, xln, T=TL):
    nc = p.nc
    NT = T // 512
    ls = contextlib.ExitStack()
    xcb = p.sbuf("ln_xc", [128, NKD * T], BF16, ls)
    xsq = p.sbuf("ln_xsq", [128, 4 * 512], BF16, ls)
    r_m = p.sbuf("ln_rm", [1, T], F32, ls)
    r_q = p.sbuf("ln_rq", [1, T], F32, ls)
    r_var = p.sbuf("ln_rvar", [1, T], F32, ls)
    r_tmp = p.sbuf("ln_rtmp", [1, T], F32, ls)
    r_rstd = p.sbuf("ln_rrstd", [1, T], F32, ls)
    mrb = p.sbuf("ln_mrb", [128, 2 * 512], F32, ls)
    scr = p.sbuf("ln_scr", [128, 512], F32, ls)
    ps_s = p.psum("ln_ps_s", [1, 512], scope=ls)
    ps_q = p.psum("ln_ps_q", [1, 512], scope=ls)
    ps_m = p.psum("ln_ps_m", [128, 512], scope=ls)
    ps_r = p.psum("ln_ps_r", [128, 512], scope=ls)
    xl, mm, mmf, ax, vx = p.sem("xl"), p.sem("mm"), p.sem("mmf"), p.sem("ax"), p.sem("vx")
    s0 = {k: p.cnt[k] for k in ("xl", "mm", "mmf", "ax", "vx")}
    AXTG = NKD + 2

    with nc.Block(f"ln_{p.uid()}") as blk:
        @blk.gpsimd
        def _(g):
            g.dma_start(out=xcb[:].rearrange("p (k t) -> p k t", k=NKD),
                        in_=x_dram.rearrange("(k p) t -> p k t", p=128)).then_inc(xl, 16)

        @blk.scalar
        def _(s):
            s.wait_ge(xl, s0["xl"] + 16)
            for tg in range(NT):
                for k in range(NKD):
                    gsq = tg * NKD + k
                    if gsq >= 4:
                        tq, kq = divmod(gsq - 4, NKD)
                        if kq < NKD - 1:
                            s.wait_ge(mmf, s0["mmf"] + tq * (NKD - 1) + kq + 1)
                        else:
                            s.wait_ge(mm, s0["mm"] + 2 * tq + 1)
                    s.activation(out=xsq[:, (gsq % 4) * 512:(gsq % 4) * 512 + 512],
                                 in_=xcb[:, k * T + tg * 512:k * T + tg * 512 + 512],
                                 func=AF.Square).then_inc(ax, 1)
                s.wait_ge(vx, s0["vx"] + 2 * tg + 1)
                sl = slice(tg * 512, tg * 512 + 512)
                s.activation(out=r_tmp[0:1, sl], in_=r_var[0:1, sl], func=AF.Ln, bias=1e-5)
                s.activation(out=r_rstd[0:1, sl], in_=r_tmp[0:1, sl], func=AF.Exp, scale=-0.5).then_inc(ax, 1)
                s.wait_ge(mm, s0["mm"] + 2 * tg + 2)
                s.activation(out=mrb[:, 0:512], in_=ps_m[:, :], func=AF.Copy)
                s.activation(out=mrb[:, 512:1024], in_=ps_r[:, :], func=AF.Copy).then_inc(ax, 1)

        @blk.tensor
        def _(t):
            t.wait_ge(xl, s0["xl"] + 16)
            for tg in range(NT):
                if tg >= 1:
                    t.wait_ge(vx, s0["vx"] + 2 * tg - 1)
                for k in range(NKD):
                    gsq = tg * NKD + k
                    t.matmul(ps_s[:, :], c["onesc"][:, :],
                             xcb[:, k * T + tg * 512:k * T + tg * 512 + 512],
                             start=(k == 0), stop=(k == NKD - 1))
                    t.wait_ge(ax, s0["ax"] + tg * AXTG + k + 1)
                    inst = t.matmul(ps_q[:, :], c["onesc"][:, :],
                             xsq[:, (gsq % 4) * 512:(gsq % 4) * 512 + 512],
                             start=(k == 0), stop=(k == NKD - 1))
                    inst.then_inc(mmf, 1) if k < NKD - 1 else inst.then_inc(mm, 1)
                t.wait_ge(vx, s0["vx"] + 2 * tg + 1)
                t.wait_ge(ax, s0["ax"] + tg * AXTG + NKD + 1)
                sl = slice(tg * 512, tg * 512 + 512)
                t.matmul(ps_m[:, :], c["ones1f"][0:1, 0:128], r_m[0:1, sl], start=True, stop=True)
                t.matmul(ps_r[:, :], c["ones1f"][0:1, 0:128], r_rstd[0:1, sl], start=True, stop=True).then_inc(mm, 1)

        @blk.vector
        def _(v):
            for tg in range(NT):
                sl = slice(tg * 512, tg * 512 + 512)
                v.wait_ge(mm, s0["mm"] + 2 * tg + 1)
                v.tensor_scalar(r_m[0:1, sl], ps_s[0:1, :], 1.0 / D, None, ALU.mult, ALU.bypass)
                v.tensor_scalar(r_q[0:1, sl], ps_q[0:1, :], 1.0 / D, None, ALU.mult, ALU.bypass)
                v.tensor_tensor(r_tmp[0:1, sl], r_m[0:1, sl], r_m[0:1, sl], ALU.mult)
                v.tensor_tensor(r_var[0:1, sl], r_q[0:1, sl], r_tmp[0:1, sl], ALU.subtract).then_inc(vx, 1)
                v.wait_ge(ax, s0["ax"] + tg * AXTG + NKD + 2)
                for k in range(NKD):
                    v.tensor_tensor(scr[:, :], xcb[:, k * T + tg * 512:k * T + tg * 512 + 512],
                                    mrb[:, 0:512], ALU.subtract)
                    inst = v.tensor_tensor(xln[:, k * T + tg * 512:k * T + tg * 512 + 512],
                                    scr[:, :], mrb[:, 512:1024], ALU.mult)
                inst.then_inc(vx, 1)

    p.cnt["xl"] = s0["xl"] + 16
    p.cnt["ax"] = s0["ax"] + NT * AXTG
    p.cnt["mm"] = s0["mm"] + NT * 2
    p.cnt["mmf"] = s0["mmf"] + NT * (NKD - 1)
    p.cnt["vx"] = s0["vx"] + NT * 2
    ls.close()


def emit_ffn(p, c, xln, io, I, scale=0.5, T=TL):
    nc = p.nc
    NKI = I // 128
    NT = T // 512
    OSLAB = 256
    slabs = []
    o0 = 0
    while o0 < I:
        w = min(OSLAB, I - o0)
        slabs.append((o0, w))
        o0 += w
    NS = len(slabs)
    G_of = [w // 128 * NT for (_, w) in slabs]
    Gtot = sum(G_of)

    ls = contextlib.ExitStack()
    wg_sb = p.sbuf("ff_wg", [128, 2 * NKD * OSLAB], BF16, ls)
    wu_sb = p.sbuf("ff_wu", [128, 2 * NKD * OSLAB], BF16, ls)
    sg = p.sbuf("ff_sg", [128, 2 * 512], BF16, ls)
    h_sb = p.sbuf("ff_h", [128, 2 * (OSLAB // 128) * T], BF16, ls)
    psg = [p.psum("ff_psg", [128, 512], scope=ls) for _ in range(2)]
    psu = [p.psum("ff_psu", [128, 512], scope=ls) for _ in range(2)]
    wl, mm, ax, vx, st = p.sem("wl"), p.sem("mm"), p.sem("ax"), p.sem("vx"), p.sem("st")
    s2 = p.sem("s2")
    hw = io["wgT"].tensor.dtype == BF16
    s0 = {k: p.cnt[k] for k in ("wl", "mm", "ax", "vx", "st", "s2")}
    if hw:
        wls, wlk = (p.sem("wh"), p.sem("w2")), ("wh", "w2")
    else:
        wls, wlk = (wl, wl), ("wl", "wl")
    s0[wlk[0]] = p.cnt[wlk[0]]
    s0[wlk[1]] = p.cnt[wlk[1]]
    sts, stk = (st, s2), ("st", "s2")
    wsz = NKD * OSLAB

    def wslice(buf_sb, b, k, o, width):
        base = b * wsz + k * OSLAB + o
        return buf_sb[:, base:base + width]

    def wloads(eng, s):
        os0, w = slabs[s]
        if s >= 2:
            eng.wait_ge(mm, s0["mm"] + 2 * sum(G_of[:s - 1]))
        b = s % 2
        for wt_d, dst in ((io["wgT"], wg_sb), (io["wuT"], wu_sb)):
            eng.dma_start(
                out=dst[:, b * wsz:(b + 1) * wsz]
                    .rearrange("p (k o) -> p k o", o=OSLAB)[:, :, 0:w],
                in_=wt_d[:, os0:os0 + w].rearrange("(k p) o -> p k o", p=128),
            ).then_inc(wls[s % 2] if hw else wl, 16)

    with nc.Block(f"ffnA_{p.uid()}") as blk:
        if not hw:
            @blk.gpsimd
            def _(g):
                for s in range(NS):
                    wloads(g, s)

        @blk.tensor
        def _(t):
            gi = 0
            for s, (os0, w) in enumerate(slabs):
                if hw:
                    t.wait_ge(wls[s % 2], s0[wlk[s % 2]] + 32 * (s // 2 + 1))
                else:
                    t.wait_ge(wl, s0["wl"] + 16 * 2 * (s + 1))
                b = s % 2
                for oc in range(w // 128):
                    for tg in range(NT):
                        if gi >= 2:
                            t.wait_ge(ax, s0["ax"] + gi - 1)
                            t.wait_ge(vx, s0["vx"] + gi - 1)
                        for k in range(NKD):
                            inst = t.matmul(psg[gi % 2][:, :], wslice(wg_sb, b, k, oc * 128, 128),
                                     xln[:, k * T + tg * 512:k * T + tg * 512 + 512],
                                     start=(k == 0), stop=(k == NKD - 1))
                        inst.then_inc(mm, 1)
                        for k in range(NKD):
                            inst = t.matmul(psu[gi % 2][:, :], wslice(wu_sb, b, k, oc * 128, 128),
                                     xln[:, k * T + tg * 512:k * T + tg * 512 + 512],
                                     start=(k == 0), stop=(k == NKD - 1))
                        inst.then_inc(mm, 1)
                        gi += 1

        @blk.scalar
        def _(s_):
            gi = 0
            for s, (os0, w) in enumerate(slabs):
                for oc in range(w // 128):
                    for tg in range(NT):
                        s_.wait_ge(mm, s0["mm"] + 2 * gi + 1)
                        if gi >= 2:
                            s_.wait_ge(vx, s0["vx"] + gi - 1)
                        s_.activation(out=sg[:, (gi % 2) * 512:(gi % 2) * 512 + 512],
                                      in_=psg[gi % 2][:, :], func=AF.Silu).then_inc(ax, 1)
                        gi += 1

        @blk.vector
        def _(v):
            gi = 0
            for s, (os0, w) in enumerate(slabs):
                b = s % 2
                if s >= 2:
                    v.wait_ge(sts[s % 2], s0[stk[s % 2]] + 16 * ((s - 2) // 2 + 1))
                for oc in range(w // 128):
                    for tg in range(NT):
                        v.wait_ge(mm, s0["mm"] + 2 * gi + 2)
                        v.wait_ge(ax, s0["ax"] + gi + 1)
                        v.tensor_tensor(
                            h_sb[:, b * (OSLAB // 128) * T + oc * T + tg * 512:
                                    b * (OSLAB // 128) * T + oc * T + tg * 512 + 512],
                            psu[gi % 2][:, :], sg[:, (gi % 2) * 512:(gi % 2) * 512 + 512],
                            ALU.mult).then_inc(vx, 1)
                        gi += 1

        @blk.sync
        def _(sy):
            if hw:
                wloads(sy, 0)
                wloads(sy, 1)
            gi = 0
            for s, (os0, w) in enumerate(slabs):
                b = s % 2
                gi += G_of[s]
                sy.wait_ge(vx, s0["vx"] + gi)
                noc = w // 128
                sy.dma_start(
                    out=io["h"][os0 // 128:os0 // 128 + noc, :, :].rearrange("o p t -> p o t"),
                    in_=h_sb[:, b * (OSLAB // 128) * T:b * (OSLAB // 128) * T + noc * T]
                        .rearrange("p (o t) -> p o t", o=noc),
                ).then_inc(sts[s % 2], 16)
                if hw and s + 2 < NS:
                    wloads(sy, s + 2)
            sy.wait_ge(st, s0["st"] + 16 * (NS - NS // 2))
            sy.wait_ge(s2, s0["s2"] + 16 * (NS // 2))

    if hw:
        p.cnt["wh"] = s0["wh"] + 32 * (NS - NS // 2)
        p.cnt["w2"] = s0["w2"] + 32 * (NS // 2)
    else:
        p.cnt["wl"] = s0["wl"] + 16 * 2 * NS
    p.cnt["mm"] = s0["mm"] + 2 * Gtot
    p.cnt["ax"] = s0["ax"] + Gtot
    p.cnt["vx"] = s0["vx"] + Gtot
    p.cnt["st"] = s0["st"] + 16 * (NS - NS // 2)
    p.cnt["s2"] = s0["s2"] + 16 * (NS // 2)
    ls.close()


def emit_copy_dram(p, src, dst):
    nc = p.nc
    sg = p.sem("sg")
    s0 = p.cnt["sg"]
    with nc.Block(f"copyx_{p.uid()}") as blk:
        @blk.gpsimd
        def _(g):
            g.dma_start(out=dst, in_=src).then_inc(sg, 16)
            g.wait_ge(sg, s0 + 16)
    p.cnt["sg"] = s0 + 16


def emit_retnet(p, c, io):
    nc = p.nc
    H = 4
    OW = H * R_DH
    NOC = OW // 128
    NTG = TF // 512
    CPG = 512 // R_C
    ls = contextlib.ExitStack()
    xg_sb = p.sbuf("rn_xg", [128, NKD * 512], BF16, ls)
    wq_sb = p.sbuf("rn_wq", [128, NKD * OW], BF16, ls)
    wk_sb = p.sbuf("rn_wk", [128, NKD * OW], BF16, ls)
    wv_sb = p.sbuf("rn_wv", [128, NKD * OW], BF16, ls)
    bq_sb = p.sbuf("rn_bq", [1, OW], BF16, ls)
    bk_sb = p.sbuf("rn_bk", [1, OW], BF16, ls)
    bv_sb = p.sbuf("rn_bv", [1, OW], BF16, ls)
    dt_sb = p.sbuf("rn_dt", [128, H * 2 * 256], BF16, ls)
    cs_sb = p.sbuf("rn_cs", [128, H * 256], BF16, ls)
    kv_sb = p.sbuf("rn_kv", [128, H * 256], BF16, ls)
    gmc_sb = p.sbuf("rn_gmc", [128, H], F32, ls)
    q_sb = p.sbuf("rn_q", [128, 2 * NOC * 512], BF16, ls)
    k_sb = p.sbuf("rn_k", [128, 2 * NOC * 512], BF16, ls)
    v_sb = p.sbuf("rn_v", [128, 2 * 4 * OW], BF16, ls)
    o_sb = p.sbuf("rn_o", [128, NOC * 512], BF16, ls)
    S_f = p.sbuf("rn_S", [128, H * 2 * 256], F32, ls)
    S_b = p.sbuf("rn_Sb", [128, H * 2 * 256], BF16, ls)
    at_sb = p.sbuf("rn_at", [128, 2 * 256], BF16, ls)
    qt_sb = p.sbuf("rn_qt", [128, 2 * 256], BF16, ls)
    kt_sb = p.sbuf("rn_kt", [128, 2 * 256], BF16, ls)
    ktr_sb = p.sbuf("rn_ktr", [128, 2 * 256], BF16, ls)
    ps_p = [p.psum("rn_pp", [128, 512], scope=ls) for _ in range(2)]
    ps_a = [p.psum("rn_pa", [128, 256], scope=ls) for _ in range(2)]
    ps_o = [p.psum("rn_po", [128, 256], scope=ls) for _ in range(2)]
    ps_t = p.psum("rn_pt", [128, 512], BF16, scope=ls)
    xl, wl, mm, ax, vx, st = (p.sem(k) for k in ("xl", "wl", "mm", "ax", "vx", "st"))
    s0 = {k: p.cnt[k] for k in ("xl", "wl", "mm", "ax", "vx", "st")}

    steps = [("P", 0), ("P", 1)]
    for t in range(8):
        steps.append(("R", t))
        if t + 2 < NTG:
            steps.append(("P", t + 2))
    NCH = NTG * CPG * H
    mmA = [0] * NCH; mmB = [0] * NCH; mmC = [0] * NCH; mmD = [0] * NCH
    vx1 = [0] * NCH; vx2 = [0] * NCH; vx3 = [0] * NCH
    axO = [0] * NCH; axK = [0] * NCH
    ax_after_P = [0] * NTG; mm_after_P = [0] * NTG
    axP = {}
    mmP = {}
    mm_cnt = s0["mm"]; ax_cnt = s0["ax"]; vx_cnt = s0["vx"]
    n_glob = 0
    for kind, t in steps:
        if kind == "P":
            for gi in range(3 * NOC):
                mm_cnt += 1
                mmP[(t, gi)] = mm_cnt
                ax_cnt += 1
                axP[(t, gi)] = ax_cnt
            ax_after_P[t] = ax_cnt
            mm_after_P[t] = mm_cnt
        else:
            for l in range(CPG * H):
                n = n_glob
                vx_cnt += 1; vx1[n] = vx_cnt
                mm_cnt += 1; mmA[n] = mm_cnt
                vx_cnt += 1; vx2[n] = vx_cnt
                mm_cnt += 1; mmB[n] = mm_cnt
                ax_cnt += 1; axO[n] = ax_cnt
                mm_cnt += 1; mmC[n] = mm_cnt
                ax_cnt += 1; axK[n] = ax_cnt
                mm_cnt += 1; mmD[n] = mm_cnt
                vx_cnt += 1; vx3[n] = vx_cnt
                n_glob += 1
    mm_tot, ax_tot, vx_tot = mm_cnt, ax_cnt, vx_cnt

    def ch_info(n):
        tg, r = divmod(n, CPG * H)
        cc, j = divmod(r, H)
        return tg, cc, j, tg * CPG + cc

    def qk_sl(buf, sb, j, kc, lo, w):
        base = buf * NOC * 512 + (j * 2 + kc) * 512
        return sb[:, base + lo:base + lo + w]

    def v_sl(buf, tsub, lo, w):
        base = buf * 4 * OW + tsub * OW
        return v_sb[:, base + lo:base + lo + w]

    with nc.Block(f"retnet_{p.uid()}") as blk:
        @blk.gpsimd
        def _(g):
            for wt, dst in ((io["wqT"], wq_sb), (io["wkT"], wk_sb), (io["wvT"], wv_sb)):
                g.dma_start(out=dst[:].rearrange("p (k o) -> p k o", k=NKD),
                            in_=wt.rearrange("(k p) o -> p k o", p=128)).then_inc(wl, 16)
            for bb, dst in ((io["bq"], bq_sb), (io["bk"], bk_sb), (io["bv"], bv_sb)):
                g.dma_start(out=dst[:, :], in_=bass.AP(bb.tensor, 0, [[0, 1], [1, OW]])).then_inc(wl, 16)
            g.dma_start(out=dt_sb[:].rearrange("p (h m i) -> p h m i", h=H, m=2),
                        in_=io["dtT"].rearrange("h (m p) i -> p h m i", p=128)).then_inc(wl, 16)
            g.dma_start(out=cs_sb[:].rearrange("p (h i) -> p h i", h=H),
                        in_=io["cst"].rearrange("h p i -> p h i")).then_inc(wl, 16)
            g.dma_start(out=kv_sb[:].rearrange("p (h i) -> p h i", h=H),
                        in_=io["kvt"].rearrange("h p i -> p h i")).then_inc(wl, 16)
            g.dma_start(out=gmc_sb[:, :], in_=bass.AP(io["gmc"].tensor, 0, [[0, 128], [1, H]])).then_inc(wl, 16)
            for tg in range(NTG):
                if tg >= 1:
                    g.wait_ge(mm, mm_after_P[tg - 1])
                slot, tgl = divmod(tg, 4)
                g.dma_start(out=xg_sb[:].rearrange("p (k t) -> p k t", k=NKD),
                            in_=io["xg"][slot, :, tgl * 512:tgl * 512 + 512]
                            .rearrange("(k p) t -> p k t", p=128)).then_inc(xl, 16)

        @blk.tensor
        def _(t_):
            t_.wait_ge(wl, s0["wl"] + 16 * 10)
            n = 0
            for kind, tg in steps:
                if kind == "P":
                    t_.wait_ge(xl, s0["xl"] + 16 * (tg + 1))
                    b = tg % 2
                    gi = 0
                    for w_sb, b_sb, feat in ((wq_sb, bq_sb, True), (wk_sb, bk_sb, True),
                                             (wv_sb, bv_sb, False)):
                        for oc in range(NOC):
                            pp = ps_p[gi % 2]
                            key = (tg, gi - 2)
                            if key in axP:
                                t_.wait_ge(ax, axP[key])
                            elif tg >= 1 and gi < 2:
                                t_.wait_ge(ax, axP[(tg - 1, 3 * NOC - 2 + gi)])
                            if feat:
                                for kk in range(NKD):
                                    t_.matmul(pp[:, :], w_sb[:, kk * OW + oc * 128:kk * OW + oc * 128 + 128],
                                              xg_sb[:, kk * 512:(kk + 1) * 512],
                                              start=(kk == 0), stop=False)
                                t_.matmul(pp[:, :], b_sb[0:1, oc * 128:oc * 128 + 128],
                                          c["ones1"][0:1, 0:512], start=False, stop=True).then_inc(mm, 1)
                            else:
                                tsub, og = divmod(oc, 2)
                                for kk in range(NKD):
                                    t_.matmul(pp[:, :],
                                              xg_sb[:, kk * 512 + tsub * 128:kk * 512 + tsub * 128 + 128],
                                              w_sb[:, kk * OW + og * 512:kk * OW + (og + 1) * 512],
                                              start=(kk == 0), stop=False)
                                t_.matmul(pp[:, :], c["ones1"][0:1, 0:128],
                                          b_sb[0:1, og * 512:og * 512 + 512],
                                          start=False, stop=True).then_inc(mm, 1)
                            gi += 1
                else:
                    for l in range(CPG * H):
                        tg_, cc, j, cab = ch_info(n)
                        b = tg_ % 2
                        lo = cc * 256
                        if n >= 1:
                            t_.wait_ge(vx, vx3[n - 1])
                        if l == 0:
                            t_.wait_ge(ax, ax_after_P[tg_])
                        for mj in range(2):
                            for kc in range(2):
                                inst = t_.matmul(ps_a[mj][:, :],
                                          qk_sl(b, k_sb, j, kc, lo + mj * 128, 128),
                                          qk_sl(b, q_sb, j, kc, lo, 256),
                                          start=(kc == 0), stop=(kc == 1))
                        inst.then_inc(mm, 1)
                        t_.wait_ge(vx, vx2[n])
                        if n >= 2:
                            t_.wait_ge(ax, axO[n - 2])
                        if n >= 4:
                            t_.wait_ge(vx, vx3[n - 4])
                        for vc in range(2):
                            po = ps_o[vc]
                            first = True
                            for jc in range(2):
                                t_.matmul(po[:, :], v_sl(b, cc * 2 + jc, j * 256 + vc * 128, 128),
                                          at_sb[:, jc * 256:jc * 256 + 256],
                                          start=first, stop=False)
                                first = False
                            for kc in range(2):
                                inst = t_.matmul(po[:, :], S_b[:, (j * 2 + kc) * 256 + vc * 128:
                                                           (j * 2 + kc) * 256 + vc * 128 + 128],
                                          qt_sb[:, kc * 256:kc * 256 + 256],
                                          start=False, stop=(kc == 1))
                        inst.then_inc(mm, 1)
                        if n >= 1:
                            t_.wait_ge(ax, axK[n - 1])
                        for jc in range(2):
                            for kc in range(2):
                                inst = t_.transpose(ps_t[:, (jc * 2 + kc) * 128:(jc * 2 + kc) * 128 + 128],
                                             kt_sb[:, kc * 256 + jc * 128:kc * 256 + jc * 128 + 128],
                                             c["ident"][:, :])
                        inst.then_inc(mm, 1)
                        t_.wait_ge(ax, axK[n])
                        for kcd in range(2):
                            pa = ps_a[kcd]
                            for jc in range(2):
                                inst = t_.matmul(pa[:, :], ktr_sb[:, jc * 256 + kcd * 128:jc * 256 + kcd * 128 + 128],
                                          v_sl(b, cc * 2 + jc, j * 256, 256),
                                          start=(jc == 0), stop=(jc == 1))
                        inst.then_inc(mm, 1)
                        n += 1

        @blk.vector
        def _(v):
            v.wait_ge(wl, s0["wl"] + 16 * 10)
            v.memset(S_f[:, :], 0.0)
            v.memset(S_b[:, :], 0.0)
            n = 0
            for kind, tg in steps:
                if kind != "R":
                    continue
                for l in range(CPG * H):
                    tg_, cc, j, cab = ch_info(n)
                    b = tg_ % 2
                    lo = cc * 256
                    if l == 0:
                        v.wait_ge(ax, ax_after_P[tg_])
                    if n >= 1:
                        v.wait_ge(mm, mmC[n - 1])
                    for kc in range(2):
                        v.tensor_tensor(qt_sb[:, kc * 256:kc * 256 + 256],
                                        qk_sl(b, q_sb, j, kc, lo, 256),
                                        cs_sb[:, j * 256:j * 256 + 256], ALU.mult)
                    for kc in range(2):
                        inst = v.tensor_tensor(kt_sb[:, kc * 256:kc * 256 + 256],
                                               qk_sl(b, k_sb, j, kc, lo, 256),
                                               kv_sb[:, j * 256:j * 256 + 256], ALU.mult)
                    inst.then_inc(vx, 1)
                    v.wait_ge(mm, mmA[n])
                    for mj in range(2):
                        inst = v.tensor_tensor(at_sb[:, mj * 256:mj * 256 + 256],
                                               ps_a[mj][:, :],
                                               dt_sb[:, (j * 2 + mj) * 256:(j * 2 + mj) * 256 + 256],
                                               ALU.mult)
                    inst.then_inc(vx, 1)
                    v.wait_ge(mm, mmD[n])
                    for kcd in range(2):
                        sl_ = slice((j * 2 + kcd) * 256, (j * 2 + kcd) * 256 + 256)
                        v.scalar_tensor_tensor(S_f[:, sl_], S_f[:, sl_], gmc_sb[:, j:j + 1],
                                               ps_a[kcd][:, :], ALU.mult, ALU.add)
                    for kcd in range(2):
                        inst = v.tensor_copy(S_b[:, (j * 2 + kcd) * 256:(j * 2 + kcd) * 256 + 256],
                                             S_f[:, (j * 2 + kcd) * 256:(j * 2 + kcd) * 256 + 256])
                    inst.then_inc(vx, 1)
                    n += 1

        @blk.scalar
        def _(s_):
            n = 0
            for kind, tg in steps:
                if kind == "P":
                    b = tg % 2
                    gi = 0
                    for which in range(3):
                        for oc in range(NOC):
                            s_.wait_ge(mm, mmP[(tg, gi)])
                            dst = (q_sb, k_sb, v_sb)[which]
                            if which < 2:
                                s_.activation(out=dst[:, b * NOC * 512 + oc * 512:b * NOC * 512 + (oc + 1) * 512],
                                              in_=ps_p[gi % 2][:, :], func=AF.Copy).then_inc(ax, 1)
                            else:
                                tsub, og = divmod(oc, 2)
                                s_.activation(out=v_sb[:, b * 4 * OW + tsub * OW + og * 512:
                                                          b * 4 * OW + tsub * OW + og * 512 + 512],
                                              in_=ps_p[gi % 2][:, :], func=AF.Copy).then_inc(ax, 1)
                            gi += 1
                else:
                    if tg >= 1:
                        s_.wait_ge(st, s0["st"] + 16 * tg)
                    for l in range(CPG * H):
                        tg_, cc, j, cab = ch_info(n)
                        b = tg_ % 2
                        s_.wait_ge(mm, mmB[n])
                        for vc in range(2):
                            inst = s_.activation(out=o_sb[:, (j * 2 + vc) * 512 + cc * 256:
                                                     (j * 2 + vc) * 512 + cc * 256 + 256],
                                          in_=ps_o[vc][:, :], func=AF.Copy)
                        inst.then_inc(ax, 1)
                        s_.wait_ge(mm, mmC[n])
                        for jc in range(2):
                            for kc in range(2):
                                inst = s_.activation(out=ktr_sb[:, jc * 256 + kc * 128:jc * 256 + kc * 128 + 128],
                                              in_=ps_t[:, (jc * 2 + kc) * 128:(jc * 2 + kc) * 128 + 128],
                                              func=AF.Copy)
                        inst.then_inc(ax, 1)
                        n += 1

        @blk.sync
        def _(sy):
            n_end = 0
            for kind, tg in steps:
                if kind != "R":
                    continue
                n_end += CPG * H
                sy.wait_ge(ax, axO[n_end - 1])
                b = tg % 2
                sy.dma_start(out=io["o_sh"][:, :, tg * 512:tg * 512 + 512].rearrange("o p t -> p o t"),
                             in_=o_sb[:].rearrange("p (o t) -> p o t", o=NOC)).then_inc(st, 16)
            sy.wait_ge(st, s0["st"] + 16 * NTG)

    p.cnt["xl"] = s0["xl"] + 16 * NTG
    p.cnt["wl"] = s0["wl"] + 16 * 10
    p.cnt["mm"] = mm_tot
    p.cnt["ax"] = ax_tot
    p.cnt["vx"] = vx_tot
    p.cnt["st"] = s0["st"] + 16 * NTG
    ls.close()


def emit_down(p, c, hT, wdT, x_acc, NKI, scale=0.5, T=TL, sel=None, wd_cache=None):
    nc = p.nc
    NT = T // 512
    ls = contextlib.ExitStack()
    hblk = p.sbuf("ff_hblk", [128, NKI * 512], BF16, ls)
    if sel is not None:
        hraw = p.sbuf("ff_hraw", [128, 2 * NKI * 512], BF16, ls)
        hscr = p.sbuf("ff_hscr", [128, 512], BF16, ls)
    vx = p.sem("vx")
    wd_sb = p.sbuf("ff_wd", [128, 2 * NKI * 128], BF16, ls)
    fsc = p.sbuf("ff_fsc", [128, 2 * 512], F32, ls)
    psd = [p.psum("ff_psd", [128, 512], scope=ls) for _ in range(2)]
    wl, xl, mm, ax, sg, wb, wh = (p.sem(k) for k in ("wl", "xl", "mm", "ax", "sg", "wb", "wh"))
    wbo, w2, wlo = p.sem("wbo"), p.sem("w2"), p.sem("wlo")
    s0 = {k: p.cnt[k] for k in ("wl", "xl", "mm", "ax", "sg", "vx", "wb", "wh",
                                "wbo", "w2", "wlo")}
    whs, wbs, wls = (wh, w2), (wb, wbo), (wl, wlo)
    whk, wbk, wlk = ("wh", "w2"), ("wb", "wbo"), ("wl", "wlo")
    wd_f32 = wdT.tensor.dtype != BF16
    if wd_f32 and wd_cache is None:
        wd_cache = _drt(nc, f"wdcache{p.uid()}", [NKI * 128, D], BF16)

    NTOT = NT * NKD

    def wd_load(eng, i, hw, sem):
        dc = i % NKD
        src = wd_cache if hw and wd_f32 else wdT
        eng.dma_start(out=wd_sb[:, (i % 2) * NKI * 128:(i % 2) * NKI * 128 + NKI * 128]
                      .rearrange("p (k o) -> p k o", k=NKI),
                      in_=src[:, dc * 128:dc * 128 + 128]
                      .rearrange("(k p) o -> p k o", p=128)).then_inc(sem, 16)

    def h_load(eng, tg):
        if tg >= 1:
            eng.wait_ge(mm, s0["mm"] + tg * NKD)
        if sel is None:
            eng.dma_start(out=hblk[:].rearrange("p (k t) -> p k t", k=NKI),
                          in_=hT[:, :, tg * 512:tg * 512 + 512]
                          .rearrange("k p t -> p k t")).then_inc(xl, 16)
        else:
            hT_ex, msel, ncper = sel
            nslot = NKI // ncper
            for hf in range(2):
                for slot in range(nslot):
                    eng.dma_start(
                        out=hraw[:, (hf * NKI + slot * ncper) * 512:(hf * NKI + (slot + 1) * ncper) * 512]
                        .rearrange("p (k t) -> p k t", k=ncper),
                        in_=hT_ex[slot, :, :, hf * TL + tg * 512:hf * TL + tg * 512 + 512]
                        .rearrange("k p t -> p k t")).then_inc(xl, 16)

    with nc.Block(f"ffnB_{p.uid()}") as blk:
        @blk.gpsimd
        def _(g):
            if wd_f32:
                wd_load(g, 0, False, wls[0])
                wd_load(g, 1, False, wls[1])
            for tg in range(NT):
                for dc in range(NKD):
                    i = tg * NKD + dc
                    g.wait_ge(ax, s0["ax"] + i + 1)
                    g.dma_start(out=x_acc[dc * 128:dc * 128 + 128, tg * 512:tg * 512 + 512],
                                in_=fsc[:, (i % 2) * 512:(i % 2) * 512 + 512],
                                accum_op=ALU.add).then_inc(sg, 16)
                    if wd_f32 and i + 2 < NKD:
                        g.wait_ge(wbs[i % 2], s0[wbk[i % 2]] + 16 * (i // 2 + 1))
                        wd_load(g, i + 2, False, wls[i % 2])
            g.wait_ge(sg, s0["sg"] + 16 * NTOT)

        @blk.sync
        def _(sy):
            h_load(sy, 0)
            if wd_f32:
                for dc in range(NKD):
                    sy.wait_ge(wls[dc % 2], s0[wlk[dc % 2]] + 16 * (dc // 2 + 1))
                    sy.dma_start(out=wd_cache[:, dc * 128:dc * 128 + 128]
                                 .rearrange("(k p) o -> p k o", p=128),
                                 in_=wd_sb[:, (dc % 2) * NKI * 128:(dc % 2) * NKI * 128 + NKI * 128]
                                 .rearrange("p (k o) -> p k o", k=NKI)).then_inc(wbs[dc % 2], 16)
                h_load(sy, 1)
                for j in range(NKD, NTOT):
                    dcn = j % NKD
                    sy.wait_ge(mm, s0["mm"] + j - 1)
                    wbtgt = dcn // 2 + 1
                    if j - 2 < NKD:
                        wbtgt = max(wbtgt, (j - 2) // 2 + 1)
                    sy.wait_ge(wbs[dcn % 2], s0[wbk[dcn % 2]] + 16 * wbtgt)
                    wd_load(sy, j, True, whs[j % 2])
                    if dcn == NKD - 1 and j // NKD + 1 < NT:
                        h_load(sy, j // NKD + 1)
                sy.wait_ge(wb, s0["wb"] + 16 * (NKD - NKD // 2))
                sy.wait_ge(wbo, s0["wbo"] + 16 * (NKD // 2))
            else:
                wd_load(sy, 0, True, whs[0])
                wd_load(sy, 1, True, whs[1])
                for tg in range(NT):
                    for dc in range(NKD):
                        i = tg * NKD + dc
                        if dc == NKD - 1 and tg + 1 < NT:
                            h_load(sy, tg + 1)
                        if i + 2 < NTOT:
                            sy.wait_ge(mm, s0["mm"] + i + 1)
                            wd_load(sy, i + 2, True, whs[i % 2])

        if sel is not None:
            @blk.vector
            def _(v):
                hT_ex, msel, ncper = sel
                for tg in range(NT):
                    v.wait_ge(xl, s0["xl"] + 16 * 4 * (tg + 1))
                    if tg >= 1:
                        v.wait_ge(mm, s0["mm"] + tg * NKD)
                    for k in range(NKI):
                        inst = v.tensor_scalar(hscr[:, :], hraw[:, k * 512:k * 512 + 512],
                                               msel[:, 0:1], None, ALU.mult, ALU.bypass)
                        inst = v.scalar_tensor_tensor(hblk[:, k * 512:k * 512 + 512],
                                                      hraw[:, (NKI + k) * 512:(NKI + k) * 512 + 512],
                                                      msel[:, 1:2], hscr[:, :], ALU.mult, ALU.add)
                    inst.then_inc(vx, 1)

        @blk.tensor
        def _(t):
            for tg in range(NT):
                if sel is None:
                    t.wait_ge(xl, s0["xl"] + 16 * (tg + 1))
                else:
                    t.wait_ge(vx, s0["vx"] + tg + 1)
                for dc in range(NKD):
                    i = tg * NKD + dc
                    if wd_f32 and i < NKD:
                        t.wait_ge(wls[i % 2], s0[wlk[i % 2]] + 16 * (i // 2 + 1))
                    elif wd_f32:
                        j2 = i - NKD
                        t.wait_ge(whs[i % 2], s0[whk[i % 2]] + 16 * (j2 // 2 + 1))
                    else:
                        t.wait_ge(whs[i % 2], s0[whk[i % 2]] + 16 * (i // 2 + 1))
                    if i >= 2:
                        t.wait_ge(ax, s0["ax"] + i - 1)
                    for k in range(NKI):
                        inst = t.matmul(psd[i % 2][:, :],
                                 wd_sb[:, (i % 2) * NKI * 128 + k * 128:(i % 2) * NKI * 128 + k * 128 + 128],
                                 hblk[:, k * 512:k * 512 + 512],
                                 start=(k == 0), stop=(k == NKI - 1))
                    inst.then_inc(mm, 1)

        @blk.scalar
        def _(s_):
            for tg in range(NT):
                for dc in range(NKD):
                    i = tg * NKD + dc
                    s_.wait_ge(mm, s0["mm"] + i + 1)
                    if i >= 2:
                        s_.wait_ge(sg, s0["sg"] + 16 * (i - 1))
                    s_.activation(out=fsc[:, (i % 2) * 512:(i % 2) * 512 + 512],
                                  in_=psd[i % 2][:, :], func=AF.Copy, scale=scale).then_inc(ax, 1)

    nhw = NT * NKD - NKD if wd_f32 else NT * NKD
    if wd_f32:
        p.cnt["wl"] = s0["wl"] + 16 * (NKD - NKD // 2)
        p.cnt["wlo"] = s0["wlo"] + 16 * (NKD // 2)
        p.cnt["wb"] = s0["wb"] + 16 * (NKD - NKD // 2)
        p.cnt["wbo"] = s0["wbo"] + 16 * (NKD // 2)
    p.cnt["wh"] = s0["wh"] + 16 * (nhw - nhw // 2)
    p.cnt["w2"] = s0["w2"] + 16 * (nhw // 2)
    p.cnt["xl"] = s0["xl"] + 16 * NT * (1 if sel is None else 4)
    p.cnt["mm"] = s0["mm"] + NT * NKD
    p.cnt["ax"] = s0["ax"] + NT * NKD
    p.cnt["sg"] = s0["sg"] + 16 * NT * NKD
    if sel is not None:
        p.cnt["vx"] = s0["vx"] + NT
    ls.close()




def emit_resid_add(p, c, osel, x_acc, scale=1.0, T=TL):
    nc = p.nc
    NT = T // 512
    ls = contextlib.ExitStack()
    tb = p.sbuf("ra_t", [128, 2 * NKD * 512], BF16, ls)
    fb = p.sbuf("ra_f", [128, 2 * 512], F32, ls)
    xl, ax, sg = (p.sem(k) for k in ("xl", "ax", "sg"))
    s0 = {k: p.cnt[k] for k in ("xl", "ax", "sg")}
    with nc.Block(f"resid_{p.uid()}") as blk:
        @blk.gpsimd
        def _(g):
            for tg in range(NT):
                if tg >= 2:
                    g.wait_ge(ax, s0["ax"] + (tg - 1) * NKD)
                g.dma_start(out=tb[:, (tg % 2) * NKD * 512:(tg % 2 + 1) * NKD * 512]
                            .rearrange("p (k t) -> p k t", k=NKD),
                            in_=osel[:, :, tg * 512:tg * 512 + 512]
                            .rearrange("k p t -> p k t")).then_inc(xl, 16)
                for dc in range(NKD):
                    i = tg * NKD + dc
                    g.wait_ge(ax, s0["ax"] + i + 1)
                    g.dma_start(out=x_acc[dc * 128:dc * 128 + 128, tg * 512:tg * 512 + 512],
                                in_=fb[:, (i % 2) * 512:(i % 2) * 512 + 512],
                                accum_op=ALU.add).then_inc(sg, 16)
            g.wait_ge(sg, s0["sg"] + 16 * NT * NKD)

        @blk.scalar
        def _(s_):
            for tg in range(NT):
                s_.wait_ge(xl, s0["xl"] + 16 * (tg + 1))
                for dc in range(NKD):
                    i = tg * NKD + dc
                    if i >= 2:
                        s_.wait_ge(sg, s0["sg"] + 16 * (i - 1))
                    s_.activation(out=fb[:, (i % 2) * 512:(i % 2) * 512 + 512],
                                  in_=tb[:, (tg % 2) * NKD * 512 + dc * 512:
                                         (tg % 2) * NKD * 512 + dc * 512 + 512],
                                  func=AF.Copy, scale=scale).then_inc(ax, 1)

    p.cnt["xl"] = s0["xl"] + 16 * NT
    p.cnt["ax"] = s0["ax"] + NT * NKD
    p.cnt["sg"] = s0["sg"] + 16 * NT * NKD
    ls.close()


def emit_deltanet(p, c, io, phases=(1, 2, 3, 4)):
    nc = p.nc
    H = 3
    OW = H * L_DH
    NPC = OW // 128
    NTG = TF // 512
    NCH = TF // L_C
    ls = contextlib.ExitStack()
    rows15 = p.sbuf("dn_rows15", [15, TF], F32, ls)
    selhot = p.sbuf("dn_selhot", [15, 15 * 128], F32, ls)
    trim = p.sbuf("dn_trim", [64, 192], BF16, ls)
    ct_sb = p.sbuf("dn_ct", [128, 3 * NPC * 4], F32, ls)
    onw_sb = p.sbuf("dn_onw", [128, 2], F32, ls)
    eA_sb = p.sbuf("dn_eA", [3, 1], F32, ls)
    dtb_sb = p.sbuf("dn_dtb", [3, 1], F32, ls)
    l12 = contextlib.ExitStack()
    beta_r = p.sbuf("dn_beta", [3, TF], F32, l12)
    row_t = p.sbuf("dn_rowt", [3, TF], F32, l12)
    xl, wl, mm, mmf, ax, vx, st, vf = (p.sem(k) for k in
                                       ("xl", "wl", "mm", "mmf", "ax", "vx", "st", "vf"))

    l1 = contextlib.ExitStack()
    xg_sb = p.sbuf("dn_xg", [128, 2 * NKD * 512], BF16, l1)
    wA_sb = p.sbuf("dn_wA", [128, NKD * OW], BF16, l1)
    wB_sb = p.sbuf("dn_wB", [128, NKD * OW], BF16, l1)
    wb_sb = p.sbuf("dn_wb", [128, NKD * 3], BF16, l1)
    wa_sb = p.sbuf("dn_wa", [128, NKD * 3], BF16, l1)
    prj = p.sbuf("dn_prj", [128, 4 * NPC * 512], BF16, l1)
    ps_p = [p.psum("dn_pp", [128, 512], scope=l1) for _ in range(2)]
    ps_b = p.psum("dn_pb", [3, 512], scope=l1)
    ps_a2 = p.psum("dn_pa2", [3, 512], scope=l1)
    s0 = {k: p.cnt[k] for k in ("xl", "wl", "mm", "ax", "st")}
    GPT = 2 * NPC
    mm_of = {}; ax_of = {}
    mm_c, ax_c = s0["mm"], s0["ax"]
    for ps_ in range(2):
        for tg in range(NTG):
            for gi in range(GPT):
                mm_c += 1; mm_of[(ps_, tg, gi)] = mm_c
                ax_c += 1; ax_of[(ps_, tg, gi)] = ax_c
            if ps_ == 0:
                mm_c += 2; ax_c += 1
    mm_end1, ax_end1 = mm_c, ax_c

    def mm_tg_end(ps_, tg):
        return mm_of[(ps_, tg, GPT - 1)] + (2 if ps_ == 0 else 0)

    ends = [mm_tg_end(ti // NTG, ti % NTG) for ti in range(2 * NTG)]
    xlh, xlo, s2 = p.sem("xh"), p.sem("xo"), p.sem("s2")
    for k in ("xh", "xo", "s2"):
        s0[k] = p.cnt[k]
    xsem, xkey = (xlh, xlo), ("xh", "xo")
    sts, stk = (st, s2), ("st", "s2")

    def xg_load(eng, ti):
        if ti >= 2:
            eng.wait_ge(mm, ends[ti - 2])
        slot, tgl = divmod(ti % NTG, 4)
        eng.dma_start(out=xg_sb[:, (ti % 2) * NKD * 512:(ti % 2 + 1) * NKD * 512]
                      .rearrange("p (k t) -> p k t", k=NKD),
                      in_=io["xg"][slot, :, tgl * 512:tgl * 512 + 512]
                      .rearrange("(k p) t -> p k t", p=128)).then_inc(xsem[ti % 2], 16)

    with nc.Block(f"dn1_{p.uid()}") as blk:
        @blk.gpsimd
        def _(g):
            g.dma_start(out=wA_sb[:].rearrange("p (k o) -> p k o", k=NKD),
                        in_=io["wqT"].rearrange("(k p) o -> p k o", p=128)).then_inc(wl, 16)
            g.dma_start(out=wB_sb[:].rearrange("p (k o) -> p k o", k=NKD),
                        in_=io["wkT"].rearrange("(k p) o -> p k o", p=128)).then_inc(wl, 16)
            g.dma_start(out=wb_sb[:].rearrange("p (k o) -> p k o", k=NKD),
                        in_=io["wbT"].rearrange("(k p) o -> p k o", p=128)).then_inc(wl, 16)
            g.dma_start(out=wa_sb[:].rearrange("p (k o) -> p k o", k=NKD),
                        in_=io["waT"].rearrange("(k p) o -> p k o", p=128)).then_inc(wl, 16)
            g.wait_ge(mm, mm_of[(0, NTG - 1, NPC - 1)])
            g.dma_start(out=wA_sb[:].rearrange("p (k o) -> p k o", k=NKD),
                        in_=io["wvT"].rearrange("(k p) o -> p k o", p=128)).then_inc(wl, 16)
            g.wait_ge(mm, mm_tg_end(0, NTG - 1))
            g.dma_start(out=wB_sb[:].rearrange("p (k o) -> p k o", k=NKD),
                        in_=io["wgT"].rearrange("(k p) o -> p k o", p=128)).then_inc(wl, 16)

        @blk.tensor
        def _(t_):
            for ps_ in range(2):
                t_.wait_ge(wl, s0["wl"] + 16 * (4 if ps_ == 0 else 6))
                for tg in range(NTG):
                    ti = ps_ * NTG + tg
                    xb = (ti % 2) * NKD * 512
                    t_.wait_ge(xsem[ti % 2], s0[xkey[ti % 2]] + 16 * (ti // 2 + 1))
                    gi = 0
                    for w_s in (wA_sb, wB_sb):
                        for oc in range(NPC):
                            key = (ps_, tg, gi - 2)
                            if gi >= 2:
                                t_.wait_ge(ax, ax_of[key])
                            elif (ps_, tg) != (0, 0):
                                pt = (ps_, tg - 1) if tg else (0, NTG - 1)
                                t_.wait_ge(ax, ax_of[pt + (GPT - 2 + gi,)])
                            pp = ps_p[gi % 2]
                            for kk in range(NKD):
                                inst = t_.matmul(pp[:, :],
                                                 w_s[:, kk * OW + oc * 128:kk * OW + oc * 128 + 128],
                                                 xg_sb[:, xb + kk * 512:xb + (kk + 1) * 512],
                                                 start=(kk == 0), stop=(kk == NKD - 1))
                            inst.then_inc(mm, 1)
                            gi += 1
                    if ps_ == 0:
                        for kk in range(NKD):
                            inst = t_.matmul(ps_b[0:3, :], wb_sb[:, kk * 3:kk * 3 + 3],
                                             xg_sb[:, xb + kk * 512:xb + (kk + 1) * 512],
                                             start=(kk == 0), stop=(kk == NKD - 1))
                        inst.then_inc(mm, 1)
                        for kk in range(NKD):
                            inst = t_.matmul(ps_a2[0:3, :], wa_sb[:, kk * 3:kk * 3 + 3],
                                             xg_sb[:, xb + kk * 512:xb + (kk + 1) * 512],
                                             start=(kk == 0), stop=(kk == NKD - 1))
                        inst.then_inc(mm, 1)

        @blk.scalar
        def _(s_):
            for ps_ in range(2):
                for tg in range(NTG):
                    ti = ps_ * NTG + tg
                    pb = (ti % 2) * GPT * 512
                    gi = 0
                    for wi in range(2):
                        fn = AF.Silu if (ps_ == 1 and wi == 1) else AF.Copy
                        for oc in range(NPC):
                            s_.wait_ge(mm, mm_of[(ps_, tg, gi)])
                            if ti >= 2 and gi == 0:
                                s_.wait_ge(sts[ti % 2],
                                           s0[stk[ti % 2]] + 32 * ((ti - 2) // 2 + 1))
                            s_.activation(out=prj[:, pb + (wi * NPC + oc) * 512:pb + (wi * NPC + oc) * 512 + 512],
                                          in_=ps_p[gi % 2][:, :], func=fn).then_inc(ax, 1)
                            gi += 1
                    if ps_ == 0:
                        s_.wait_ge(mm, mm_tg_end(0, tg))
                        sl = slice(tg * 512, tg * 512 + 512)
                        s_.activation(out=beta_r[0:3, sl], in_=ps_b[0:3, :], func=AF.Sigmoid)
                        s_.activation(out=row_t[0:3, sl], in_=ps_a2[0:3, :], func=AF.Copy).then_inc(ax, 1)

        @blk.sync
        def _(sy):
            xg_load(sy, 0)
            xg_load(sy, 1)
            for ps_ in range(2):
                names = (("qraw", "kraw") if ps_ == 0 else ("vraw", "gsp"))
                for tg in range(NTG):
                    ti = ps_ * NTG + tg
                    pb = (ti % 2) * GPT * 512
                    sy.wait_ge(ax, ax_of[(ps_, tg, GPT - 1)])
                    for wi, dst in enumerate(names):
                        sy.dma_start(out=io[dst][:, :, tg * 512:tg * 512 + 512].rearrange("o p t -> p o t"),
                                     in_=prj[:, pb + wi * NPC * 512:pb + (wi + 1) * NPC * 512]
                                     .rearrange("p (o t) -> p o t", o=NPC)).then_inc(sts[ti % 2], 16)
                    if ti + 2 < 2 * NTG:
                        xg_load(sy, ti + 2)
            sy.wait_ge(st, s0["st"] + 32 * NTG)
            sy.wait_ge(s2, s0["s2"] + 32 * NTG)

    p.cnt["xh"] = s0["xh"] + 16 * NTG
    p.cnt["xo"] = s0["xo"] + 16 * NTG
    p.cnt["s2"] = s0["s2"] + 32 * NTG
    p.cnt["wl"] = s0["wl"] + 16 * 6
    p.cnt["mm"] = mm_end1
    p.cnt["ax"] = ax_end1
    p.cnt["st"] = s0["st"] + 32 * NTG
    l1.close()
    if 2 not in phases:
        l12.close()
        ls.close()
        return
    l2 = contextlib.ExitStack()
    ga_r = p.sbuf("dn_ga", [3, TF], F32, l2)
    ga2_r = p.sbuf("dn_ga2", [3, TF], F32, l2)
    raw = p.sbuf("dn_raw", [128, TF], BF16, l2)
    acc = p.sbuf("dn_acc", [128, TF], BF16, l2)
    tmp = p.sbuf("dn_tmp", [128, TF], BF16, l2)
    sq2 = p.sbuf("dn_sq", [128, 2 * TF], BF16, l2)
    cnv = p.sbuf("dn_cnv", [128, 2 * TF], BF16, l2)
    nrm = p.sbuf("dn_nrm", [128, 4 * 512], BF16, l2)
    vt4 = p.sbuf("dn_vt4", [128, 2 * 512], BF16, l2)
    vrow = p.sbuf("dn_vrow", [128, TF], BF16, l2)
    rbt = p.sbuf("dn_rbt", [128, 2 * 512], BF16, l2)
    tmod = p.sbuf("dn_tmod", [3, TF], mybir.dt.int8, l2)
    mskb = p.sbuf("dn_mskb", [3, TF], mybir.dt.int8, l2)
    ep_r = p.sbuf("dn_ep", [3, TF], BF16, l2)
    rn2 = p.sbuf("dn_rn2", [65, TF], BF16, l2)
    rk2 = p.sbuf("dn_rk2", [65, TF], BF16, l2)
    rnrow = p.sbuf("dn_rnr", [1, 512], F32, l2)
    ps_ss = p.psum("dn_pss", [1, 512], scope=l2)
    ps_rb = p.psum("dn_prb", [128, 512], scope=l2)
    ps_tt = p.psum("dn_ptt", [128, 512], BF16, scope=l2)
    sg = p.sem("sg")
    s0 = {k: p.cnt[k] for k in ("xl", "wl", "mm", "ax", "vx", "gx", "sg", "vf")}
    gx = p.sem("gx")

    NROW = 18
    V_conv = [0] * NROW; A_sq = [0] * NROW
    M_ss = {}; A_rn = {}; M_rb = {}; A_rb = {}; V_m = {}
    M_tt = {}; A_tt = {}; ST = {}
    vx_c = s0["vx"] + 4
    ax_c = s0["ax"] + 3
    mm_c = s0["mm"]
    st_c = s0["sg"] + 16 * 5
    for i in range(NROW):
        mi, pc = divmod(i, NPC)
        vx_c += 1; V_conv[i] = vx_c
        ax_c += 1; A_sq[i] = ax_c
        if mi < 2 and pc % 2 == 1:
            for tg in range(NTG):
                mm_c += 1; M_ss[(i, tg)] = mm_c
                ax_c += 1; A_rn[(i, tg)] = ax_c
            for pc2 in range(2):
                for tg in range(NTG):
                    mm_c += 1; M_rb[(i, pc2, tg)] = mm_c
                    ax_c += 1; A_rb[(i, pc2, tg)] = ax_c
                    vx_c += 1; V_m[(i, pc2, tg)] = vx_c
                    st_c += 16; ST[(i, pc2, tg)] = st_c
        if mi == 2:
            for tg in range(NTG):
                mm_c += 1; M_tt[(i, tg)] = mm_c
                ax_c += 1; A_tt[(i, tg)] = ax_c
                st_c += 16; ST[(i, tg)] = st_c
    vx_end, ax_end, mm_end, st_end = vx_c, ax_c, mm_c, st_c
    applies = sorted([k for k in V_m], key=lambda k: V_m[k])
    vdmas = sorted([k for k in M_tt], key=lambda k: M_tt[k])

    ssc = [p.cnt["mmf"]]
    with nc.Block(f"dn2_{p.uid()}") as blk:
        @blk.gpsimd
        def _(g):
            g.dma_start(out=ct_sb[:].rearrange("p (m c t) -> p m c t", m=3, c=NPC),
                        in_=io["convw"].rearrange("m (c p) t -> p m c t", p=128)).then_inc(wl, 16)
            g.dma_start(out=trim[:, :], in_=io["trimask"]).then_inc(wl, 16)
            g.dma_start(out=selhot[:, :], in_=io["selhot"]).then_inc(wl, 16)
            g.dma_start(out=onw_sb[:, :], in_=io["onw"].rearrange("(a p) -> p a", p=128), allow_slow_non_contiguous=True).then_inc(wl, 16)
            g.dma_start(out=eA_sb[:, :], in_=io["alog"].rearrange("(a b) -> a b", b=1)).then_inc(wl, 16)
            g.dma_start(out=dtb_sb[:, :], in_=io["dtb"].rearrange("(a b) -> a b", b=1)).then_inc(wl, 16)
            g.iota(tmod[0:3, :].rearrange("r (a b) -> r a b", a=64), [[0, 64], [1, 64]],
                   base=0, channel_multiplier=0, allow_small_or_imprecise_dtypes=True).then_inc(gx, 1)
            g.wait_ge(vx, s0["vx"] + 4)
            g.wait_ge(ax, s0["ax"] + 3)
            g.dma_start(out=rows15[0:3, :], in_=ga_r[0:3, :]).then_inc(sg, 16)
            g.dma_start(out=rows15[3:6, :], in_=beta_r[0:3, :]).then_inc(sg, 16)
            g.dma_start(out=rows15[6:9, :], in_=ep_r[0:3, :]).then_inc(sg, 16)
            g.dma_start(out=rows15[9:12, :], in_=row_t[0:3, :]).then_inc(sg, 16)
            g.dma_start(out=rows15[12:15, :], in_=ga2_r[0:3, :]).then_inc(sg, 16)

            def spills(g, j):
                mi, pc = divmod(j, NPC)
                if mi < 2 and pc % 2 == 1:
                    for pc2b in range(2):
                        for tg in range(NTG):
                            key = (j, pc2b, tg)
                            g.wait_ge(vx, V_m[key])
                            g.dma_start(out=io[("qhat", "khat")[mi]][pc - 1 + pc2b, :, tg * 512:tg * 512 + 512],
                                        in_=nrm[:, (V_m[key] % 4) * 512:(V_m[key] % 4) * 512 + 512]).then_inc(sg, 16)
                elif mi == 2:
                    for tg in range(NTG):
                        g.wait_ge(ax, A_tt[(j, tg)])
                        g.dma_start(out=io["vtok"][tg * 512:(tg + 1) * 512, (j - 12) * 128:(j - 11) * 128]
                                    .rearrange("(a p) o -> p a o", p=128),
                                    in_=vt4[:, (tg % 2) * 512:(tg % 2) * 512 + 512]
                                    .rearrange("p (a o) -> p a o", o=128)).then_inc(sg, 16)

            for i in range(NROW):
                if i >= 2:
                    spills(g, i - 2)
                mi, pc = divmod(i, NPC)
                if i >= 1:
                    g.wait_ge(vx, V_conv[i - 1])
                g.dma_start(out=raw[:, :],
                            in_=io[("qraw", "kraw", "vraw")[mi]][pc, :, :]).then_inc(xl, 16)
            spills(g, NROW - 2)
            spills(g, NROW - 1)
            g.wait_ge(sg, st_end)

        @blk.scalar
        def _(s_):
            def ssync(inst):
                ssc[0] += 1
                inst.then_inc(mmf, 1)
                s_.wait_ge(mmf, ssc[0])
            s_.wait_ge(wl, s0["wl"] + 16 * 6)
            s_.activation(out=eA_sb[0:3, :], in_=eA_sb[0:3, :], func=AF.Exp)
            s_.wait_ge(vx, s0["vx"] + 1)
            ssync(s_.activation(out=ga_r[0:3, :], in_=ga2_r[0:3, :], func=AF.Abs))
            ssync(s_.activation(out=ga_r[0:3, :], in_=ga_r[0:3, :], func=AF.Exp, scale=-1.0))
            s_.activation(out=ga_r[0:3, :], in_=ga_r[0:3, :], func=AF.Ln, bias=1.0).then_inc(ax, 1)
            s_.wait_ge(vx, s0["vx"] + 3)
            s_.activation(out=ep_r[0:3, :], in_=ga_r[0:3, :], func=AF.Exp).then_inc(ax, 1)
            s_.wait_ge(vx, s0["vx"] + 4)
            s_.activation(out=ga2_r[0:3, :], in_=ga2_r[0:3, :], func=AF.Exp).then_inc(ax, 1)
            for i in range(NROW):
                mi, pc = divmod(i, NPC)
                s_.wait_ge(vx, V_conv[i])
                if mi < 2:
                    pc2 = pc % 2
                    if pc2 == 0 and i >= 2:
                        pa = i - 1 if (i - 1) % NPC % 2 == 1 and (i - 1) // NPC < 2 else i - 2
                        if pa >= 0 and pa // NPC < 2 and pa % NPC % 2 == 1:
                            s_.wait_ge(vx, V_m[(pa, 1, NTG - 1)])
                    ssync(s_.activation(out=cnv[:, pc2 * TF:(pc2 + 1) * TF], in_=acc[:, :], func=AF.Silu))
                    s_.activation(out=sq2[:, pc2 * TF:(pc2 + 1) * TF],
                                  in_=cnv[:, pc2 * TF:(pc2 + 1) * TF], func=AF.Square).then_inc(ax, 1)
                    if pc2 == 1:
                        hh = (pc - 1) // 2
                        rr = (rn2, rk2)[mi]
                        rsl = slice(32 * hh, 32 * hh + 1)
                        for tg in range(NTG):
                            sl = slice(tg * 512, tg * 512 + 512)
                            s_.wait_ge(mm, M_ss[(i, tg)])
                            ssync(s_.activation(out=rnrow[0:1, :], in_=ps_ss[0:1, :], func=AF.Ln, bias=1e-6))
                            s_.activation(out=rr[rsl, sl], in_=rnrow[0:1, :],
                                          func=AF.Exp, scale=-0.5).then_inc(ax, 1)
                        for pc2b in range(2):
                            for tg in range(NTG):
                                s_.wait_ge(mm, M_rb[(i, pc2b, tg)])
                                k2 = (pc2b, tg)
                                if k2 > (0, 1):
                                    prev2 = (pc2b, tg - 2) if tg >= 2 else (pc2b - 1, NTG - 2 + tg)
                                    s_.wait_ge(vx, V_m[(i,) + prev2])
                                s_.activation(out=rbt[:, (tg % 2) * 512:(tg % 2) * 512 + 512],
                                              in_=ps_rb[:, :], func=AF.Copy).then_inc(ax, 1)
                else:
                    if i >= 13:
                        s_.wait_ge(mm, M_tt[(i - 1, NTG - 1)])
                    s_.activation(out=vrow[:, :], in_=acc[:, :], func=AF.Silu).then_inc(ax, 1)
                    for tg in range(NTG):
                        s_.wait_ge(mm, M_tt[(i, tg)])
                        if (i, tg) > (12, 1):
                            pk = (i, tg - 2) if tg >= 2 else (i - 1, NTG - 2 + tg)
                            s_.wait_ge(sg, ST[pk])
                        for tc4 in range(4):
                            inst = s_.activation(
                                out=vt4[:, (tg % 2) * 512 + tc4 * 128:(tg % 2) * 512 + tc4 * 128 + 128],
                                in_=ps_tt[:, tc4 * 128:tc4 * 128 + 128], func=AF.Copy)
                        inst.then_inc(ax, 1)

        @blk.vector
        def _(v):
            v.wait_ge(wl, s0["wl"] + 16 * 6)
            v.tensor_scalar(ga2_r[0:3, :], row_t[0:3, :], dtb_sb[0:3, 0:1], None,
                            ALU.add, ALU.bypass).then_inc(vx, 1)
            v.wait_ge(ax, s0["ax"] + 1)
            v.tensor_scalar(row_t[0:3, :], ga2_r[0:3, :], 0.0, None, ALU.max,
                            ALU.bypass).then_inc(vf, 1)
            v.wait_ge(vf, s0["vf"] + 1)
            v.tensor_tensor(row_t[0:3, :], row_t[0:3, :], ga_r[0:3, :], ALU.add)
            v.tensor_scalar(ga_r[0:3, :], row_t[0:3, :], eA_sb[0:3, 0:1], -1.0,
                            ALU.mult, ALU.mult).then_inc(vx, 1)
            v.wait_ge(gx, s0["gx"] + 1)
            src_, dst_ = ga_r, ga2_r
            for ss in range(6):
                w = 1 << ss
                v.tensor_scalar(mskb[0:3, :], tmod[0:3, :], float(w), None, ALU.is_ge, ALU.bypass)
                v.tensor_tensor(row_t[0:3, 0:TF - w], src_[0:3, 0:TF - w], mskb[0:3, w:TF], ALU.mult)
                v.tensor_copy(dst_[0:3, 0:w], src_[0:3, 0:w])
                v.tensor_tensor(dst_[0:3, w:TF], src_[0:3, w:TF], row_t[0:3, 0:TF - w], ALU.add)
                src_, dst_ = dst_, src_
            if src_ is not ga_r:
                v.tensor_copy(ga_r[0:3, :], src_[0:3, :])
            v.tensor_copy(ga2_r[0:3, :], ga_r[0:3, :]).then_inc(vx, 1)
            v.wait_ge(ax, s0["ax"] + 2)
            v.tensor_tensor(row_t[0:3, :], beta_r[0:3, :], ep_r[0:3, :], ALU.mult)
            gae = bass.AP(ga_r.ap().tensor, 63, [ga_r.ap().ap[0], [64, 64], [0, 64]])
            v.tensor_tensor(ga2_r[0:3, :].rearrange("r (a b) -> r a b", a=64),
                            gae, ga_r[0:3, :].rearrange("r (a b) -> r a b", a=64),
                            ALU.subtract).then_inc(vx, 1)
            for i in range(NROW):
                mi, pc = divmod(i, NPC)
                v.wait_ge(xl, s0["xl"] + 16 * (i + 1))
                rb_ = raw[:, :]
                wcol = lambda tap, mi=mi, pc=pc: ct_sb[:, (mi * NPC + pc) * 4 + tap:(mi * NPC + pc) * 4 + tap + 1]
                if i >= 1:
                    v.wait_ge(ax, A_sq[i - 1])
                inst = v.tensor_scalar(acc[:, :], rb_, wcol(3), None, ALU.mult, ALU.bypass)
                for tap in range(3):
                    d = 3 - tap
                    v.tensor_scalar(tmp[:, 0:TF - d], rb_[:, 0:TF - d], wcol(tap), None, ALU.mult, ALU.bypass)
                    inst = v.tensor_tensor(acc[:, d:TF], acc[:, d:TF], tmp[:, 0:TF - d], ALU.add)
                inst.then_inc(vx, 1)
                if mi < 2 and pc % 2 == 1:
                    for pc2b in range(2):
                        for tg in range(NTG):
                            key = (i, pc2b, tg)
                            v.wait_ge(ax, A_rb[key])
                            kidx = V_m[key]
                            aidx = applies.index(key)
                            if aidx >= 4:
                                v.wait_ge(sg, ST[applies[aidx - 4]])
                            v.tensor_tensor(nrm[:, (kidx % 4) * 512:(kidx % 4) * 512 + 512],
                                            cnv[:, pc2b * TF + tg * 512:pc2b * TF + tg * 512 + 512],
                                            rbt[:, (tg % 2) * 512:(tg % 2) * 512 + 512],
                                            ALU.mult).then_inc(vx, 1)

        @blk.tensor
        def _(t_):
            for i in range(NROW):
                mi, pc = divmod(i, NPC)
                if mi < 2 and pc % 2 == 1:
                    rr = (rn2, rk2)[mi]
                    hh = (pc - 1) // 2
                    rsl = slice(32 * hh, 32 * hh + 1)
                    sskeys = sorted(M_ss, key=M_ss.get)
                    for tg in range(NTG):
                        t_.wait_ge(ax, A_sq[i])
                        idx = sskeys.index((i, tg))
                        if idx >= 1:
                            t_.wait_ge(ax, A_rn[sskeys[idx - 1]])
                        for pc2 in range(2):
                            inst = t_.matmul(ps_ss[0:1, :], c["onesc"][:, :],
                                             sq2[:, pc2 * TF + tg * 512:pc2 * TF + tg * 512 + 512],
                                             start=(pc2 == 0), stop=(pc2 == 1))
                        inst.then_inc(mm, 1)
                    for pc2b in range(2):
                        for tg in range(NTG):
                            t_.wait_ge(ax, A_rn[(i, NTG - 1)])
                            key = (pc2b, tg)
                            if key != (0, 0):
                                prev = (pc2b, tg - 1) if tg else (pc2b - 1, NTG - 1)
                                t_.wait_ge(ax, A_rb[(i,) + prev])
                            else:
                                prevs = [kk for kk in A_rb if kk[0] < i]
                                if prevs:
                                    t_.wait_ge(ax, A_rb[max(prevs, key=lambda kk: A_rb[kk])])
                            sl = slice(tg * 512, tg * 512 + 512)
                            t_.matmul(ps_rb[:, :], c["ones65"][rsl, 0:128], rr[rsl, sl],
                                      start=True, stop=True).then_inc(mm, 1)
                elif mi == 2:
                    for tg in range(NTG):
                        t_.wait_ge(ax, A_sq[i])
                        if tg >= 1:
                            t_.wait_ge(ax, A_tt[(i, tg - 1)])
                        elif i >= 13:
                            t_.wait_ge(ax, A_tt[(i - 1, NTG - 1)])
                        for tc4 in range(4):
                            inst = t_.transpose(ps_tt[:, tc4 * 128:tc4 * 128 + 128],
                                                vrow[:, tg * 512 + tc4 * 128:tg * 512 + tc4 * 128 + 128],
                                                c["ident"][:, :])
                        inst.then_inc(mm, 1)

    p.cnt["xl"] = s0["xl"] + 16 * NROW
    p.cnt["wl"] = s0["wl"] + 16 * 6
    p.cnt["mm"] = mm_end
    p.cnt["ax"] = ax_end
    p.cnt["vx"] = vx_end
    p.cnt["gx"] = s0["gx"] + 1
    p.cnt["sg"] = st_end
    p.cnt["mmf"] = ssc[0]
    p.cnt["vf"] = s0["vf"] + 1
    l2.close()
    l12.close()
    if 3 not in phases:
        ls.close()
        return
    ktok = _drt(nc, f"ktok{p.uid()}", [TF, OW], BF16)
    l2t = contextlib.ExitStack()
    kt_in = p.sbuf("dn_ktin", [128, 2 * 512], BF16, l2t)
    kt_out = p.sbuf("dn_ktout", [128, 2 * 512], BF16, l2t)
    ps_kt = [p.psum("dn_pskt", [128, 512], BF16, scope=l2t) for _ in range(2)]
    sgs = p.sem("sg")
    xlh2, xlo2 = p.sem("xh"), p.sem("xo")
    s0t = {k: p.cnt[k] for k in ("xh", "mm", "ax", "sg", "xo")}
    xsem2, xkey2 = (xlh2, xlo2), ("xh", "xo")
    NTIL = NPC * NTG
    with nc.Block(f"dn2t_{p.uid()}") as blk:
        @blk.sync
        def _(sy):
            for i in range(NTIL):
                pc, tg = divmod(i, NTG)
                if i >= 2:
                    sy.wait_ge(mm, s0t["mm"] + i - 1)
                sy.dma_start(out=kt_in[:, (i % 2) * 512:(i % 2) * 512 + 512],
                             in_=io["khat"][pc, :, tg * 512:tg * 512 + 512]
                             ).then_inc(xsem2[i % 2], 16)

        @blk.tensor
        def _(t_):
            for i in range(NTIL):
                t_.wait_ge(xsem2[i % 2], s0t[xkey2[i % 2]] + 16 * (i // 2 + 1))
                if i >= 2:
                    t_.wait_ge(ax, s0t["ax"] + i - 1)
                for tc4 in range(4):
                    inst = t_.transpose(ps_kt[i % 2][:, tc4 * 128:tc4 * 128 + 128],
                                        kt_in[:, (i % 2) * 512 + tc4 * 128:(i % 2) * 512 + tc4 * 128 + 128],
                                        c["ident"][:, :])
                inst.then_inc(mm, 1)

        @blk.scalar
        def _(s_):
            for i in range(NTIL):
                s_.wait_ge(mm, s0t["mm"] + i + 1)
                if i >= 2:
                    s_.wait_ge(sgs, s0t["sg"] + 16 * (i - 1))
                s_.activation(out=kt_out[:, (i % 2) * 512:(i % 2) * 512 + 512],
                              in_=ps_kt[i % 2][:, :], func=AF.Copy).then_inc(ax, 1)

        @blk.gpsimd
        def _(g):
            for i in range(NTIL):
                pc, tg = divmod(i, NTG)
                g.wait_ge(ax, s0t["ax"] + i + 1)
                g.dma_start(out=ktok[tg * 512:(tg + 1) * 512, pc * 128:(pc + 1) * 128]
                            .rearrange("(a p) o -> p a o", p=128),
                            in_=kt_out[:, (i % 2) * 512:(i % 2) * 512 + 512]
                            .rearrange("p (a o) -> p a o", o=128)).then_inc(sgs, 16)
            g.wait_ge(sgs, s0t["sg"] + 16 * NTIL)
    p.cnt["xh"] = s0t["xh"] + 16 * (NTIL - NTIL // 2)
    p.cnt["xo"] = s0t["xo"] + 16 * (NTIL // 2)
    p.cnt["mm"] = s0t["mm"] + NTIL
    p.cnt["ax"] = s0t["ax"] + NTIL
    p.cnt["sg"] = s0t["sg"] + 16 * NTIL
    l2t.close()
    l3 = contextlib.ExitStack()
    identf = p.sbuf("dn_identf", [128, 128], F32, l3)
    qch = p.sbuf("dn_qch", [128, 2 * NPC * 64], BF16, l3)
    kch = p.sbuf("dn_kch", [128, 2 * NPC * 64], BF16, l3)
    vch = p.sbuf("dn_vch", [64, 2 * OW], BF16, l3)
    ktk = p.sbuf("dn_ktk", [64, 2 * OW], BF16, l3)
    tcol = p.sbuf("dn_tcol", [64, 2 * 16], F32, l3)
    e_sb = p.sbuf("dn_e", [64, 3 * 64], F32, l3)
    e2_sb = p.sbuf("dn_e2", [64, 3 * 64], F32, l3)
    d1_sb = p.sbuf("dn_d1", [64, 3 * 64], F32, l3)
    w1_sb = p.sbuf("dn_w1", [64, 3 * 64], F32, l3)
    mb_sb = p.sbuf("dn_mb", [64, 3 * 64], F32, l3)
    n_sb = p.sbuf("dn_n", [64, 3 * 64], BF16, l3)
    q1_sb = p.sbuf("dn_q1", [64, 3 * 64], BF16, l3)
    gt_sb = p.sbuf("dn_gt", [64, 3 * 64], BF16, l3)
    g_sb = p.sbuf("dn_g", [64, 3 * 64], BF16, l3)
    pq_sb = p.sbuf("dn_pq", [64, 3 * 4 * 64], BF16, l3)
    ep_t = p.sbuf("dn_ept", [128, 3 * 64], BF16, l3)
    bep_t = p.sbuf("dn_bept", [128, 3 * 64], BF16, l3)
    ala_c = p.sbuf("dn_ala", [128, 3], F32, l3)
    kp_sb = p.sbuf("dn_kp", [128, 3 * 128], BF16, l3)
    qp_sb = p.sbuf("dn_qp", [128, 3 * 128], BF16, l3)
    kbt_sb = p.sbuf("dn_kbt", [64, 3 * 256], BF16, l3)
    b_sb = p.sbuf("dn_b", [64, 3 * 256], BF16, l3)
    u_sb = p.sbuf("dn_u", [64, 3 * 256], BF16, l3)
    S_f = p.sbuf("dn_S", [128, 3 * 2 * 256], F32, l3)
    S_b = p.sbuf("dn_Sb", [128, 3 * 2 * 256], BF16, l3)
    obuf = p.sbuf("dn_obuf", [128, NPC * TF], BF16, l3)
    l3p = contextlib.ExitStack()
    P1 = p.psum("dn_P1", [64, 384], scope=l3p)
    P2 = p.psum("dn_P2", [64, 208], scope=l3p)
    P3 = p.psum("dn_P3", [64, 384], scope=l3p)
    P4 = p.psum("dn_P4", [64, 512], scope=l3p)
    P5 = p.psum("dn_P5", [64, 256], scope=l3p)
    P6 = p.psum("dn_P6", [128, 512], scope=l3p)
    P8 = p.psum("dn_P8", [128, 512], scope=l3p)
    P7 = p.psum("dn_P7", [128, 512], scope=l3p)

    class _Sched:
        def __init__(self, p):
            self.p = p
            self.names = {"t": "mm", "s": "ax", "v": "vx", "g": "xl", "y": "st"}
            self.sems = {e: p.sem(nm) for e, nm in self.names.items()}
            self.cnt = {e: p.cnt[nm] for e, nm in self.names.items()}
            self.events = {e: [] for e in self.names}

        extra = None
        total = 0
        nops = None

        def ev(self, eng, deps, fn, inc=None):
            if DN3_EVLIM is not None and self.total >= DN3_EVLIM:
                fn = self.nops[eng]
            _Sched.total = self.total + 1
            if inc is None:
                inc = 16 if eng in "gy" else 1
            self.cnt[eng] += inc
            h = (eng, self.cnt[eng])
            waits = {}
            for d in list(deps) + [self.extra]:
                if d is None:
                    continue
                de, dc = d
                if de != eng:
                    waits[de] = max(waits.get(de, 0), dc)
            self.events[eng].append((waits, fn, inc))
            return h

        def body(self, eng, proxy):
            for waits, fn, inc in self.events[eng]:
                for de, dc in waits.items():
                    proxy.wait_ge(self.sems[de], dc)
                fn(proxy).then_inc(self.sems[eng], inc)

        def finalize(self):
            for e, nm in self.names.items():
                self.p.cnt[nm] = self.cnt[e]

    sch = _Sched(p)
    sch.nops = {
        "t": lambda t_: t_.matmul(P3[:, 0:64], pq_sb[:, 0:64], pq_sb[:, 0:64],
                                  start=True, stop=True),
        "s": lambda s_: s_.activation(out=tcol[:, 31:32], in_=identf[0:64, 0:1], func=AF.Copy),
        "v": lambda v: v.memset(tcol[:, 31:32], 0.0),
        "g": lambda g: g.dma_start(out=tcol[:, 31:32], in_=identf[0:64, 0:1]),
        "y": lambda y: y.dma_start(out=tcol[:, 31:32], in_=identf[0:64, 0:1]),
    }
    NCHK = TF // L_C
    NHD = 3
    if DN3_LIMIT is not None:
        NCHK, NHD = DN3_LIMIT
    HS = list(range(NHD))
    HAB = [hh for hh in HS if hh < 2]
    OBK = lambda h: (P6 if h != 1 else P8)
    L = {}

    e_idf = sch.ev("v", [], lambda v: v.tensor_copy(identf[:, :], c["ident"][:, :]))
    e_S0 = sch.ev("v", [], lambda v: v.memset(S_f[:, :], 0.0))
    e_S0b = sch.ev("v", [e_S0], lambda v: v.memset(S_b[:, :], 0.0))

    def chsl(sb, nb, oc, w=64):
        return sb[:, (nb * NPC + oc) * 64:(nb * NPC + oc) * 64 + w]

    def hsl(sb, h, w=64):
        return sb[:, h * w:h * w + w]

    for n in range(NCHK):
        nb = n % 2
        c0 = n * L_C
        tb = nb * 16
        sch.extra = L.get(("s_c", n - 1)) if DN3_SERIAL else None
        L[("ldq", n)] = sch.ev("g", [L.get(("mq", n - 2)), L.get(("kqk", n - 2))],
            lambda g, n=n, nb=nb: g.dma_start(
                out=qch[:, nb * NPC * 64:(nb + 1) * NPC * 64].rearrange("p (o t) -> p o t", o=NPC),
                in_=io["qhat"][:, :, n * L_C:(n + 1) * L_C].rearrange("o p t -> p o t")))
        L[("ldk", n)] = sch.ev("g", [L.get(("mq", n - 2)), L.get(("kqk", n - 2))],
            lambda g, n=n, nb=nb: g.dma_start(
                out=kch[:, nb * NPC * 64:(nb + 1) * NPC * 64].rearrange("p (o t) -> p o t", o=NPC),
                in_=io["khat"][:, :, n * L_C:(n + 1) * L_C].rearrange("o p t -> p o t")))
        L[("ldv", n)] = sch.ev("g", [L.get(("b", n - 2))],
            lambda g, n=n, nb=nb: g.dma_start(out=vch[0:64, nb * OW:(nb + 1) * OW],
                                              in_=io["vtok"][n * L_C:(n + 1) * L_C, :]))
        L[("ldkt", n)] = sch.ev("g", [L.get(("kqk", n - 2))],
            lambda g, n=n, nb=nb: g.dma_start(out=ktk[0:64, nb * OW:(nb + 1) * OW],
                                              in_=ktok[n * L_C:(n + 1) * L_C, :]))
        e_tcT = sch.ev("t", [e_idf, L.get(("gt", n - 1))],
                       lambda t_, c0=c0: t_.transpose(P2[:, 192:207], rows15[0:15, c0:c0 + 64],
                                                      identf[0:15, 0:15]))
        e_tc = sch.ev("s", [e_tcT, L.get(("q1", n - 2)), L.get(("kqk", n - 2)), L.get(("b", n - 2))],
                      lambda s_, tb=tb: s_.activation(out=tcol[:, tb:tb + 15], in_=P2[:, 192:207],
                                                      func=AF.Copy))
        L[("tc", n)] = e_tc
        def f_gb(t_, c0=c0):
            for h in HS:
                t_.matmul(P1[:, h * 64:h * 64 + 64], selhot[:, h * 128:h * 128 + 64],
                          rows15[:, c0:c0 + 64], start=True, stop=True)
            for h in HS:
                inst = t_.matmul(P1[:, 192 + h * 64:192 + h * 64 + 64],
                                 selhot[:, (3 + h) * 128:(3 + h) * 128 + 64],
                                 rows15[:, c0:c0 + 64], start=True, stop=True)
            return inst
        e_gb = sch.ev("t", [L.get(("d1", n - 1)), L.get(("mbw", n - 1)), e_tc], f_gb)
        def f_al(t_, c0=c0):
            for h in HS:
                t_.matmul(P7[:, h * 160:h * 160 + 64], selhot[:, (6 + h) * 128:(6 + h) * 128 + 128],
                          rows15[:, c0:c0 + 64], start=True, stop=True)
                t_.matmul(P7[:, h * 160 + 64:h * 160 + 128], selhot[:, (9 + h) * 128:(9 + h) * 128 + 128],
                          rows15[:, c0:c0 + 64], start=True, stop=True)
                inst = t_.matmul(P7[:, h * 160 + 128:h * 160 + 129], selhot[:, h * 128:h * 128 + 128],
                                 rows15[:, c0 + 63:c0 + 64], start=True, stop=True)
            return inst
        e_al = sch.ev("t", [L.get(("alc", n - 1))], f_al)

        def f_alc(s_):
            for h in HS:
                s_.activation(out=hsl(ep_t, h), in_=P7[:, h * 160:h * 160 + 64], func=AF.Copy)
                s_.activation(out=hsl(bep_t, h), in_=P7[:, h * 160 + 64:h * 160 + 128], func=AF.Copy)
                inst = s_.activation(out=ala_c[:, h:h + 1], in_=P7[:, h * 160 + 128:h * 160 + 129],
                                     func=AF.Exp)
            return inst
        e_alc = sch.ev("s", [e_al, L.get(("kqk", n - 1)), L.get(("s_c", n - 1))], f_alc)
        L[("alc", n)] = e_alc
        e_d1 = sch.ev("v", [e_gb, e_tc], lambda v, tb=tb: [
            v.tensor_scalar(hsl(d1_sb, h), P1[:, h * 64:h * 64 + 64],
                            tcol[:, tb + h:tb + h + 1], None, ALU.subtract, ALU.bypass)
            for h in HS][-1])
        L[("d1", n)] = e_d1
        e_e0 = sch.ev("v", [e_d1, L.get(("eE", n - 1))], lambda v: [
            v.tensor_scalar(hsl(w1_sb, h), hsl(d1_sb, h), 0.0, None, ALU.min, ALU.bypass)
            for h in HS][-1])
        e_eE = sch.ev("s", [e_e0, L.get(("gt", n - 1))], lambda s_: [
            s_.activation(out=hsl(e_sb, h), in_=hsl(w1_sb, h), func=AF.Exp) for h in HS][-1])
        L[("eE", n)] = e_eE
        e_e2p = sch.ev("v", [e_d1, L.get(("eE2", n - 1))], lambda v: [
            v.tensor_scalar(hsl(mb_sb, h), hsl(d1_sb, h), -1.0, 0.0, ALU.mult, ALU.min)
            for h in HS][-1])
        e_eE2 = sch.ev("s", [e_e2p, L.get(("q1", n - 1))], lambda s_: [
            s_.activation(out=hsl(e2_sb, h), in_=hsl(mb_sb, h), func=AF.Exp) for h in HS][-1])
        L[("eE2", n)] = e_eE2
        def f_mt(t_, nb=nb):
            for h in HS:
                t_.matmul(P2[:, h * 64:h * 64 + 64], chsl(kch, nb, h * 2), chsl(kch, nb, h * 2),
                          start=True, stop=False)
                inst = t_.matmul(P2[:, h * 64:h * 64 + 64], chsl(kch, nb, h * 2 + 1),
                                 chsl(kch, nb, h * 2 + 1), start=False, stop=True)
            return inst
        e_mt = sch.ev("t", [L[("ldk", n)], e_tc], f_mt)
        L[("mt", n)] = e_mt
        def f_mbw(v):
            for h in HS:
                v.tensor_tensor(hsl(w1_sb, h), P2[:, h * 64:h * 64 + 64], hsl(e_sb, h), ALU.mult)
            for h in HS:
                v.tensor_tensor(hsl(mb_sb, h), P1[:, 192 + h * 64:192 + h * 64 + 64],
                                trim[:, 0:64], ALU.mult)
            for h in HS:
                v.tensor_tensor(hsl(d1_sb, h), hsl(w1_sb, h), hsl(mb_sb, h), ALU.mult)
            for h in HS:
                inst = v.tensor_scalar(hsl(n_sb, h), hsl(d1_sb, h), -1.0, None, ALU.mult, ALU.bypass)
            return inst
        e_mbw = sch.ev("v", [e_mt, e_eE, e_gb, e_eE2], f_mbw)
        L[("mbw", n)] = e_mbw
        def f_q1(v, tb=tb):
            for h in HS:
                v.tensor_tensor(hsl(w1_sb, h), P2[:, h * 64:h * 64 + 64], hsl(e2_sb, h), ALU.mult)
            for h in HS:
                v.tensor_tensor(hsl(w1_sb, h), hsl(w1_sb, h), trim[:, 128:192], ALU.mult)
            for h in HS:
                inst = v.tensor_scalar(hsl(q1_sb, h), hsl(w1_sb, h),
                                       tcol[:, tb + 3 + h:tb + 4 + h], -1.0, ALU.mult, ALU.mult)
            return inst
        e_q1 = sch.ev("v", [e_mbw, e_eE2, e_tc], f_q1)
        L[("q1", n)] = e_q1

        def f_mq(t_, nb=nb):
            for h in HS:
                t_.matmul(P2[:, h * 64:h * 64 + 64], chsl(kch, nb, h * 2),
                          chsl(qch, nb, h * 2), start=True, stop=False)
                inst = t_.matmul(P2[:, h * 64:h * 64 + 64], chsl(kch, nb, h * 2 + 1),
                                 chsl(qch, nb, h * 2 + 1), start=False, stop=True)
            return inst
        e_mq = sch.ev("t", [L[("ldq", n)], L[("ldk", n)], e_q1], f_mq)
        L[("mq", n)] = e_mq

        def f_gt(v):
            for h in HS:
                v.tensor_tensor(hsl(w1_sb, h), P2[:, h * 64:h * 64 + 64],
                                hsl(e_sb, h), ALU.mult)
            for h in HS:
                inst = v.tensor_tensor(hsl(gt_sb, h), hsl(w1_sb, h), trim[:, 64:128], ALU.mult)
            return inst
        e_gt = sch.ev("v", [e_mq, e_eE, L.get(("o_c", n - 1))], f_gt)
        L[("gt", n)] = e_gt
        def f_g0(v):
            for h in HS:
                v.tensor_tensor(hsl(g_sb, h), hsl(n_sb, h), c["ident"][0:64, 0:64], ALU.add)
            for h in HS:
                v.tensor_copy(pq_sb[:, h * 256:h * 256 + 64], hsl(n_sb, h))
            for h in HS:
                inst = v.tensor_copy(pq_sb[:, h * 256 + 64:h * 256 + 128], hsl(q1_sb, h))
            return inst
        e_g0 = sch.ev("v", [e_mbw, e_q1, L.get(("u", n - 1)), L.get(("m1l", n - 1))], f_g0)
        ev_nev4p = L.get(("nev4", n - 1))
        ev_eg4p = L.get(("eg4", n - 1))
        e_evac = e_eg = None
        for it in range(5):
            pb = (it % 2) * 128
            qb = pb + 64
            pn = ((it + 1) % 2) * 128
            qn = pn + 64

            def f_m23(t_, pb=pb, qb=qb):
                for h in HS:
                    t_.matmul(P3[:, h * 64:h * 64 + 64], pq_sb[:, h * 256 + qb:h * 256 + qb + 64],
                              pq_sb[:, h * 256 + pb:h * 256 + pb + 64], start=True, stop=True)
                for h in HS:
                    inst = t_.matmul(P3[:, 192 + h * 64:192 + h * 64 + 64],
                                     pq_sb[:, h * 256 + pb:h * 256 + pb + 64],
                                     pq_sb[:, h * 256 + qb:h * 256 + qb + 64], start=True, stop=True)
                return inst
            e_m23 = sch.ev("t", [e_g0, ev_eg4p, ev_nev4p] if it == 0 else [e_evac, e_eg], f_m23)

            def f_evac(s_, pn=pn, qn=qn):
                for h in HS:
                    s_.activation(out=pq_sb[:, h * 256 + pn:h * 256 + pn + 64],
                                  in_=P3[:, h * 64:h * 64 + 64], func=AF.Copy)
                    inst = s_.activation(out=pq_sb[:, h * 256 + qn:h * 256 + qn + 64],
                                         in_=P3[:, 192 + h * 64:192 + h * 64 + 64], func=AF.Copy)
                return inst
            e_evac = sch.ev("s", [e_m23], f_evac)

            def f_m1(t_, qn=qn):
                for h in HS:
                    inst = t_.matmul(P3[:, h * 64:h * 64 + 64],
                                     pq_sb[:, h * 256 + qn:h * 256 + qn + 64],
                                     hsl(g_sb, h), start=True, stop=True)
                return inst
            e_m1 = sch.ev("t", [e_evac], f_m1)

            def f_eg(v):
                for h in HS:
                    inst = v.tensor_tensor(hsl(g_sb, h), hsl(g_sb, h),
                                           P3[:, h * 64:h * 64 + 64], ALU.add)
                return inst
            e_eg = sch.ev("v", [e_m1], f_eg)
        L[("nev4", n)] = e_evac
        L[("eg4", n)] = e_eg
        L[("m1l", n)] = e_m1
        def f_kqk(v, nb=nb, tb=tb):
            for h in HS:
                v.tensor_tensor(kp_sb[:, h * 128:h * 128 + 64], chsl(kch, nb, h * 2),
                                hsl(bep_t, h), ALU.mult)
                v.tensor_tensor(kp_sb[:, h * 128 + 64:h * 128 + 128], chsl(kch, nb, h * 2 + 1),
                                hsl(bep_t, h), ALU.mult)
                v.tensor_tensor(qp_sb[:, h * 128:h * 128 + 64], chsl(qch, nb, h * 2),
                                hsl(ep_t, h), ALU.mult)
                v.tensor_tensor(qp_sb[:, h * 128 + 64:h * 128 + 128], chsl(qch, nb, h * 2 + 1),
                                hsl(ep_t, h), ALU.mult)
                inst = v.tensor_scalar(kbt_sb[0:64, h * 256:h * 256 + 256],
                                       ktk[0:64, nb * OW + h * 256:nb * OW + h * 256 + 256],
                                       tcol[:, tb + 12 + h:tb + 13 + h], None, ALU.mult, ALU.bypass)
            return inst
        e_kqk = sch.ev("v", [e_alc, L[("ldk", n)], L[("ldq", n)], L[("ldkt", n)], e_tc,
                             L.get(("sc_c", n - 1))], f_kqk)
        L[("kqk", n)] = e_kqk

        def PUsl(h):
            return P4[:, (h % 2) * 256:(h % 2) * 256 + 256] if h < 2 else P5[:, :]

        def f_p2(t_):
            for h in HS:
                t_.matmul(PUsl(h), kp_sb[:, h * 128:h * 128 + 64],
                          S_b[:, (h * 2 + 0) * 256:(h * 2 + 0) * 256 + 256], start=True, stop=False)
                inst = t_.matmul(PUsl(h), kp_sb[:, h * 128 + 64:h * 128 + 128],
                                 S_b[:, (h * 2 + 1) * 256:(h * 2 + 1) * 256 + 256],
                                 start=False, stop=True)
            return inst
        e_p2 = sch.ev("t", [e_kqk, L.get(("ue", n - 1)), L.get(("s_c", n - 1), e_S0b)], f_p2)
        L[("p2", n)] = e_p2

        def f_b(v, nb=nb, tb=tb):
            for h in HS:
                inst = v.scalar_tensor_tensor(b_sb[0:64, h * 256:h * 256 + 256],
                                              vch[0:64, nb * OW + h * 256:nb * OW + h * 256 + 256],
                                              tcol[:, tb + 3 + h:tb + 4 + h], PUsl(h),
                                              ALU.mult, ALU.subtract)
            return inst
        e_b = sch.ev("v", [e_p2, L[("ldv", n)], e_tc, L.get(("u", n - 1))], f_b)
        L[("b", n)] = e_b

        def f_u(t_):
            for h in HS:
                inst = t_.matmul(PUsl(h), hsl(g_sb, h), b_sb[0:64, h * 256:h * 256 + 256],
                                 start=True, stop=True)
            return inst
        e_u = sch.ev("t", [L[("eg4", n)], e_b], f_u)
        L[("u", n)] = e_u

        def f_ue(s_):
            for h in HS:
                inst = s_.activation(out=u_sb[0:64, h * 256:h * 256 + 256], in_=PUsl(h), func=AF.Copy)
            return inst
        e_ue = sch.ev("s", [e_u, L.get(("sc_c", n - 1))], f_ue)
        L[("ue", n)] = e_ue
        def f_o(t_, hs):
            for h in hs:
                for mv in range(2):
                    ob = OBK(h)[:, mv * 256:mv * 256 + 64]
                    t_.matmul(ob, S_b[:, (h * 2 + 0) * 256 + mv * 128:(h * 2 + 0) * 256 + mv * 128 + 128],
                              qp_sb[:, h * 128:h * 128 + 64], start=True, stop=False)
                    t_.matmul(ob, S_b[:, (h * 2 + 1) * 256 + mv * 128:(h * 2 + 1) * 256 + mv * 128 + 128],
                              qp_sb[:, h * 128 + 64:h * 128 + 128], start=False, stop=False)
                    inst = t_.matmul(ob, u_sb[0:64, h * 256 + mv * 128:h * 256 + mv * 128 + 128],
                                     hsl(gt_sb, h), start=False, stop=True)
            return inst

        def f_oev(s_, hs, n=n):
            for h in hs:
                for mv in range(2):
                    inst = s_.activation(
                        out=obuf[:, (h * 2 + mv) * TF + n * 64:(h * 2 + mv) * TF + n * 64 + 64],
                        in_=OBK(h)[:, mv * 256:mv * 256 + 64], func=AF.Copy, scale=1.0 / 16)
            return inst

        def f_sc(t_, hs):
            for h in hs:
                t_.matmul(OBK(h)[:, 0:256], kbt_sb[0:64, h * 256:h * 256 + 128],
                          u_sb[0:64, h * 256:h * 256 + 256], start=True, stop=True)
                inst = t_.matmul(OBK(h)[:, 256:512], kbt_sb[0:64, h * 256 + 128:h * 256 + 256],
                                 u_sb[0:64, h * 256:h * 256 + 256], start=True, stop=True)
            return inst

        def f_s(v, hs):
            for h in hs:
                sf0 = S_f[:, (h * 2 + 0) * 256:(h * 2 + 0) * 256 + 256]
                sf1 = S_f[:, (h * 2 + 1) * 256:(h * 2 + 1) * 256 + 256]
                v.scalar_tensor_tensor(sf0, sf0, ala_c[:, h:h + 1], OBK(h)[:, 0:256],
                                       ALU.mult, ALU.add)
                v.scalar_tensor_tensor(sf1, sf1, ala_c[:, h:h + 1], OBK(h)[:, 256:512],
                                       ALU.mult, ALU.add)
                v.tensor_copy(S_b[:, (h * 2 + 0) * 256:(h * 2 + 0) * 256 + 256], sf0)
                inst = v.tensor_copy(S_b[:, (h * 2 + 1) * 256:(h * 2 + 1) * 256 + 256], sf1)
            return inst
        e_oab = sch.ev("t", [e_kqk, e_ue, e_gt, L.get(("s_c", n - 1), e_S0b)],
                       lambda t_, f=f_o: f(t_, HAB))
        e_oevab = sch.ev("s", [e_oab], lambda s_, f=f_oev: f(s_, HAB))
        e_scab = sch.ev("t", [e_oevab, e_kqk, e_ue], lambda t_, f=f_sc: f(t_, HAB))
        e_sab = sch.ev("v", [e_scab, e_alc], lambda v, f=f_s: f(v, HAB))
        L[("s_ab", n)] = e_sab
        if NHD > 2:
            e_oc = sch.ev("t", [e_sab, e_kqk, e_ue, e_gt], lambda t_, f=f_o: f(t_, [2]))
            e_oevc = sch.ev("s", [e_oc], lambda s_, f=f_oev: f(s_, [2]))
            e_scc = sch.ev("t", [e_oevc, e_kqk, e_ue], lambda t_, f=f_sc: f(t_, [2]))
            e_sc = sch.ev("v", [e_scc, e_alc], lambda v, f=f_s: f(v, [2]))
        else:
            e_oc, e_oevc, e_scc, e_sc = e_oab, e_oevab, e_scab, e_sab
        L[("o_c", n)] = e_oc
        L[("sc_c", n)] = e_scc
        L[("s_c", n)] = e_sc

    with nc.Block(f"dn3_{p.uid()}") as blk:
        @blk.tensor
        def _(t_):
            sch.body("t", t_)

        @blk.scalar
        def _(s_):
            sch.body("s", s_)

        @blk.vector
        def _(v):
            sch.body("v", v)

        @blk.gpsimd
        def _(g):
            sch.body("g", g)
    sch.finalize()
    l3p.close()
    if "odbg" in io:
        st2 = p.sem("st")
        s0d = p.cnt["st"]
        with nc.Block(f"odbg_{p.uid()}") as blk:
            @blk.sync
            def _(sy):
                sy.dma_start(out=io["odbg"].rearrange("o p t -> p o t"),
                             in_=obuf[:].rearrange("p (o t) -> p o t", o=NPC)).then_inc(st2, 16)
                sy.wait_ge(st2, s0d + 16)
        p.cnt["st"] = s0d + 16
    if 4 not in phases:
        l3.close()
        ls.close()
        return
    l4 = contextlib.ExitStack()
    sqo = p.sbuf("dn_sqo", [128, 2 * 512], BF16, l4)
    grow = p.sbuf("dn_grow", [128, TF], BF16, l4)
    rbt2 = p.sbuf("dn_rbt2", [128, 2 * 512], BF16, l4)
    ot_sb = p.sbuf("dn_ot", [128, TF], BF16, l4)
    t4 = p.sbuf("dn_t4", [128, 512], BF16, l4)
    rno = p.sbuf("dn_rno", [1, TF], F32, l4)
    rns = p.sbuf("dn_rns", [1, 512], F32, l4)
    ps_s4 = p.psum("dn_ps4", [1, 512], scope=l4)
    ps_r4 = p.psum("dn_pr4", [128, 512], scope=l4)
    sg = p.sem("sg")
    s0 = {k: p.cnt[k] for k in ("xl", "mm", "ax", "vx", "sg")}
    A_sq4 = {}; M_ss4 = {}; A_rn4 = {}; M_rb4 = {}; A_rb4 = {}; V_m4 = {}
    ax_c = s0["ax"]; mm_c = s0["mm"]; vx_c = s0["vx"]
    for h in range(3):
        for tg in range(NTG):
            ax_c += 1; A_sq4[(h, tg)] = ax_c
            mm_c += 1; M_ss4[(h, tg)] = mm_c
            ax_c += 1; A_rn4[(h, tg)] = ax_c
        for oc2 in range(2):
            for tg in range(NTG):
                mm_c += 1; M_rb4[(h, oc2, tg)] = mm_c
                ax_c += 1; A_rb4[(h, oc2, tg)] = ax_c
                vx_c += 1; V_m4[(h, oc2, tg)] = vx_c
    kof = lambda h, oc2, tg: (h * 2 + oc2) * TF + tg * 512

    with nc.Block(f"dn4_{p.uid()}") as blk:
        @blk.gpsimd
        def _(g):
            for h in range(3):
                for oc2 in range(2):
                    oc = h * 2 + oc2
                    if oc >= 1:
                        g.wait_ge(vx, V_m4[(h, oc2 - 1, NTG - 1)] if oc2 else V_m4[(h - 1, 1, NTG - 1)])
                    g.dma_start(out=grow[:, :], in_=io["gsp"][oc, :, :]).then_inc(xl, 16)
                    g.wait_ge(vx, V_m4[(h, oc2, NTG - 1)])
                    g.dma_start(out=io["ot"][oc, :, :], in_=ot_sb[:, :]).then_inc(sg, 16)
            g.wait_ge(sg, s0["sg"] + 16 * 6)

        @blk.scalar
        def _(s_):
            for h in range(3):
                for tg in range(NTG):
                    if (h, tg) != (0, 0):
                        s_.wait_ge(mm, M_ss4[(h, tg - 1)] if tg else M_ss4[(h - 1, NTG - 1)])
                    s_.activation(out=sqo[:, 0:512], in_=obuf[:, kof(h, 0, tg):kof(h, 0, tg) + 512],
                                  func=AF.Square)
                    s_.activation(out=sqo[:, 512:1024], in_=obuf[:, kof(h, 1, tg):kof(h, 1, tg) + 512],
                                  func=AF.Square).then_inc(ax, 1)
                    s_.wait_ge(mm, M_ss4[(h, tg)])
                    sl = slice(tg * 512, tg * 512 + 512)
                    s_.activation(out=rns[0:1, :], in_=ps_s4[0:1, :], func=AF.Ln,
                                  scale=1.0 / 256, bias=1e-5)
                    s_.activation(out=rno[0:1, sl], in_=rns[0:1, :], func=AF.Exp,
                                  scale=-0.5).then_inc(ax, 1)
                for oc2 in range(2):
                    for tg in range(NTG):
                        s_.wait_ge(mm, M_rb4[(h, oc2, tg)])
                        k2 = (oc2, tg)
                        if k2 > (0, 1):
                            prev2 = (oc2, tg - 2) if tg >= 2 else (oc2 - 1, NTG - 2 + tg)
                            s_.wait_ge(vx, V_m4[(h,) + prev2])
                        elif h > 0 and k2 <= (0, 1):
                            s_.wait_ge(vx, V_m4[(h - 1, 1, NTG - 2 + tg)])
                        s_.activation(out=rbt2[:, (tg % 2) * 512:(tg % 2) * 512 + 512],
                                      in_=ps_r4[:, :], func=AF.Copy).then_inc(ax, 1)

        @blk.tensor
        def _(t_):
            for h in range(3):
                for tg in range(NTG):
                    t_.wait_ge(ax, A_sq4[(h, tg)])
                    if (h, tg) != (0, 0):
                        t_.wait_ge(ax, A_rn4[(h, tg - 1)] if tg else A_rn4[(h - 1, NTG - 1)])
                    t_.matmul(ps_s4[0:1, :], c["onesc"][:, :], sqo[:, 0:512], start=True, stop=False)
                    inst = t_.matmul(ps_s4[0:1, :], c["onesc"][:, :], sqo[:, 512:1024],
                                     start=False, stop=True)
                    inst.then_inc(mm, 1)
                for oc2 in range(2):
                    for tg in range(NTG):
                        t_.wait_ge(ax, A_rn4[(h, NTG - 1)])
                        k2 = (oc2, tg)
                        if k2 != (0, 0):
                            prev = (oc2, tg - 1) if tg else (oc2 - 1, NTG - 1)
                            t_.wait_ge(ax, A_rb4[(h,) + prev])
                        elif h > 0:
                            t_.wait_ge(ax, A_rb4[(h - 1, 1, NTG - 1)])
                        sl = slice(tg * 512, tg * 512 + 512)
                        t_.matmul(ps_r4[:, :], c["ones1f"][0:1, 0:128], rno[0:1, sl],
                                  start=True, stop=True).then_inc(mm, 1)

        @blk.vector
        def _(v):
            for h in range(3):
                for oc2 in range(2):
                    oc = h * 2 + oc2
                    for tg in range(NTG):
                        v.wait_ge(ax, A_rb4[(h, oc2, tg)])
                        v.wait_ge(xl, s0["xl"] + 16 * (oc + 1))
                        if tg == 0 and oc >= 1:
                            v.wait_ge(sg, s0["sg"] + 16 * oc)
                        v.scalar_tensor_tensor(t4[:, :], obuf[:, kof(h, oc2, tg):kof(h, oc2, tg) + 512],
                                               onw_sb[:, oc2:oc2 + 1],
                                               rbt2[:, (tg % 2) * 512:(tg % 2) * 512 + 512],
                                               ALU.mult, ALU.mult)
                        v.tensor_tensor(ot_sb[:, tg * 512:tg * 512 + 512], t4[:, :],
                                        grow[:, tg * 512:tg * 512 + 512], ALU.mult).then_inc(vx, 1)

    p.cnt["xl"] = s0["xl"] + 16 * 6
    p.cnt["mm"] = mm_c
    p.cnt["ax"] = ax_c
    p.cnt["vx"] = vx_c
    p.cnt["sg"] = s0["sg"] + 16 * 6
    l4.close()
    l3.close()
    ls.close()


_PROGS = {}


def _drt(nc, nm, sh, dt, kind=None):
    t = nc.dram_tensor(nm, sh, dt, kind=kind) if kind else nc.dram_tensor(nm, sh, dt)
    return t.ap()


def build_ghost_build():
    p = Prog()
    nc = p.nc
    io = {
        "gidxT": _drt(nc, "gidxT", [256, GI], I32, "ExternalInput"),
        "uidxT": _drt(nc, "uidxT", [256, GI], I32, "ExternalInput"),
        "didxT": _drt(nc, "didxT", [688, D], I32, "ExternalInput"),
        "lut": _drt(nc, "lut", [16], F32, "ExternalInput"),
        "wg_sh": _drt(nc, "wg_sh", [256, GI], BF16, "ExternalOutput"),
        "wu_sh": _drt(nc, "wu_sh", [256, GI], BF16, "ExternalOutput"),
        "wd_sh": _drt(nc, "wd_sh", [688, D], BF16, "ExternalOutput"),
    }
    c = emit_consts(p)
    emit_ghost_build(p, io)
    p.S.close()
    return p


def build_ffn(I, wdt):
    p = Prog()
    nc = p.nc
    io = {
        "x": _drt(nc, "x", [D, TL], F32, "ExternalInput"),
        "wgT": _drt(nc, "wgT", [D, I], wdt, "ExternalInput"),
        "wuT": _drt(nc, "wuT", [D, I], wdt, "ExternalInput"),
        "wdT": _drt(nc, "wdT", [I, D], wdt, "ExternalInput"),
        "xo": _drt(nc, "xo", [D, TL], F32, "ExternalOutput"),
        "h": _drt(nc, "hspill", [I // 128, 128, TL], BF16),
    }
    c = emit_consts(p)
    emit_copy_dram(p, io["x"], io["xo"])
    xln = p.sbuf("xln", [128, NKD * TL], BF16)
    emit_ln(p, c, io["x"], xln)
    emit_ffn(p, c, xln, dict(wgT=io["wgT"], wuT=io["wuT"], h=io["h"]), I=I, scale=0.5)
    emit_down(p, c, io["h"], io["wdT"], io["xo"], I // 128, 0.5)
    p.S.close()
    return p


def build_lnout():
    p = Prog()
    nc = p.nc
    x = _drt(nc, "x", [D, TL], F32, "ExternalInput")
    xlno = _drt(nc, "xlno", [D, TL], BF16, "ExternalOutput")
    c = emit_consts(p)
    xln = p.sbuf("xln", [128, NKD * TL], BF16)
    emit_ln(p, c, x, xln)
    st = p.sem("st")
    s0 = p.cnt["st"]
    with nc.Block(f"out_{p.uid()}") as blk:
        @blk.sync
        def _(sy):
            sy.dma_start(out=xlno.rearrange("(k p) t -> p k t", p=128),
                         in_=xln[:].rearrange("p (k t) -> p k t", k=NKD)).then_inc(st, 16)
            sy.wait_ge(st, s0 + 16)
    p.cnt["st"] = s0 + 16
    p.S.close()
    return p


def build_retnet():
    p = Prog()
    nc = p.nc
    io = {
        "xg": _drt(nc, "xg", [2, D, TL], BF16, "ExternalInput"),
        "wqT": _drt(nc, "wqT", [D, 1024], F32, "ExternalInput"),
        "wkT": _drt(nc, "wkT", [D, 1024], F32, "ExternalInput"),
        "wvT": _drt(nc, "wvT", [D, 1024], F32, "ExternalInput"),
        "bq": _drt(nc, "bq", [1024], F32, "ExternalInput"),
        "bk": _drt(nc, "bk", [1024], F32, "ExternalInput"),
        "bv": _drt(nc, "bv", [1024], F32, "ExternalInput"),
        "dtT": _drt(nc, "dtT", [4, 256, 256], F32, "ExternalInput"),
        "cst": _drt(nc, "cst", [4, 128, 256], F32, "ExternalInput"),
        "kvt": _drt(nc, "kvt", [4, 128, 256], F32, "ExternalInput"),
        "gmc": _drt(nc, "gmc", [4], F32, "ExternalInput"),
        "o_sh": _drt(nc, "o_sh", [8, 128, TF], BF16, "ExternalOutput"),
    }
    c = emit_consts(p)
    emit_retnet(p, c, io)
    p.S.close()
    return p


def build_resid():
    p = Prog()
    nc = p.nc
    x = _drt(nc, "x", [D, TL], F32, "ExternalInput")
    osel = _drt(nc, "osel", [NKD, 128, TL], BF16, "ExternalInput")
    xo = _drt(nc, "xo", [D, TL], F32, "ExternalOutput")
    c = emit_consts(p)
    emit_copy_dram(p, x, xo)
    emit_resid_add(p, c, osel, xo, scale=1.0)
    p.S.close()
    return p


def build_deltanet():
    p = Prog()
    nc = p.nc
    io = {
        "xg": _drt(nc, "xg", [2, D, TL], BF16, "ExternalInput"),
        "wqT": _drt(nc, "wqT", [D, 768], F32, "ExternalInput"),
        "wkT": _drt(nc, "wkT", [D, 768], F32, "ExternalInput"),
        "wvT": _drt(nc, "wvT", [D, 768], F32, "ExternalInput"),
        "wgT": _drt(nc, "wgT", [D, 768], F32, "ExternalInput"),
        "wbT": _drt(nc, "wbT", [D, 3], F32, "ExternalInput"),
        "waT": _drt(nc, "waT", [D, 3], F32, "ExternalInput"),
        "convw": _drt(nc, "convw", [3, 768, 4], F32, "ExternalInput"),
        "onw": _drt(nc, "onw", [256], F32, "ExternalInput"),
        "alog": _drt(nc, "alog", [3], F32, "ExternalInput"),
        "dtb": _drt(nc, "dtb", [3], F32, "ExternalInput"),
        "trimask": _drt(nc, "trimask", [64, 192], F32, "ExternalInput"),
        "selhot": _drt(nc, "selhot", [15, 15 * 128], F32, "ExternalInput"),
        "qraw": _drt(nc, "qraw", [6, 128, TF], BF16),
        "kraw": _drt(nc, "kraw", [6, 128, TF], BF16),
        "vraw": _drt(nc, "vraw", [6, 128, TF], BF16),
        "gsp": _drt(nc, "gsp", [6, 128, TF], BF16),
        "qhat": _drt(nc, "qhat", [6, 128, TF], BF16),
        "khat": _drt(nc, "khat", [6, 128, TF], BF16),
        "vtok": _drt(nc, "vtok", [TF, 768], BF16),
        "ot": _drt(nc, "ot", [6, 128, TF], BF16, "ExternalOutput"),
    }
    c = emit_consts(p)
    emit_deltanet(p, c, io)
    p.S.close()
    return p


def build_wo_resid():
    p = Prog()
    nc = p.nc
    x = _drt(nc, "x", [D, TL], F32, "ExternalInput")
    osel = _drt(nc, "osel", [12, 128, TL], BF16, "ExternalInput")
    woT = _drt(nc, "woT", [KEY, D], F32, "ExternalInput")
    xo = _drt(nc, "xo", [D, TL], F32, "ExternalOutput")
    c = emit_consts(p)
    emit_copy_dram(p, x, xo)
    emit_down(p, c, osel, woT, xo, NKI=12, scale=1.0)
    p.S.close()
    return p


def _prog(name, fn):
    if name not in _PROGS:
        _PROGS[name] = fn()
    return _PROGS[name]


def _run(p, in_maps, **kw):
    return run_bass_kernel_spmd(p.nc, in_maps, core_ids=list(range(8)), **kw).results


def _retnet_consts():
    g = GAMMAS.astype(np.float64)
    idx = np.arange(R_C)
    ii, jj = idx[:, None], idx[None, :]
    out = {}
    for hg in range(2):
        hs = slice(hg * 4, hg * 4 + 4)
        Dm = np.where(ii >= jj, g[hs][:, None, None] ** (ii - jj), 0.0)
        out[hg] = dict(
            dtT=np.ascontiguousarray(Dm.transpose(0, 2, 1)).astype(np.float32),
            cst=np.broadcast_to((g[hs][:, None] ** (idx + 1))[:, None, :],
                                (4, 128, R_C)).astype(np.float32).copy(),
            kvt=np.broadcast_to((g[hs][:, None] ** (R_C - 1 - idx))[:, None, :],
                                (4, 128, R_C)).astype(np.float32).copy(),
            gmc=(g[hs] ** R_C).astype(np.float32),
        )
    return out


def _delta_consts():
    ii, jj = np.arange(64)[:, None], np.arange(64)[None, :]
    trimask = np.concatenate([(ii < jj), (ii <= jj), (ii > jj)], 1).astype(np.float32)
    selhot = np.zeros((15, 15 * 128), np.float32)
    for r in range(15):
        selhot[r, r * 128:(r + 1) * 128] = 1.0
    return trimask, selhot


def kernel(**inp):
    ac = np.ascontiguousarray
    x = inp["L0_attn_x"]
    xs = [ac(x[c // 2, (c % 2) * TL:(c % 2 + 1) * TL, :].T.astype(np.float32)) for c in range(8)]

    gT = ac(inp["L0_ffn_gate_idx"].T)
    uT = ac(inp["L0_ffn_up_idx"].T)
    dT = ac(inp["L0_ffn_down_idx"].T)
    p = _prog("ghost", build_ghost_build)
    r = _run(p, [dict(gidxT=ac(gT[256 * c:256 * (c + 1)]), uidxT=ac(uT[256 * c:256 * (c + 1)]),
                      didxT=ac(dT[688 * c:688 * (c + 1)]), lut=inp["L0_ffn_lut"])
                 for c in range(8)])
    WgT = np.concatenate([r[c]["wg_sh"] for c in range(8)], 0)
    WuT = np.concatenate([r[c]["wu_sh"] for c in range(8)], 0)
    WdT = np.concatenate([r[c]["wd_sh"] for c in range(8)], 0)

    def ffn_launch(xs, wgT, wuT, wdT, I, wdt):
        p = _prog(f"ffn{I}", lambda: build_ffn(I, wdt))
        r = _run(p, [dict(x=xs[c], wgT=wgT, wuT=wuT, wdT=wdT) for c in range(8)])
        return [r[c]["xo"] for c in range(8)]

    def ln_launch(xs):
        p = _prog("lnout", build_lnout)
        r = _run(p, [dict(x=xs[c]) for c in range(8)])
        return [r[c]["xlno"] for c in range(8)]

    xs = ffn_launch(xs, WgT, WuT, WdT, GI, BF16)

    xln = ln_launch(xs)
    rc = _retnet_consts()
    Wq0, Wk0, Wv0 = inp["L0_attn_Wq"], inp["L0_attn_Wk"], inp["L0_attn_Wv"]
    bq0, bk0, bv0 = inp["L0_attn_bq"], inp["L0_attn_bk"], inp["L0_attn_bv"]
    p = _prog("retnet", build_retnet)
    r = _run(p, [dict(xg=np.stack([xln[(c // 2) * 2], xln[(c // 2) * 2 + 1]]),
                      wqT=ac(Wq0[:, (c % 2) * 1024:(c % 2 + 1) * 1024]),
                      wkT=ac(Wk0[:, (c % 2) * 1024:(c % 2 + 1) * 1024]),
                      wvT=ac(Wv0[:, (c % 2) * 1024:(c % 2 + 1) * 1024]),
                      bq=ac(bq0[(c % 2) * 1024:(c % 2 + 1) * 1024]),
                      bk=ac(bk0[(c % 2) * 1024:(c % 2 + 1) * 1024]),
                      bv=ac(bv0[(c % 2) * 1024:(c % 2 + 1) * 1024]),
                      **rc[c % 2]) for c in range(8)])
    p = _prog("resid", build_resid)
    r2in = []
    for c in range(8):
        pb = (c // 2) * 2
        osel = np.stack([r[pb + dc // 8]["o_sh"][dc % 8][:, (c % 2) * TL:(c % 2 + 1) * TL]
                         for dc in range(NKD)])
        r2in.append(dict(x=xs[c], osel=osel))
    r2 = _run(p, r2in)
    xs = [r2[c]["xo"] for c in range(8)]

    xs = ffn_launch(xs, WgT, WuT, WdT, GI, BF16)

    wgu = inp["L1_ffn_w_gate_up"]
    wgT1 = ac(wgu[:LI].T)
    wuT1 = ac(wgu[LI:].T)
    wdT1 = ac(inp["L1_ffn_w_down"].T)
    xs = ffn_launch(xs, wgT1, wuT1, wdT1, LI, F32)

    xln = ln_launch(xs)
    trimask, selhot = _delta_consts()
    Wq1, Wk1, Wv1, Wg1 = (inp["L1_attn_Wq"], inp["L1_attn_Wk"], inp["L1_attn_Wv"], inp["L1_attn_Wg"])
    Wb1, Wa1 = inp["L1_attn_Wb"], inp["L1_attn_Wa"]
    cq1, ck1, cv1 = inp["L1_attn_conv_q"], inp["L1_attn_conv_k"], inp["L1_attn_conv_v"]
    p = _prog("delta", build_deltanet)
    r = _run(p, [dict(xg=np.stack([xln[(c // 2) * 2], xln[(c // 2) * 2 + 1]]),
                      wqT=ac(Wq1[(c % 2) * 768:(c % 2 + 1) * 768].T),
                      wkT=ac(Wk1[(c % 2) * 768:(c % 2 + 1) * 768].T),
                      wvT=ac(Wv1[(c % 2) * 768:(c % 2 + 1) * 768].T),
                      wgT=ac(Wg1[(c % 2) * 768:(c % 2 + 1) * 768].T),
                      wbT=ac(Wb1[(c % 2) * 3:(c % 2 + 1) * 3].T),
                      waT=ac(Wa1[(c % 2) * 3:(c % 2 + 1) * 3].T),
                      convw=np.stack([cq1[(c % 2) * 768:(c % 2 + 1) * 768],
                                      ck1[(c % 2) * 768:(c % 2 + 1) * 768],
                                      cv1[(c % 2) * 768:(c % 2 + 1) * 768]]),
                      onw=inp["L1_attn_o_norm_w"],
                      alog=ac(inp["L1_attn_A_log"][(c % 2) * 3:(c % 2 + 1) * 3]),
                      dtb=ac(inp["L1_attn_dt_bias"][(c % 2) * 3:(c % 2 + 1) * 3]),
                      trimask=trimask, selhot=selhot) for c in range(8)])
    p = _prog("wo", build_wo_resid)
    woT = ac(inp["L1_attn_Wo"].T)
    r2in = []
    for c in range(8):
        pb = (c // 2) * 2
        osel = np.stack([r[pb + oc // 6]["ot"][oc % 6][:, (c % 2) * TL:(c % 2 + 1) * TL]
                         for oc in range(12)])
        r2in.append(dict(x=xs[c], osel=osel, woT=woT))
    r2 = _run(p, r2in)
    xs = [r2[c]["xo"] for c in range(8)]

    xs = ffn_launch(xs, wgT1, wuT1, wdT1, LI, F32)

    out = np.empty((4, TF, D), np.float32)
    for c in range(8):
        out[c // 2, (c % 2) * TL:(c % 2 + 1) * TL, :] = xs[c].T
    return out


def emit_collective(p, kind, groups, in_ap, out_ap, alu=None):
    nc = p.nc
    cc = p.sem("cc")
    s0 = p.cnt["cc"]
    with nc.Block(f"cc_{p.uid()}") as blk:
        @blk.gpsimd
        def _(g):
            g.collective_compute(kind, alu or ALU.bypass, replica_groups=groups,
                                 ins=[in_ap], outs=[out_ap]).then_inc(cc, 1)
            g.wait_ge(cc, s0 + 1)
    p.cnt["cc"] = s0 + 1


def emit_xln_out(p, c, xln, dst):
    nc = p.nc
    st = p.sem("st")
    s0 = p.cnt["st"]
    with nc.Block(f"xlnout_{p.uid()}") as blk:
        @blk.sync
        def _(sy):
            sy.dma_start(out=dst.rearrange("(k p) t -> p k t", p=128),
                         in_=xln[:].rearrange("p (k t) -> p k t", k=NKD)).then_inc(st, 16)
            sy.wait_ge(st, s0 + 16)
    p.cnt["st"] = s0 + 16


def emit_halfsel_resid(p, c, o_ex, msel, x_acc, ncper, T=TL):
    nc = p.nc
    NT = T // 512
    NB = 4
    ls = contextlib.ExitStack()
    tb = p.sbuf("hs_t", [128, NB * 2 * 512], BF16, ls)
    fb = p.sbuf("hs_f", [128, 2 * 512], F32, ls)
    sc_t = p.sbuf("hs_s", [128, 2 * 512], BF16, ls)
    xl, vx, sg = (p.sem(k) for k in ("xl", "vx", "sg"))
    hsem = [xl, p.sem("xo"), p.sem("st"), p.sem("s2")]
    hkey = ["xl", "xo", "st", "s2"]
    s0 = {k: p.cnt[k] for k in ("xl", "vx", "sg", "xo", "st", "s2")}
    NTOT = NKD * NT
    with nc.Block(f"halfsel_{p.uid()}") as blk:
        @blk.sync
        def _(sy):
            for dc in range(NKD):
                slot, ch = divmod(dc, ncper)
                for tg in range(NT):
                    i = dc * NT + tg
                    if i >= NB:
                        sy.wait_ge(vx, s0["vx"] + i - NB + 1)
                    for hf in range(2):
                        sy.dma_start(out=tb[:, ((i % NB) * 2 + hf) * 512:((i % NB) * 2 + hf) * 512 + 512],
                                     in_=o_ex[slot, ch, :, hf * TL + tg * 512:hf * TL + tg * 512 + 512]
                                     ).then_inc(hsem[i % NB], 16)

        @blk.gpsimd
        def _(g):
            for i in range(NTOT):
                dc, tg = divmod(i, NT)
                g.wait_ge(vx, s0["vx"] + i + 1)
                g.dma_start(out=x_acc[dc * 128:dc * 128 + 128, tg * 512:tg * 512 + 512],
                            in_=fb[:, (i % 2) * 512:(i % 2) * 512 + 512],
                            accum_op=ALU.add).then_inc(sg, 16)
            g.wait_ge(sg, s0["sg"] + 16 * NTOT)

        @blk.vector
        def _(v):
            for dc in range(NKD):
                for tg in range(NT):
                    i = dc * NT + tg
                    v.wait_ge(hsem[i % NB], s0[hkey[i % NB]] + 32 * (i // NB + 1))
                    if i >= 2:
                        v.wait_ge(sg, s0["sg"] + 16 * (i - 1))
                    b0 = ((i % NB) * 2) * 512
                    b1 = ((i % NB) * 2 + 1) * 512
                    v.tensor_scalar(sc_t[:, (i % 2) * 512:(i % 2) * 512 + 512],
                                    tb[:, b0:b0 + 512], msel[:, 0:1],
                                    None, ALU.mult, ALU.bypass)
                    v.scalar_tensor_tensor(fb[:, (i % 2) * 512:(i % 2) * 512 + 512],
                                           tb[:, b1:b1 + 512], msel[:, 1:2],
                                           sc_t[:, (i % 2) * 512:(i % 2) * 512 + 512],
                                           ALU.mult, ALU.add).then_inc(vx, 1)
    for _r in range(NB):
        _nr = NTOT // NB + (1 if NTOT % NB > _r else 0)
        p.cnt[hkey[_r]] = s0[hkey[_r]] + 32 * _nr
    p.cnt["vx"] = s0["vx"] + NTOT
    p.cnt["sg"] = s0["sg"] + 16 * NTOT
    ls.close()


def build_fused():
    p = Prog()
    nc = p.nc
    io = {}
    io["x"] = _drt(nc, "x", [D, TL], F32, "ExternalInput")
    io["gidxT"] = _drt(nc, "gidxT", [256, GI], I32, "ExternalInput")
    io["uidxT"] = _drt(nc, "uidxT", [256, GI], I32, "ExternalInput")
    io["didxT"] = _drt(nc, "didxT", [688, D], I32, "ExternalInput")
    io["lut"] = _drt(nc, "lut", [16], F32, "ExternalInput")
    for nm, sh in (("r_wqT", [D, 1024]), ("r_wkT", [D, 1024]), ("r_wvT", [D, 1024]),
                   ("r_bq", [1024]), ("r_bk", [1024]), ("r_bv", [1024]),
                   ("r_dtT", [4, 256, 256]), ("r_cst", [4, 128, 256]),
                   ("r_kvt", [4, 128, 256]), ("r_gmc", [4]),
                   ("d_wqT", [D, 768]), ("d_wkT", [D, 768]), ("d_wvT", [D, 768]),
                   ("d_wgT", [D, 768]), ("d_wbT", [D, 3]), ("d_waT", [D, 3]),
                   ("d_convw", [3, 768, 4]), ("d_onw", [256]), ("d_alog", [3]),
                   ("d_dtb", [3]), ("d_trimask", [64, 192]), ("d_selhot", [15, 15 * 128]),
                   ("f_wgT", [D, LI]), ("f_wuT", [D, LI]), ("f_wdT", [LI, D]),
                   ("woT", [KEY, D]), ("msel", [2])):
        io[nm] = _drt(nc, nm, sh, F32, "ExternalInput")
    io["xo"] = _drt(nc, "xo", [D, TL], F32, "ExternalOutput")
    wg_sh = _drt(nc, "wg_sh", [256, GI], BF16)
    wu_sh = _drt(nc, "wu_sh", [256, GI], BF16)
    wd_sh = _drt(nc, "wd_sh", [688, D], BF16)
    WgT = nc.dram_tensor("WgT", [D, GI], BF16, addr_space="Shared").ap()
    WuT = nc.dram_tensor("WuT", [D, GI], BF16, addr_space="Shared").ap()
    WdT = nc.dram_tensor("WdT", [GI, D], BF16, addr_space="Shared").ap()
    h0 = _drt(nc, "h0", [GI // 128, 128, TL], BF16)
    h1 = _drt(nc, "h1", [LI // 128, 128, TL], BF16)
    agin = _drt(nc, "agin", [D, TL], BF16)
    xg0 = _drt(nc, "xg0", [2, D, TL], BF16)
    o_sh = _drt(nc, "o_sh", [8, 128, TF], BF16)
    o_ex = _drt(nc, "o_ex", [2, 8, 128, TF], BF16)
    xg1 = _drt(nc, "xg1", [2, D, TL], BF16)
    ot = _drt(nc, "ot", [6, 128, TF], BF16)
    ot_ex = _drt(nc, "ot_ex", [2, 6, 128, TF], BF16)
    dnsp = {nm: _drt(nc, nm, [6, 128, TF], BF16)
            for nm in ("qraw", "kraw", "vraw", "gsp", "qhat", "khat")}
    vtok = _drt(nc, "vtok", [TF, 768], BF16)

    PAIRS = [[0, 1], [2, 3], [4, 5], [6, 7]]
    ALL8 = [list(range(8))]

    c = emit_consts(p)
    msel_sb = p.sbuf("mselsb", [128, 2], F32)
    xl = p.sem("xl")
    s0 = p.cnt["xl"]
    with nc.Block(f"mselld_{p.uid()}") as blk:
        @blk.gpsimd
        def _(g):
            g.dma_start(out=msel_sb[:, :],
                        in_=bass.AP(io["msel"].tensor, 0, [[0, 128], [1, 2]])).then_inc(xl, 16)
            g.wait_ge(xl, s0 + 16)
    p.cnt["xl"] = s0 + 16

    emit_copy_dram(p, io["x"], io["xo"])
    emit_ghost_build(p, dict(gidxT=io["gidxT"], uidxT=io["uidxT"], didxT=io["didxT"],
                             lut=io["lut"], wg_sh=wg_sh, wu_sh=wu_sh, wd_sh=wd_sh))
    emit_collective(p, "AllGather", ALL8, wg_sh, WgT)
    emit_collective(p, "AllGather", ALL8, wu_sh, WuT)
    emit_collective(p, "AllGather", ALL8, wd_sh, WdT)

    wd1b = _drt(nc, "wd1b", [LI, D], BF16)
    wob = _drt(nc, "wob", [KEY, D], BF16)
    wd1_cached = [False]

    def ffn_phase(wgT, wuT, wdT, I, h):
        cache = None
        if wdT is io["f_wdT"]:
            if wd1_cached[0]:
                wdT = wd1b
            else:
                cache = wd1b
                wd1_cached[0] = True
        ls = contextlib.ExitStack()
        xln = p.sbuf("xln", [128, NKD * TL], BF16, ls)
        emit_ln(p, c, io["xo"], xln)
        emit_ffn(p, c, xln, dict(wgT=wgT, wuT=wuT, h=h), I=I, scale=0.5)
        ls.close()
        emit_down(p, c, h, wdT, io["xo"], I // 128, 0.5, wd_cache=cache)

    def ln_ag_phase(xg):
        ls = contextlib.ExitStack()
        xln = p.sbuf("xln", [128, NKD * TL], BF16, ls)
        emit_ln(p, c, io["xo"], xln)
        emit_xln_out(p, c, xln, agin)
        ls.close()
        emit_collective(p, "AllGather", PAIRS, agin,
                        xg.rearrange("a d t -> (a d) t"))

    ffn_phase(WgT, WuT, WdT, GI, h0)
    ln_ag_phase(xg0)
    emit_retnet(p, c, dict(xg=xg0, wqT=io["r_wqT"], wkT=io["r_wkT"], wvT=io["r_wvT"],
                           bq=io["r_bq"], bk=io["r_bk"], bv=io["r_bv"], dtT=io["r_dtT"],
                           cst=io["r_cst"], kvt=io["r_kvt"], gmc=io["r_gmc"], o_sh=o_sh))
    emit_collective(p, "AllGather", PAIRS, o_sh.rearrange("a p t -> (a p) t"),
                    o_ex.rearrange("s a p t -> (s a p) t"))
    emit_halfsel_resid(p, c, o_ex, msel_sb, io["xo"], ncper=8)
    ffn_phase(WgT, WuT, WdT, GI, h0)
    ffn_phase(io["f_wgT"], io["f_wuT"], io["f_wdT"], LI, h1)
    ln_ag_phase(xg1)
    emit_deltanet(p, c, dict(xg=xg1, wqT=io["d_wqT"], wkT=io["d_wkT"], wvT=io["d_wvT"],
                             wgT=io["d_wgT"], wbT=io["d_wbT"], waT=io["d_waT"],
                             convw=io["d_convw"], onw=io["d_onw"], alog=io["d_alog"],
                             dtb=io["d_dtb"], trimask=io["d_trimask"], selhot=io["d_selhot"],
                             vtok=vtok, ot=ot, **dnsp))
    emit_collective(p, "AllGather", PAIRS, ot.rearrange("a p t -> (a p) t"),
                    ot_ex.rearrange("s a p t -> (s a p) t"))
    emit_down(p, c, None, io["woT"], io["xo"], NKI=12, scale=1.0,
              sel=(ot_ex, msel_sb, 6), wd_cache=wob)
    ffn_phase(io["f_wgT"], io["f_wuT"], io["f_wdT"], LI, h1)
    p.S.close()
    return p


FUSED = True


def kernel_fused(**inp):
    ac = np.ascontiguousarray
    x = inp["L0_attn_x"]
    rc = _retnet_consts()
    trimask, selhot = _delta_consts()
    gT = ac(inp["L0_ffn_gate_idx"].T)
    uT = ac(inp["L0_ffn_up_idx"].T)
    dT = ac(inp["L0_ffn_down_idx"].T)
    wgu = inp["L1_ffn_w_gate_up"]
    wgT1, wuT1 = ac(wgu[:LI].T), ac(wgu[LI:].T)
    wdT1 = ac(inp["L1_ffn_w_down"].T)
    woT = ac(inp["L1_attn_Wo"].T)
    Wq1, Wk1, Wv1, Wg1 = (inp["L1_attn_Wq"], inp["L1_attn_Wk"], inp["L1_attn_Wv"], inp["L1_attn_Wg"])
    Wb1, Wa1 = inp["L1_attn_Wb"], inp["L1_attn_Wa"]
    cq1, ck1, cv1 = inp["L1_attn_conv_q"], inp["L1_attn_conv_k"], inp["L1_attn_conv_v"]
    in_maps = []
    for cid in range(8):
        hg = cid % 2
        hr = slice(hg * 1024, (hg + 1) * 1024)
        dr_ = slice(hg * 768, (hg + 1) * 768)
        hs3 = slice(hg * 3, (hg + 1) * 3)
        in_maps.append(dict(
            x=ac(x[cid // 2, (cid % 2) * TL:(cid % 2 + 1) * TL, :].T.astype(np.float32)),
            gidxT=ac(gT[256 * cid:256 * (cid + 1)]), uidxT=ac(uT[256 * cid:256 * (cid + 1)]),
            didxT=ac(dT[688 * cid:688 * (cid + 1)]), lut=inp["L0_ffn_lut"],
            r_wqT=ac(inp["L0_attn_Wq"][:, hr]), r_wkT=ac(inp["L0_attn_Wk"][:, hr]),
            r_wvT=ac(inp["L0_attn_Wv"][:, hr]),
            r_bq=ac(inp["L0_attn_bq"][hr]), r_bk=ac(inp["L0_attn_bk"][hr]),
            r_bv=ac(inp["L0_attn_bv"][hr]),
            r_dtT=rc[hg]["dtT"], r_cst=rc[hg]["cst"], r_kvt=rc[hg]["kvt"], r_gmc=rc[hg]["gmc"],
            d_wqT=ac(Wq1[dr_].T), d_wkT=ac(Wk1[dr_].T), d_wvT=ac(Wv1[dr_].T),
            d_wgT=ac(Wg1[dr_].T), d_wbT=ac(Wb1[hs3].T), d_waT=ac(Wa1[hs3].T),
            d_convw=np.stack([cq1[dr_], ck1[dr_], cv1[dr_]]),
            d_onw=inp["L1_attn_o_norm_w"], d_alog=ac(inp["L1_attn_A_log"][hs3]),
            d_dtb=ac(inp["L1_attn_dt_bias"][hs3]),
            d_trimask=trimask, d_selhot=selhot,
            f_wgT=wgT1, f_wuT=wuT1, f_wdT=wdT1, woT=woT,
            msel=np.array([1.0 - (cid % 2), float(cid % 2)], np.float32),
        ))
    p = _prog("fused", build_fused)
    r = _run(p, in_maps)
    out = np.empty((4, TF, D), np.float32)
    for cid in range(8):
        out[cid // 2, (cid % 2) * TL:(cid % 2 + 1) * TL, :] = r[cid]["xo"].T
    return out


_kernel_unfused = kernel


def kernel(**inp):
    if FUSED:
        return kernel_fused(**inp)
    return _kernel_unfused(**inp)
